# version 13
# baseline (speedup 1.0000x reference)
import numpy as np
from contextlib import ExitStack

import concourse.bass as bass
import concourse.mybir as mybir
from concourse.bass_utils import run_bass_kernel_spmd

F32 = mybir.dt.float32
BF16 = mybir.dt.bfloat16
AF = mybir.ActivationFunctionType
ALU = mybir.AluOpType
AX = mybir.AxisListType

P = 128
D = 2048
E = 4096
N = 16
DTR = 128
INTER = 8192
LCORE = 1024
HALO = 128
LX = LCORE + HALO
TE = E // P
TD = D // P
NCORES = 8
EPS = 1e-5
NPG = 2
NGR = N // NPG

_NC_CACHE = {}


def build_nc(n_cores=NCORES):
    nc = bass.Bass("TRN2", target_bir_lowering=False)

    xin = nc.dram_tensor("x_shard", [LCORE, D], F32, kind="ExternalInput").ap()
    xprev = nc.dram_tensor("x_prev", [HALO, D], F32, kind="ExternalInput").ap()
    onehot = nc.dram_tensor("onehot", [1, NCORES], F32, kind="ExternalInput").ap()
    ident_in = nc.dram_tensor("ident", [P, P], F32, kind="ExternalInput").ap()

    WinT = nc.dram_tensor("WinT", [D, E], F32, kind="ExternalInput").ap()
    WskT = nc.dram_tensor("WskT", [D, E], F32, kind="ExternalInput").ap()
    Wd1T = nc.dram_tensor("Wd1T", [E, DTR], F32, kind="ExternalInput").ap()
    Wd2T = nc.dram_tensor("Wd2T", [DTR, E], F32, kind="ExternalInput").ap()
    WBCT = nc.dram_tensor("WBCT", [E, 2 * N], F32, kind="ExternalInput").ap()
    WoutT = nc.dram_tensor("WoutT", [E, D], F32, kind="ExternalInput").ap()
    guT = nc.dram_tensor("guT", [D, 2 * INTER], F32, kind="ExternalInput").ap()
    dwT = nc.dram_tensor("dwT", [INTER, D], F32, kind="ExternalInput").ap()
    A_log = nc.dram_tensor("A_log", [E, N], F32, kind="ExternalInput").ap()
    conv_w = nc.dram_tensor("conv_w", [E, 4], F32, kind="ExternalInput").ap()
    conv_b = nc.dram_tensor("conv_b", [1, E], F32, kind="ExternalInput").ap()
    bd2_in = nc.dram_tensor("bd2", [1, E], F32, kind="ExternalInput").ap()
    WD_in = nc.dram_tensor("W_D", [1, E], F32, kind="ExternalInput").ap()
    nw_in = nc.dram_tensor("norm_w", [1, D], F32, kind="ExternalInput").ap()

    out = nc.dram_tensor("out_shard", [LCORE, D], F32, kind="ExternalOutput").ap()

    uc_sp = nc.dram_tensor("uc_sp", [E, LCORE], BF16)
    ssk_sp = nc.dram_tensor("ssk_sp", [E, LCORE], BF16)
    z_sp = nc.dram_tensor("z_sp", [E, LCORE], BF16)
    o1_sp = nc.dram_tensor("o1_sp", [D, LCORE], F32)
    xt_sp = nc.dram_tensor("xt_sp", [D, LCORE], F32)
    winb_sp = nc.dram_tensor("winb_sp", [D, E], BF16)
    wskb_sp = nc.dram_tensor("wskb_sp", [D, E], BF16)
    bc_sp = nc.dram_tensor("bc_sp", [2 * N, LCORE], BF16)
    ag_in = [nc.dram_tensor(f"ag_in_{t}", [P, 2 * N], F32) for t in range(TE)]
    ag_out = [nc.dram_tensor(f"ag_out_{t}", [n_cores * P, 2 * N], F32) for t in range(TE)]

    es = ExitStack()
    with es:
        class Sem:
            def __init__(self, name):
                self.h = es.enter_context(nc.semaphore(name))
                self.n = 0

        sq = Sem("sq"); gq = Sem("gq")
        sT = Sem("sT"); sS = Sem("sS"); sV = Sem("sV"); sP = Sem("sP"); sC = Sem("sC")

        def vf(v, inst):
            inst.then_inc(sV.h, 1); sV.n += 1
            v.wait_ge(sV.h, sV.n)

        def sf(s, inst):
            inst.then_inc(sS.h, 1); sS.n += 1
            s.wait_ge(sS.h, sS.n)

        ident = es.enter_context(nc.sbuf_tensor("ident_sb", [P, P], F32))
        ones_f = es.enter_context(nc.sbuf_tensor("ones_f", [1, P], F32))
        ones_b = es.enter_context(nc.sbuf_tensor("ones_b", [1, P], BF16))
        onescol_b = es.enter_context(nc.sbuf_tensor("onescol_b", [P, 1], BF16))
        oh_bc = es.enter_context(nc.sbuf_tensor("oh_bc", [P, NCORES], F32))
        A_sb = es.enter_context(nc.sbuf_tensor("A_sb", [P, TE * N], F32))
        cw_sb = es.enter_context(nc.sbuf_tensor("cw_sb", [P, TE * 4], F32))
        cb_sb = es.enter_context(nc.sbuf_tensor("cb_sb", [P, TE], F32))
        bd2_sb = es.enter_context(nc.sbuf_tensor("bd2_sb", [P, TE], F32))
        wd_sb = es.enter_context(nc.sbuf_tensor("wd_sb", [P, TE], F32))
        nw_sb = es.enter_context(nc.sbuf_tensor("nw_sb", [P, TD], F32))
        hbuf = es.enter_context(nc.sbuf_tensor("hbuf", [P, N * NCORES], F32))

        _scoped = {}

        def salloc(name, shape, dt):
            cm = nc.sbuf_tensor(name, shape, dt)
            t = cm.__enter__()
            _scoped[name] = cm
            return t

        def sfree(*names):
            for nm in names:
                _scoped.pop(nm).__exit__(None, None, None)

        dpre_b = salloc("dpre_b", [P, LCORE], BF16)
        uc_ring = salloc("uc_ring", [P, 2 * LCORE], BF16)
        ssk_tile = salloc("ssk_tile", [P, LCORE], BF16)

        def barrier():
            with nc.Block() as blk:
                def mk(sem):
                    def f(e):
                        e.nop().then_inc(sem.h, 1)
                    return f
                blk.tensor(mk(sT)); blk.scalar(mk(sS)); blk.vector(mk(sV))
                blk.gpsimd(mk(sP)); blk.sync(mk(sq))
                sT.n += 1; sS.n += 1; sV.n += 1; sP.n += 1; sq.n += 1
                tt, ss, vv, pp, qq, gg = sT.n, sS.n, sV.n, sP.n, sq.n, gq.n

                def wall(e):
                    e.wait_ge(sT.h, tt); e.wait_ge(sS.h, ss); e.wait_ge(sV.h, vv)
                    e.wait_ge(sP.h, pp); e.wait_ge(sq.h, qq); e.wait_ge(gq.h, gg)
                blk.tensor(wall); blk.scalar(wall); blk.vector(wall)
                blk.gpsimd(wall); blk.sync(wall)

        with (
            nc.sbuf_tensor("vrow", [TE, P], F32) as vrow,
            nc.psum_tensor("psK0", [P, P], F32) as psK0,
        ):
            with nc.Block() as blk:
                def q_(sync, _qb=sq.n):
                    sync.wait_ge(sq.h, _qb)
                    sync.dma_start(ident[:], ident_in).then_inc(sq.h, 16)
                    sync.dma_start(oh_bc[0:1, :], onehot).then_inc(sq.h, 16)
                blk.sync(q_); sq.n += 32
                ld0 = sq.n

                def g_(g):
                    g.memset(ones_f[:], 1.0)
                    g.memset(ones_b[:], 1.0)
                    g.memset(onescol_b[:], 1.0)
                    g.memset(hbuf[:], 0.0).then_inc(sP.h, 1)
                blk.gpsimd(g_); sP.n += 1
                ones_done = sP.n

            for src, dst, rows in ((conv_b, cb_sb, TE), (bd2_in, bd2_sb, TE),
                                   (WD_in, wd_sb, TE), (nw_in, nw_sb, TD)):
                with nc.Block() as blk:
                    sprev = sS.n

                    def q_(sync, src=src, rows=rows, sprev=sprev, _qb=sq.n):
                        sync.wait_ge(sq.h, _qb)
                        sync.wait_ge(sS.h, sprev)
                        sync.dma_start(vrow[0:rows, :],
                                       src.rearrange("o (r p) -> (o r) p", p=P)).then_inc(sq.h, 16)
                    blk.sync(q_); sq.n += 16
                    ldv = sq.n

                    def t_(t, rows=rows, ldv=ldv):
                        t.wait_ge(sq.h, ldv)
                        t.wait_ge(sS.h, sS.n)
                        t.transpose(psK0[:, 0:rows], vrow[0:rows, :],
                                    ident[0:rows, 0:rows]).then_inc(sT.h, 1)
                    blk.tensor(t_); sT.n += 1
                    trv = sT.n

                    def s_(s, dst=dst, rows=rows, trv=trv):
                        s.wait_ge(sT.h, trv)
                        s.copy(dst[:, 0:rows], psK0[:, 0:rows]).then_inc(sS.h, 1)
                    blk.scalar(s_); sS.n += 1

            with nc.Block() as blk:
                def q_(sync, _qb=sq.n):
                    sync.wait_ge(sq.h, _qb)
                    for t in range(TE):
                        sync.dma_start(cw_sb[:, 4 * t:4 * t + 4],
                                       conv_w[P * t:P * (t + 1), :]).then_inc(sq.h, 16)
                        sync.dma_start(A_sb[:, N * t:N * (t + 1)],
                                       A_log[P * t:P * (t + 1), :]).then_inc(sq.h, 16)
                blk.sync(q_); sq.n += 32 * TE
                lda = sq.n

                def s_(s):
                    s.wait_ge(sq.h, lda)
                    for t in range(TE):
                        s.activation(A_sb[:, N * t:N * (t + 1)], A_sb[:, N * t:N * (t + 1)],
                                     AF.Exp).then_inc(sS.h, 1)
                blk.scalar(s_); sS.n += TE
                expa = sS.n

                def v_(v):
                    v.wait_ge(sS.h, expa)
                    v.tensor_scalar_mul(A_sb[:], A_sb[:], -1.0).then_inc(sV.h, 1)
                blk.vector(v_); sV.n += 1

                sprev2 = sS.n

                def t_(t, sprev2=sprev2, cur=None):
                    t.wait_ge(sq.h, sq.n)
                    t.wait_ge(sP.h, ones_done)
                    t.wait_ge(sS.h, sprev2)
                    t.matmul(psK0[:, 0:NCORES], ones_f[0:1, :], oh_bc[0:1, :],
                             start=True, stop=True).then_inc(sT.h, 1)
                blk.tensor(t_); sT.n += 1
                ohm = sT.n

                def s2_(s):
                    s.wait_ge(sT.h, ohm)
                    s.copy(oh_bc[:], psK0[:, 0:NCORES]).then_inc(sS.h, 1)
                blk.scalar(s2_); sS.n += 1

        barrier()

        keW_row = salloc("keW_row", [1, E], BF16)
        keS_row = salloc("keS_row", [1, E], BF16)
        keW_f = salloc("keW_f", [1, E], F32)
        keS_f = salloc("keS_f", [1, E], F32)
        with (
            nc.sbuf_tensor("wstg", [P, 2 * E], F32) as wstg,
            nc.sbuf_tensor("wstgb", [P, 2 * E], BF16) as wstgb,
            nc.sbuf_tensor("ke_acc", [P, TE], F32) as ke_acc,
            nc.sbuf_tensor("keT", [TE, P], F32) as keT,
            nc.psum_tensor("psK1", [P, P], F32) as psK1,
            nc.psum_tensor("psK2", [P, TE], F32) as psK2,
        ):
            for wi, (wsrc, wdst, kef, keb) in enumerate(
                    ((WinT, winb_sp, keW_f, keW_row), (WskT, wskb_sp, keS_f, keS_row))):
                for dt in range(TD):
                    buf = dt % 2
                    with nc.Block() as blk:
                        sfree0 = sS.n
                        tfree0 = sT.n

                        def q_(sync, dt=dt, buf=buf, wsrc=wsrc, sfree0=sfree0, tfree0=tfree0,
                               _qb=sq.n):
                            sync.wait_ge(sq.h, _qb)
                            sync.wait_ge(sS.h, sfree0)
                            sync.wait_ge(sT.h, tfree0)
                            sync.dma_start(wstg[:, buf * E:(buf + 1) * E],
                                           wsrc[P * dt:P * (dt + 1), :]).then_inc(sq.h, 16)
                        blk.sync(q_); sq.n += 16
                        ldw_ = sq.n

                        def s_(s, dt=dt, buf=buf, ldw_=ldw_):
                            s.wait_ge(sq.h, ldw_)
                            s.activation(wstgb[:, buf * E:(buf + 1) * E],
                                         wstg[:, buf * E:(buf + 1) * E],
                                         AF.Copy, scale=nw_sb[:, dt:dt + 1]).then_inc(sS.h, 1)
                        blk.scalar(s_); sS.n += 1
                        cast_done = sS.n

                        def q2_(sync, dt=dt, buf=buf, wdst=wdst, cast_done=cast_done,
                                _qb=sq.n):
                            sync.wait_ge(sq.h, _qb)
                            sync.wait_ge(sS.h, cast_done)
                            sync.dma_start(wdst.ap()[P * dt:P * (dt + 1), :],
                                           wstgb[:, buf * E:(buf + 1) * E]).then_inc(sq.h, 16)
                        blk.sync(q2_); sq.n += 16

                        vke_prev = sV.n

                        def t_(t, dt=dt, buf=buf, cast_done=cast_done, vke_prev=vke_prev):
                            t.wait_ge(sS.h, cast_done)
                            t.wait_ge(sV.h, vke_prev)
                            for eb in range(TE):
                                mm = t.matmul(psK2[:, eb:eb + 1],
                                              wstgb[:, buf * E + P * eb:buf * E + P * (eb + 1)],
                                              onescol_b[:],
                                              start=True, stop=True)
                                if eb == TE - 1:
                                    mm.then_inc(sT.h, 1)
                        blk.tensor(t_); sT.n += 1
                        kemm_t = sT.n

                        vke_prev2 = sV.n

                        def vke_(v, dt=dt, kemm_t=kemm_t, vke_prev2=vke_prev2):
                            v.wait_ge(sT.h, kemm_t)
                            v.wait_ge(sV.h, vke_prev2)
                            if dt == 0:
                                v.tensor_copy(ke_acc[:], psK2[:]).then_inc(sV.h, 1)
                            else:
                                v.tensor_add(ke_acc[:], ke_acc[:], psK2[:]).then_inc(sV.h, 1)
                        blk.vector(vke_); sV.n += 1

                with nc.Block() as blk:
                    keacc_v = sV.n

                    def t2_(t, keacc_v=keacc_v, gprev=gq.n):
                        t.wait_ge(sV.h, keacc_v)
                        t.wait_ge(gq.h, gprev)
                        t.transpose(psK1[0:TE, :], ke_acc[:], ident[:]).then_inc(sT.h, 1)
                    blk.tensor(t2_); sT.n += 1
                    ketr = sT.n

                    def s15_(s, ketr=ketr):
                        s.wait_ge(sT.h, ketr)
                        s.copy(keT[:], psK1[0:TE, :]).then_inc(sS.h, 1)
                    blk.scalar(s15_); sS.n += 1
                    ketc = sS.n

                    def g_(g, kef=kef, ketc=ketc, _gb=gq.n):
                        g.wait_ge(gq.h, _gb)
                        g.wait_ge(sS.h, ketc)
                        for eb in range(TE):
                            g.dma_start(kef[0:1, P * eb:P * (eb + 1)],
                                        keT[eb:eb + 1, :]).then_inc(gq.h, 16)
                    blk.gpsimd(g_); gq.n += 16 * TE
                    kedma = gq.n

                    def s2_(s, keb=keb, kef=kef, kedma=kedma):
                        s.wait_ge(gq.h, kedma)
                        s.copy(keb[:], kef[:]).then_inc(sS.h, 1)
                    blk.scalar(s2_); sS.n += 1

        barrier()

        xT_bf = salloc("xT_bf", [P, TD * LX], BF16)
        rb_bc = salloc("rb_bc", [P, LX], F32)
        r_row = salloc("r_row", [1, LX], F32)
        nmr_rowb = salloc("nmr_rowb", [1, LX], BF16)
        nmr_f = salloc("nmr_f", [1, LX], F32)
        es2 = ExitStack()
        sx = es2.enter_context(nc.sbuf_tensor("sx", [P, 2 * D], F32))
        xstage = es2.enter_context(nc.sbuf_tensor("xstage", [P, TD * P], F32))
        sxsq = es2.enter_context(nc.sbuf_tensor("sxsq", [P, D], F32))
        scol = es2.enter_context(nc.sbuf_tensor("scol", [P, 16], F32))
        rcols = es2.enter_context(nc.sbuf_tensor("rcols", [P, 9], F32))
        nmrcols = es2.enter_context(nc.sbuf_tensor("nmrcols", [P, 9], F32))
        rowst = es2.enter_context(nc.sbuf_tensor("rowst", [9, P], F32))
        rowst2 = es2.enter_context(nc.sbuf_tensor("rowst2", [9, P], F32))
        psX = es2.enter_context(nc.psum_tensor("psX", [P, P], F32))
        psX2 = es2.enter_context(nc.psum_tensor("psX2", [P, P], F32))
        with es2:
            NJ = 9
            for j in range(NJ):
                buf = j % 2
                with nc.Block() as blk:
                    vprev = sV.n
                    tprev = sT.n
                    s_base = sS.n

                    def q_(sync, j=j, buf=buf, vprev=vprev, tprev=tprev, _qb=sq.n):
                        sync.wait_ge(sq.h, _qb)
                        sync.wait_ge(sV.h, vprev)
                        sync.wait_ge(sT.h, tprev)
                        src = xprev if j == 0 else xin[P * (j - 1):P * j, :]
                        sync.dma_start(sx[:, buf * D:(buf + 1) * D], src).then_inc(sq.h, 16)
                    blk.sync(q_); sq.n += 16
                    ldx = sq.n

                    def s_(s, buf=buf, ldx=ldx):
                        s.wait_ge(sq.h, ldx)
                        s.wait_ge(sV.h, sV.n)
                        s.activation(sxsq[:], sx[:, buf * D:(buf + 1) * D], AF.Square).then_inc(sS.h, 1)
                    blk.scalar(s_); sS.n += 1
                    sqd = sS.n
                    s_base = sS.n

                    def v_(v, j=j, buf=buf, ldx=ldx, sqd=sqd):
                        x_ = sx[:, buf * D:(buf + 1) * D]
                        v.wait_ge(sq.h, ldx)
                        vf(v, v.reduce_sum(scol[:, 0:1], x_, AX.X))
                        v.wait_ge(sS.h, sqd)
                        vf(v, v.reduce_sum(scol[:, 1:2], sxsq[:], AX.X))
                        vf(v, v.tensor_scalar_mul(scol[:, 2:3], scol[:, 0:1], 1.0 / D))
                        vf(v, v.tensor_mul(scol[:, 3:4], scol[:, 2:3], scol[:, 2:3]))
                        vf(v, v.tensor_scalar_mul(scol[:, 4:5], scol[:, 1:2], 1.0 / D))
                        vf(v, v.tensor_sub(scol[:, 5:6], scol[:, 4:5], scol[:, 3:4]))
                        vf(v, v.tensor_scalar_add(scol[:, 6:7], scol[:, 5:6], EPS))
                        vf(v, v.reciprocal(scol[:, 7:8], scol[:, 6:7]))
                        vf(v, v.tensor_mul(scol[:, 8:9], scol[:, 5:6], scol[:, 7:8]))
                        v.tensor_scalar_add(scol[:, 9:10], scol[:, 8:9], EPS).then_inc(sV.h, 1)
                    blk.vector(v_); sV.n += 1
                    vstat = sV.n

                    def s2_(s, vstat=vstat):
                        s.wait_ge(sV.h, vstat)
                        sf(s, s.activation(scol[:, 10:11], scol[:, 6:7], AF.Ln))
                        s.activation(scol[:, 11:12], scol[:, 9:10], AF.Ln).then_inc(sS.h, 1)
                    blk.scalar(s2_); sS.n += 1
                    slog = sS.n

                    def v2_(v, slog=slog):
                        v.wait_ge(sS.h, slog)
                        v.tensor_add(scol[:, 12:13], scol[:, 10:11], scol[:, 11:12]).then_inc(sV.h, 1)
                    blk.vector(v2_); sV.n += 1
                    vln = sV.n

                    def s3_(s, vln=vln):
                        s.wait_ge(sV.h, vln)
                        s.activation(scol[:, 13:14], scol[:, 12:13], AF.Exp, scale=-0.5).then_inc(sS.h, 1)
                    blk.scalar(s3_); sS.n += 1
                    sr = sS.n

                    def v3_(v, j=j, sr=sr):
                        v.wait_ge(sS.h, sr)
                        vf(v, v.tensor_copy(rcols[:, j:j + 1], scol[:, 13:14]))
                        vf(v, v.tensor_mul(scol[:, 14:15], scol[:, 2:3], scol[:, 13:14]))
                        v.tensor_scalar_mul(nmrcols[:, j:j + 1], scol[:, 14:15], -1.0).then_inc(sV.h, 1)
                    blk.vector(v3_); sV.n += 1

                    s_evac_base = sS.n
                    g_evac_base = gq.n
                    nse = 2 if j >= 1 else 1

                    def t_(t, j=j, buf=buf, ldx=ldx, s_evac_base=s_evac_base, nse=nse):
                        t.wait_ge(sq.h, ldx)
                        x_ = sx[:, buf * D:(buf + 1) * D]
                        for dt in range(TD):
                            ps = psX if dt % 2 == 0 else psX2
                            if dt >= 2:
                                t.wait_ge(sS.h, s_evac_base + nse * (dt - 1))
                            t.transpose(ps[:], x_[:, P * dt:P * (dt + 1)], ident[:]).then_inc(sT.h, 1)
                    blk.tensor(t_); sT.n += TD
                    ttr = sT.n

                    def s4_(s, j=j, ttr=ttr, g_evac_base=g_evac_base):
                        if j >= 1:
                            s.wait_ge(gq.h, g_evac_base)
                        for dt in range(TD):
                            s.wait_ge(sT.h, ttr - TD + dt + 1)
                            ps = psX if dt % 2 == 0 else psX2
                            s.copy(xT_bf[:, dt * LX + P * j:dt * LX + P * (j + 1)],
                                   ps[:]).then_inc(sS.h, 1)
                            if j >= 1:
                                s.copy(xstage[:, dt * P:(dt + 1) * P],
                                       ps[:]).then_inc(sS.h, 1)
                    blk.scalar(s4_); sS.n += nse * TD
                    sxe = sS.n

                    if j >= 1:
                        def g_(g, j=j, sxe=sxe, nse=nse, _gb=gq.n):
                            g.wait_ge(gq.h, _gb)
                            for dt in range(TD):
                                g.wait_ge(sS.h, sxe - nse * TD + nse * (dt + 1))
                                g.dma_start(xt_sp.ap()[P * dt:P * (dt + 1),
                                                       P * (j - 1):P * j],
                                            xstage[:, dt * P:(dt + 1) * P]).then_inc(gq.h, 16)
                        blk.gpsimd(g_); gq.n += 16 * TD

            with nc.Block() as blk:
                vdone = sV.n
                gdone = gq.n
                sdone = sS.n

                def t_(t, vdone=vdone, gdone=gdone, sdone=sdone):
                    t.wait_ge(sV.h, vdone)
                    t.wait_ge(gq.h, gdone)
                    t.wait_ge(sS.h, sdone)
                    t.transpose(psX[0:9, :], rcols[:], ident[:]).then_inc(sT.h, 1)
                    t.transpose(psX2[0:9, :], nmrcols[:], ident[:]).then_inc(sT.h, 1)
                blk.tensor(t_); sT.n += 2
                rtr = sT.n

                def se_(s, rtr=rtr):
                    s.wait_ge(sT.h, rtr)
                    s.copy(rowst[:], psX[0:9, :])
                    s.copy(rowst2[:], psX2[0:9, :]).then_inc(sS.h, 1)
                blk.scalar(se_); sS.n += 1
                rse = sS.n

                def g_(g, rse=rse, _gb=gq.n):
                    g.wait_ge(gq.h, _gb)
                    g.wait_ge(sS.h, rse)
                    for j in range(9):
                        g.dma_start(r_row[0:1, P * j:P * (j + 1)], rowst[j:j + 1, :]).then_inc(gq.h, 16)
                        g.dma_start(nmr_f[0:1, P * j:P * (j + 1)], rowst2[j:j + 1, :]).then_inc(gq.h, 16)
                blk.gpsimd(g_); gq.n += 16 * 18
                rdma = gq.n

                def s_(s, rdma=rdma):
                    s.wait_ge(gq.h, rdma)
                    s.copy(nmr_rowb[:], nmr_f[:]).then_inc(sS.h, 1)
                blk.scalar(s_); sS.n += 1

                sb_base = sS.n

                def t2_(t, rdma=rdma, sb_base=sb_base):
                    t.wait_ge(gq.h, rdma)
                    for i in range(9):
                        ps = psX if i % 2 == 0 else psX2
                        if i >= 2:
                            t.wait_ge(sS.h, sb_base + i - 1)
                        t.matmul(ps[:], ones_f[0:1, :], r_row[0:1, P * i:P * (i + 1)],
                                 start=True, stop=True).then_inc(sT.h, 1)
                blk.tensor(t2_); sT.n += 9
                btr = sT.n

                def s5_(s, btr=btr):
                    for i in range(9):
                        s.wait_ge(sT.h, btr - 9 + i + 1)
                        ps = psX if i % 2 == 0 else psX2
                        s.copy(rb_bc[:, P * i:P * (i + 1)], ps[:]).then_inc(sS.h, 1)
                blk.scalar(s5_); sS.n += 9

        barrier()

        WS = 160
        ucmm_done = [0] * TE
        ucspill_done = [0] * TE
        ucdone_arr = [0] * TE
        sskd_arr = [0] * TE
        es2 = ExitStack()
        wv = es2.enter_context(nc.sbuf_tensor("wv", [P, 2 * TD * P], BF16))
        wvs = es2.enter_context(nc.sbuf_tensor("wvs", [P, 2 * TD * P], BF16))
        wsm = es2.enter_context(nc.sbuf_tensor("wsm", [P, 2 * WS], BF16))
        wsm_f = es2.enter_context(nc.sbuf_tensor("wsm_f", [P, 2 * WS], F32))
        up_ext = es2.enter_context(nc.sbuf_tensor("up_ext", [P, LX], F32))
        cva = es2.enter_context(nc.sbuf_tensor("cva", [P, LCORE], F32))
        cvb = es2.enter_context(nc.sbuf_tensor("cvb", [P, LCORE], F32))
        skf = es2.enter_context(nc.sbuf_tensor("skf", [P, LCORE], F32))
        psU = es2.enter_context(nc.psum_tensor("psU", [P, 512], F32))
        psU2 = es2.enter_context(nc.psum_tensor("psU2", [P, 512], F32))
        psDP = es2.enter_context(nc.psum_tensor("psDP", [P, LCORE], F32))
        psBC = es2.enter_context(nc.psum_tensor("psBC", [2 * N, LCORE], F32))
        with es2:
            UCHUNKS = [(0, 512), (512, 1024), (1024, LX)]
            _bc3 = None
            for t in range(TE):
                buf = t % 2
                ucslot = t % 2
                if t % 4 == 0:
                    if _bc3 is not None:
                        _bc3.__exit__(None, None, None)
                    _bc3 = nc.Block()
                    blk = _bc3.__enter__()
                if True:
                    if t == 0:
                        s3tbase = sT.n
                        s3t_arr = {}
                        s3wc_arr = {}
                    tprev = s3t_arr.get(t - 2, s3tbase)

                    def q_(sync, t=t, buf=buf, tprev=tprev, _qb=sq.n,
                           _wc=s3wc_arr.get(t - 2, 0)):
                        sync.wait_ge(sq.h, _qb)
                        sync.wait_ge(sT.h, tprev)
                        if t >= 2:
                            sync.wait_ge(sV.h, _wc)
                        sync.dma_start(
                            wv[:, buf * TD * P:(buf + 1) * TD * P].rearrange("p (a c) -> p a c", c=P),
                            winb_sp.ap()[:, P * t:P * (t + 1)].rearrange("(a p) c -> p a c", p=P),
                        ).then_inc(sq.h, 16)
                        sync.dma_start(
                            wvs[:, buf * TD * P:(buf + 1) * TD * P].rearrange("p (a c) -> p a c", c=P),
                            wskb_sp.ap()[:, P * t:P * (t + 1)].rearrange("(a p) c -> p a c", p=P),
                        ).then_inc(sq.h, 16)
                        sync.dma_start(wsm_f[:, buf * WS:buf * WS + P],
                                       Wd1T[P * t:P * (t + 1), :]).then_inc(sq.h, 16)
                        sync.dma_start(wsm_f[:, buf * WS + P:buf * WS + WS],
                                       WBCT[P * t:P * (t + 1), :]).then_inc(sq.h, 16)
                    blk.sync(q_); sq.n += 64
                    ldw1 = sq.n

                    def v00_(v, buf=buf, ldw1=ldw1):
                        v.wait_ge(sq.h, ldw1)
                        v.tensor_copy(wsm[:, buf * WS:(buf + 1) * WS],
                                      wsm_f[:, buf * WS:(buf + 1) * WS]).then_inc(sV.h, 1)
                    blk.vector(v00_); sV.n += 1
                    wdcast = sV.n
                    s3wc_arr[t] = wdcast

                    vprev_evac = sV.n

                    def t_(t_eng, t=t, buf=buf, ldw1=ldw1, vprev_evac=vprev_evac):
                        t_eng.wait_ge(sq.h, ldw1)
                        t_eng.wait_ge(sV.h, vprev_evac)
                        for ci, (c0, c1) in enumerate(UCHUNKS):
                            ps = psU if ci % 2 == 0 else psU2
                            cw = c1 - c0
                            if ci == 2:
                                t_eng.wait_ge(sV.h, vprev_evac + 1)
                            for dt in range(TD):
                                t_eng.matmul(ps[:, 0:cw],
                                             wv[:, buf * TD * P + dt * P:buf * TD * P + (dt + 1) * P],
                                             xT_bf[:, dt * LX + c0:dt * LX + c1],
                                             start=(dt == 0), stop=False)
                            t_eng.matmul(ps[:, 0:cw], keW_row[0:1, P * t:P * (t + 1)],
                                         nmr_rowb[0:1, c0:c1],
                                         start=False, stop=True).then_inc(sT.h, 1)
                    blk.tensor(t_); sT.n += 3
                    umm = sT.n

                    def v_(v, umm=umm):
                        for ci, (c0, c1) in enumerate(UCHUNKS):
                            cw = c1 - c0
                            ps = psU if ci % 2 == 0 else psU2
                            v.wait_ge(sT.h, umm - 3 + ci + 1)
                            v.tensor_mul(up_ext[:, c0:c1], ps[:, 0:cw],
                                         rb_bc[:, c0:c1]).then_inc(sV.h, 1)
                    blk.vector(v_); sV.n += 3
                    uev = sV.n

                    def v2_(v, t=t, uev=uev):
                        if t >= 1:
                            v.wait_ge(sS.h, ucdone_arr[t - 1])
                        v.wait_ge(sV.h, uev)
                        vf(v, v.tensor_single_scalar(cva[:], up_ext[:, 125:125 + LCORE],
                                                     cw_sb[:, 4 * t:4 * t + 1], ALU.mult))
                        vf(v, v.scalar_tensor_tensor(cvb[:], up_ext[:, 126:126 + LCORE],
                                                     cw_sb[:, 4 * t + 1:4 * t + 2], cva[:],
                                                     ALU.mult, ALU.add))
                        vf(v, v.scalar_tensor_tensor(cva[:], up_ext[:, 127:127 + LCORE],
                                                     cw_sb[:, 4 * t + 2:4 * t + 3], cvb[:],
                                                     ALU.mult, ALU.add))
                        v.scalar_tensor_tensor(cvb[:], up_ext[:, 128:128 + LCORE],
                                               cw_sb[:, 4 * t + 3:4 * t + 4], cva[:],
                                               ALU.mult, ALU.add).then_inc(sV.h, 1)
                    blk.vector(v2_); sV.n += 1
                    convd = sV.n

                    def s2_(s, t=t, ucslot=ucslot, convd=convd):
                        s.wait_ge(sV.h, convd)
                        if t >= 2:
                            s.wait_ge(sT.h, ucmm_done[t - 2])
                            s.wait_ge(gq.h, ucspill_done[t - 2])
                        s.activation(uc_ring[:, ucslot * LCORE:(ucslot + 1) * LCORE], cvb[:],
                                     AF.Silu, bias=cb_sb[:, t:t + 1]).then_inc(sS.h, 1)
                    blk.scalar(s2_); sS.n += 1
                    ucdone = sS.n
                    ucdone_arr[t] = ucdone

                    def g_(g, t=t, ucslot=ucslot, ucdone=ucdone, _gb=gq.n):
                        g.wait_ge(gq.h, _gb)
                        g.wait_ge(sS.h, ucdone)
                        g.dma_start(uc_sp.ap()[P * t:P * (t + 1), :],
                                    uc_ring[:, ucslot * LCORE:(ucslot + 1) * LCORE]).then_inc(gq.h, 16)
                    blk.gpsimd(g_); gq.n += 16
                    ucspill_done[t] = gq.n

                    def t2_(t_eng, t=t, buf=buf, ucslot=ucslot, ucdone=ucdone, wdcast=wdcast,
                            _dummy=None):
                        t_eng.wait_ge(sS.h, ucdone)
                        t_eng.wait_ge(sV.h, wdcast)
                        uc = uc_ring[:, ucslot * LCORE:(ucslot + 1) * LCORE]
                        for lc in range(2):
                            t_eng.matmul(psDP[:, 512 * lc:512 * (lc + 1)],
                                         wsm[:, buf * WS:buf * WS + P],
                                         uc[:, 512 * lc:512 * (lc + 1)],
                                         start=(t == 0), stop=(t == TE - 1), skip_group_check=True)
                        for lc in range(2):
                            mm = t_eng.matmul(psBC[:, 512 * lc:512 * (lc + 1)],
                                              wsm[:, buf * WS + P:buf * WS + WS],
                                              uc[:, 512 * lc:512 * (lc + 1)],
                                              start=(t == 0), stop=(t == TE - 1), skip_group_check=True)
                            if lc == 1:
                                mm.then_inc(sT.h, 1)
                    blk.tensor(t2_); sT.n += 1
                    ucmm_done[t] = sT.n

                    def t3_(t_eng, t=t, buf=buf, uev=uev):
                        t_eng.wait_ge(sV.h, uev)
                        for lc in range(2):
                            ps = psU if lc == 0 else psU2
                            c0 = HALO + 512 * lc
                            for dt in range(TD):
                                t_eng.matmul(ps[:],
                                             wvs[:, buf * TD * P + dt * P:buf * TD * P + (dt + 1) * P],
                                             xT_bf[:, dt * LX + c0:dt * LX + c0 + 512],
                                             start=(dt == 0), stop=False)
                            t_eng.matmul(ps[:], keS_row[0:1, P * t:P * (t + 1)],
                                         nmr_rowb[0:1, c0:c0 + 512],
                                         start=False, stop=True).then_inc(sT.h, 1)
                    blk.tensor(t3_); sT.n += 2
                    skmm = sT.n
                    s3t_arr[t] = skmm

                    def v3_(v, t=t, skmm=skmm):
                        if t >= 1:
                            v.wait_ge(sS.h, sskd_arr[t - 1])
                        for lc in range(2):
                            ps = psU if lc == 0 else psU2
                            v.wait_ge(sT.h, skmm - 2 + lc + 1)
                            v.tensor_mul(skf[:, 512 * lc:512 * (lc + 1)], ps[:],
                                         rb_bc[:, HALO + 512 * lc:HALO + 512 * (lc + 1)]).then_inc(sV.h, 1)
                    blk.vector(v3_); sV.n += 2
                    skev = sV.n

                    def s3_(s, t=t, skev=skev):
                        s.wait_ge(sV.h, skev)
                        if t >= 1:
                            s.wait_ge(gq.h, _sskspill[t - 1])
                        s.activation(ssk_tile[:], skf[:], AF.Silu).then_inc(sS.h, 1)
                    blk.scalar(s3_); sS.n += 1
                    sskd = sS.n
                    sskd_arr[t] = sskd

                    def g2_(g, t=t, sskd=sskd, _gb=gq.n):
                        g.wait_ge(gq.h, _gb)
                        g.wait_ge(sS.h, sskd)
                        g.dma_start(ssk_sp.ap()[P * t:P * (t + 1), :], ssk_tile[:]).then_inc(gq.h, 16)
                    blk.gpsimd(g2_); gq.n += 16
                    if t == 0:
                        _sskspill = [0] * TE
                    _sskspill[t] = gq.n

            _bc3.__exit__(None, None, None)
            _bc3 = None
            with nc.Block() as blk:
                tall = sT.n

                def s_(s, tall=tall):
                    s.wait_ge(sT.h, tall)
                    s.wait_ge(gq.h, _sskspill[TE - 1])
                    s.copy(dpre_b[:], psDP[:])
                    s.copy(ssk_tile[0:2 * N, :], psBC[:]).then_inc(sS.h, 1)
                blk.scalar(s_); sS.n += 1
                bcev = sS.n

                def g_(g, bcev=bcev, _gb=gq.n):
                    g.wait_ge(gq.h, _gb)
                    g.wait_ge(sS.h, bcev)
                    g.dma_start(bc_sp.ap(), ssk_tile[0:2 * N, :]).then_inc(gq.h, 16)
                blk.gpsimd(g_); gq.n += 16

        barrier()
        sfree("nmr_f", "nmr_rowb", "r_row", "rb_bc", "xT_bf",
              "keS_f", "keW_f", "keS_row", "keW_row")

        HS = 24
        es2 = ExitStack()
        wd2s = es2.enter_context(nc.sbuf_tensor("wd2s", [P, 2 * P], BF16))
        wd2f = es2.enter_context(nc.sbuf_tensor("wd2f", [P, 2 * P], F32))
        spx = es2.enter_context(nc.sbuf_tensor("spx", [P, LCORE], F32))
        spa = es2.enter_context(nc.sbuf_tensor("spa", [P, LCORE], F32))
        delta_f = es2.enter_context(nc.sbuf_tensor("delta_f", [P, 2 * LCORE], F32))
        srel = es2.enter_context(nc.sbuf_tensor("srel", [P, 2 * LCORE], F32))
        du_f = es2.enter_context(nc.sbuf_tensor("du_f", [P, 2 * LCORE], F32))
        a_slab = es2.enter_context(nc.sbuf_tensor("a_slab", [P, 2 * NPG * LCORE], F32))
        w_slab = es2.enter_context(nc.sbuf_tensor("w_slab", [P, 2 * NPG * LCORE], F32))
        h_slab = es2.enter_context(nc.sbuf_tensor("h_slab", [P, HS * LCORE], BF16))
        p_slab = es2.enter_context(nc.sbuf_tensor("p_slab", [P, 2 * NPG * LCORE], BF16))
        ht_buf = es2.enter_context(nc.sbuf_tensor("ht_buf", [P, NPG * LCORE], F32))
        q_buf = es2.enter_context(nc.sbuf_tensor("q_buf", [P, (N // 2) * LCORE], BF16))
        B_sb = es2.enter_context(nc.sbuf_tensor("B_sb", [P, 4 * LCORE], BF16))
        C_sb = es2.enter_context(nc.sbuf_tensor("C_sb", [P, 4 * LCORE], BF16))
        y_f = es2.enter_context(nc.sbuf_tensor("y_f", [P, LCORE], F32))
        y_f2 = es2.enter_context(nc.sbuf_tensor("y_f2", [P, LCORE], F32))
        trA = es2.enter_context(nc.sbuf_tensor("trA", [P, LCORE], F32))
        trB = es2.enter_context(nc.sbuf_tensor("trB", [P, LCORE], F32))
        trC = es2.enter_context(nc.sbuf_tensor("trC", [P, LCORE], F32))
        zt_bf = es2.enter_context(nc.sbuf_tensor("zt_bf", [P, LCORE], BF16))
        z_bf = es2.enter_context(nc.sbuf_tensor("z_bf", [P, LCORE], BF16))
        stout = es2.enter_context(nc.sbuf_tensor("stout", [P, 2 * N], F32))
        gsb = es2.enter_context(nc.sbuf_tensor("gsb", [P, NCORES * 2 * N], F32))
        hcur = es2.enter_context(nc.sbuf_tensor("hcur", [P, N], F32))
        htmp = es2.enter_context(nc.sbuf_tensor("htmp", [P, N], F32))
        hsel = es2.enter_context(nc.sbuf_tensor("hsel", [P, N], F32))
        psDel = es2.enter_context(nc.psum_tensor("psDel", [P, LCORE], F32))
        with es2:
            ag_done = [0] * TE
            std_done = [0] * TE
            zv_done = [0] * TE
            yred_done = {}
            hsel_done = [0] * TE
            scans_done_v = [0] * TE
            scv = {}
            wpool = {}
            qpool = {}
            htv = {}
            l1h = {}
            dready_a = [0] * TE
            sskl_done = [0] * TE

            def hslot(t, n):
                return ((N * t + n) % HS) * LCORE

            def emit_delta(t):
                buf = t % 2
                slot = t % 2
                with nc.Block() as blk:
                    tprev = sT.n

                    def q_(sync, t=t, buf=buf, tprev=tprev, _qb=sq.n):
                        sync.wait_ge(sq.h, _qb)
                        sync.wait_ge(sT.h, tprev)
                        sync.dma_start(wd2f[:, buf * P:(buf + 1) * P],
                                       Wd2T[:, P * t:P * (t + 1)]).then_inc(sq.h, 16)
                    blk.sync(q_); sq.n += 16
                    ldd = sq.n

                    def g_(g, t=t, slot=slot, _gb=gq.n):
                        g.wait_ge(gq.h, _gb)
                        if t >= 2:
                            g.wait_ge(sP.h, zv_done[t - 2])
                        g.dma_start(uc_ring[:, slot * LCORE:(slot + 1) * LCORE],
                                    uc_sp.ap()[P * t:P * (t + 1), :]).then_inc(gq.h, 16)
                    blk.gpsimd(g_); gq.n += 16
                    lduc = gq.n

                    def s_(s, buf=buf, ldd=ldd):
                        s.wait_ge(sq.h, ldd)
                        s.copy(wd2s[:, buf * P:(buf + 1) * P],
                               wd2f[:, buf * P:(buf + 1) * P]).then_inc(sS.h, 1)
                    blk.scalar(s_); sS.n += 1
                    w2c = sS.n

                    sfree_ = sS.n

                    def t_(t_eng, buf=buf, w2c=w2c, sfree_=sfree_):
                        t_eng.wait_ge(sS.h, w2c)
                        t_eng.wait_ge(sS.h, sfree_)
                        for lc in range(2):
                            mm = t_eng.matmul(psDel[:, 512 * lc:512 * (lc + 1)],
                                              wd2s[:, buf * P:(buf + 1) * P],
                                              dpre_b[:, 512 * lc:512 * (lc + 1)],
                                              start=True, stop=True)
                            if lc == 1:
                                mm.then_inc(sT.h, 1)
                    blk.tensor(t_); sT.n += 1
                    dmm = sT.n

                    vfree = sV.n

                    def v0_(v, t=t, dmm=dmm, vfree=vfree):
                        v.wait_ge(sT.h, dmm)
                        v.wait_ge(sV.h, vfree)
                        v.tensor_scalar_add(spx[:], psDel[:], bd2_sb[:, t:t + 1]).then_inc(sV.h, 1)
                    blk.vector(v0_); sV.n += 1
                    vbias = sV.n

                    def s2_(s, vbias=vbias):
                        s.wait_ge(sV.h, vbias)
                        sf(s, s.activation(spa[:], spx[:], AF.Abs))
                        sf(s, s.activation(spa[:], spa[:], AF.Exp, scale=-1.0))
                        sf(s, s.activation(spa[:], spa[:], AF.Ln, bias=1.0))
                        s.activation(spx[:], spx[:], AF.Relu).then_inc(sS.h, 1)
                    blk.scalar(s2_); sS.n += 1
                    spd = sS.n

                    def v_(v, slot=slot, spd=spd, lduc=lduc,
                           _df=wpool.get((t - 2, N - 1), 0)):
                        v.wait_ge(sS.h, spd)
                        v.wait_ge(sP.h, _df)
                        dslab = delta_f[:, slot * LCORE:(slot + 1) * LCORE]
                        vf(v, v.tensor_add(dslab, spa[:], spx[:]))
                        v.tensor_tensor_scan(srel[:, slot * LCORE:(slot + 1) * LCORE],
                                             dslab, dslab, 0.0, ALU.add, ALU.bypass)
                        v.wait_ge(gq.h, lduc)
                        v.tensor_mul(du_f[:, slot * LCORE:(slot + 1) * LCORE], dslab,
                                     uc_ring[:, slot * LCORE:(slot + 1) * LCORE]).then_inc(sV.h, 1)
                    blk.vector(v_); sV.n += 1
                    dready_a[t] = sV.n

            def emit_combine(blk, t):
                if True:
                    def g_(g, t=t, _gb=gq.n):
                        g.wait_ge(gq.h, _gb)
                        g.wait_ge(sC.h, ag_done[t])
                        if t >= 1:
                            g.wait_ge(sV.h, hsel_done[t - 1])
                        g.dma_start(
                            gsb[:, 0:n_cores * 2 * N].rearrange("p (r c) -> p r c", r=n_cores),
                            ag_out[t].ap().rearrange("(r p) c -> p r c", p=P),
                        ).then_inc(gq.h, 16)
                    blk.gpsimd(g_); gq.n += 16
                    gld = gq.n

                    def v_(v, t=t, gld=gld):
                        v.wait_ge(gq.h, gld)
                        vf(v, v.memset(hcur[:], 0.0))
                        for r in range(n_cores):
                            vf(v, v.tensor_copy(bass.AP(hbuf, r, [[N * NCORES, P], [NCORES, N]]),
                                                hcur[:]))
                            if r < n_cores - 1:
                                vf(v, v.tensor_mul(htmp[:], gsb[:, r * 2 * N + N:(r + 1) * 2 * N],
                                                   hcur[:]))
                                vf(v, v.tensor_add(hcur[:], htmp[:],
                                                   gsb[:, r * 2 * N:r * 2 * N + N]))
                        vf(v, v.tensor_mul(hbuf[:], hbuf[:],
                                           bass.AP(oh_bc, 0, [[NCORES, P], [0, N], [1, NCORES]])))
                        v.tensor_reduce(hsel[:], hbuf.ap().rearrange("p (n r) -> p n r", r=NCORES),
                                        AX.X, ALU.add).then_inc(sV.h, 1)
                    blk.vector(v_); sV.n += 1
                    hsel_done[t] = sV.n

            def emit_corr_group(blk, t, g):
                slot = t % 2
                gbuf = g % 2
                if True:
                    def q_(sync, t=t, g=g, _qb=sq.n,
                           _fr=[qpool.get((t, g * NPG + i - 4),
                                          qpool.get((t - 1, g * NPG + i + 12), 0))
                                for i in range(NPG)]):
                        sync.wait_ge(sq.h, _qb)
                        for i in range(NPG):
                            n = g * NPG + i
                            sync.wait_ge(sP.h, _fr[i])
                            sync.dma_start(
                                C_sb[:, (n % 4) * LCORE:(n % 4 + 1) * LCORE],
                                bass.AP(bc_sp, (N + n) * LCORE, [[0, P], [1, LCORE]]),
                            ).then_inc(sq.h, 16)
                    blk.sync(q_); sq.n += 16 * NPG
                    cld = sq.n

                    def s_(s, t=t, g=g, gbuf=gbuf, slot=slot,
                           _pf=(htv.get((t, (g - 1) * NPG), 0) if g >= 2
                                else (htv.get((t - 1, (g + 6) * NPG), 0) if t >= 1 else 0))):
                        s.wait_ge(sV.h, _pf)
                        for i in range(NPG):
                            n = g * NPG + i
                            s.activation(
                                p_slab[:, (gbuf * NPG + i) * LCORE:(gbuf * NPG + i + 1) * LCORE],
                                srel[:, slot * LCORE:(slot + 1) * LCORE],
                                AF.Exp, scale=A_sb[:, N * t + n:N * t + n + 1]).then_inc(sS.h, 1)
                    blk.scalar(s_); sS.n += NPG
                    pexp = sS.n

                    for i in range(NPG):
                        htv[(t, g * NPG + i)] = sV.n + i + 1

                    def v_(v, t=t, g=g, gbuf=gbuf, slot=slot, pexp=pexp,
                           _hf=[(qpool.get((t, g * NPG + i - 2),
                                           qpool.get((t - 1, g * NPG + i + 14), 0)))
                                for i in range(NPG)]):
                        v.wait_ge(sS.h, pexp)
                        v.wait_ge(sV.h, hsel_done[t])
                        for i in range(NPG):
                            n = g * NPG + i
                            v.wait_ge(sP.h, _hf[i])
                            v.scalar_tensor_tensor(
                                ht_buf[:, i * LCORE:(i + 1) * LCORE],
                                p_slab[:, (gbuf * NPG + i) * LCORE:(gbuf * NPG + i + 1) * LCORE],
                                hsel[:, n:n + 1],
                                h_slab[:, hslot(t, n):hslot(t, n) + LCORE],
                                ALU.mult, ALU.add).then_inc(sV.h, 1)
                    blk.vector(v_); sV.n += NPG

                    def p2_(v, t=t, g=g, cld=cld,
                            _qf=[(l1h.get((t - 1, 1, (g * NPG + i) // 2), 0)
                                  if g * NPG + i < N // 2
                                  else l1h.get((t, 0, ((g * NPG + i) % 8) // 2), 0))
                                 for i in range(NPG)]):
                        v.wait_ge(sq.h, cld)
                        for i in range(NPG):
                            n = g * NPG + i
                            v.wait_ge(sP.h, _qf[i])
                            v.wait_ge(sV.h, htv[(t, n)])
                            v.tensor_mul(
                                q_buf[:, (n % (N // 2)) * LCORE:(n % (N // 2) + 1) * LCORE],
                                ht_buf[:, i * LCORE:(i + 1) * LCORE],
                                C_sb[:, (n % 4) * LCORE:(n % 4 + 1) * LCORE]).then_inc(sP.h, 1)
                    blk.vector(p2_)
                    for i in range(NPG):
                        sP.n += 1
                        qpool[(t, g * NPG + i)] = sP.n

                    half = g // 4
                    pi = g % 4
                    tgt = (trA, trB, trC, (y_f if half == 0 else y_f2))[pi]

                    def pl1_(p, t=t, g=g, tgt=tgt, pi=pi):
                        p.wait_ge(sP.h, qpool[(t, 2 * g + 1)])
                        if pi == 3 and t >= 1:
                            p.wait_ge(sP.h, zv_done[t - 1])
                        p.tensor_add(tgt[:],
                                     q_buf[:, (2 * (g % 4)) * LCORE:(2 * (g % 4) + 1) * LCORE],
                                     q_buf[:, (2 * (g % 4) + 1) * LCORE:(2 * (g % 4) + 2) * LCORE]
                                     ).then_inc(sP.h, 1)
                    blk.gpsimd(pl1_); sP.n += 1
                    l1h[(t, half, pi)] = sP.n

                    if pi == 3:
                        _pb = sP.n

                        def p_(p, t=t, half=half, _pb=_pb):
                            p.wait_ge(sP.h, _pb)
                            dst = y_f if half == 0 else y_f2
                            p.tensor_add(trA[:], trA[:], trB[:])
                            p.tensor_add(dst[:], dst[:], trC[:]).then_inc(sP.h, 1)
                            p.wait_ge(sP.h, _pb + 1)
                            p.tensor_add(dst[:], dst[:], trA[:]).then_inc(sP.h, 1)
                        blk.gpsimd(p_); sP.n += 2
                        yred_done[(t, half)] = sP.n

            def emit_scan_group(blk, t, g):
                slot = t % 2
                gbuf = g % 2
                if True:
                    def q_(sync, t=t, g=g, _qb=sq.n,
                           _fr=[wpool.get((t, g * NPG + i - 4),
                                          wpool.get((t - 1, g * NPG + i + 12), 0))
                                for i in range(NPG)]):
                        sync.wait_ge(sq.h, _qb)
                        for i in range(NPG):
                            n = g * NPG + i
                            sync.wait_ge(sP.h, _fr[i])
                            sync.dma_start(
                                B_sb[:, (n % 4) * LCORE:(n % 4 + 1) * LCORE],
                                bass.AP(bc_sp, n * LCORE, [[0, P], [1, LCORE]]),
                            ).then_inc(sq.h, 16)
                    blk.sync(q_); sq.n += 16 * NPG
                    bld = sq.n

                    def s_(s, t=t, g=g, gbuf=gbuf, slot=slot,
                           _af=(scv.get((t, (g - 1) * NPG), 0) if g >= 2
                                else (scv.get((t - 1, (g + 6) * NPG), 0) if t >= 1 else 0))):
                        s.wait_ge(sV.h, dready_a[t])
                        s.wait_ge(sV.h, _af)
                        for i in range(NPG):
                            n = g * NPG + i
                            s.activation(
                                a_slab[:, (gbuf * NPG + i) * LCORE:(gbuf * NPG + i + 1) * LCORE],
                                delta_f[:, slot * LCORE:(slot + 1) * LCORE],
                                AF.Exp, scale=A_sb[:, N * t + n:N * t + n + 1]).then_inc(sS.h, 1)
                    blk.scalar(s_); sS.n += NPG
                    aexp = sS.n

                    def p_(v, t=t, g=g, gbuf=gbuf, slot=slot, bld=bld,
                           _wf=(scv.get((t, (g - 1) * NPG), 0) if g >= 2
                                else (scv.get((t - 1, (g + 6) * NPG), 0) if t >= 1 else 0))):
                        v.wait_ge(sq.h, bld)
                        v.wait_ge(sV.h, dready_a[t])
                        v.wait_ge(sV.h, _wf)
                        for i in range(NPG):
                            n = g * NPG + i
                            v.tensor_mul(
                                w_slab[:, (gbuf * NPG + i) * LCORE:(gbuf * NPG + i + 1) * LCORE],
                                du_f[:, slot * LCORE:(slot + 1) * LCORE],
                                B_sb[:, (n % 4) * LCORE:(n % 4 + 1) * LCORE]).then_inc(sP.h, 1)
                    blk.vector(p_)
                    for i in range(NPG):
                        sP.n += 1
                        wpool[(t, g * NPG + i)] = sP.n

                    for i in range(NPG):
                        scv[(t, g * NPG + i)] = sV.n + i + 1

                    def v_(v, t=t, g=g, gbuf=gbuf, slot=slot, aexp=aexp):
                        v.wait_ge(sS.h, aexp)
                        for i in range(NPG):
                            n = g * NPG + i
                            v.wait_ge(sP.h, wpool[(t, n)])
                            if n < 8:
                                if t >= 2:
                                    v.wait_ge(sV.h, htv[(t - 2, n + 8)])
                            else:
                                if t >= 1:
                                    v.wait_ge(sV.h, htv[(t - 1, n - 8)])
                            v.tensor_tensor_scan(
                                h_slab[:, hslot(t, n):hslot(t, n) + LCORE],
                                a_slab[:, (gbuf * NPG + i) * LCORE:(gbuf * NPG + i + 1) * LCORE],
                                w_slab[:, (gbuf * NPG + i) * LCORE:(gbuf * NPG + i + 1) * LCORE],
                                0.0, ALU.mult, ALU.add).then_inc(sV.h, 1)
                    blk.vector(v_)
                    sV.n += NPG

            def emit_state(t):
                slot = t % 2
                scans_done_v[t] = sV.n
                with nc.Block() as blk:
                    scansd = scans_done_v[t]

                    def v_(v, t=t, scansd=scansd):
                        v.wait_ge(sV.h, scansd)
                        if t >= 1:
                            v.wait_ge(gq.h, std_done[t - 1])
                        s0 = (N * t) % HS
                        run1 = min(N, HS - s0)
                        c1 = v.tensor_copy(stout[:, 0:run1],
                                           bass.AP(h_slab, s0 * LCORE + LCORE - 1,
                                                   [[HS * LCORE, P], [LCORE, run1]]))
                        if run1 < N:
                            v.tensor_copy(stout[:, run1:N],
                                          bass.AP(h_slab, LCORE - 1,
                                                  [[HS * LCORE, P], [LCORE, N - run1]])).then_inc(sV.h, 1)
                        else:
                            c1.then_inc(sV.h, 1)
                    blk.vector(v_); sV.n += 1
                    fext = sV.n

                    def s_(s, t=t, slot=slot, fext=fext):
                        s.wait_ge(sV.h, fext)
                        for n in range(N):
                            act = s.activation(stout[:, N + n:N + n + 1],
                                               srel[:, slot * LCORE + LCORE - 1:slot * LCORE + LCORE],
                                               AF.Exp, scale=A_sb[:, N * t + n:N * t + n + 1])
                            if n == N - 1:
                                act.then_inc(sS.h, 1)
                    blk.scalar(s_); sS.n += 1
                    ppd = sS.n

                    def g_(g, t=t, fext=fext, ppd=ppd, _gb=gq.n):
                        g.wait_ge(gq.h, _gb)
                        g.wait_ge(sV.h, fext)
                        g.wait_ge(sS.h, ppd)
                        g.dma_start(ag_in[t].ap(), stout[:]).then_inc(gq.h, 16)
                    blk.gpsimd(g_); gq.n += 16
                    std_done[t] = gq.n

                    cc_step = 16 if n_cores == 1 else 1

                    def g2_(g, t=t, cc_step=cc_step, _cb=sC.n):
                        g.wait_ge(sC.h, _cb)
                        g.wait_ge(gq.h, std_done[t])
                        if n_cores == 1:
                            g.dma_start(ag_out[t].ap()[0:P, :], ag_in[t].ap()).then_inc(sC.h, cc_step)
                        else:
                            g.collective_compute(
                                "AllGather", ALU.bypass,
                                ins=[ag_in[t].ap()], outs=[ag_out[t].ap()],
                                replica_groups=[list(range(n_cores))],
                            ).then_inc(sC.h, cc_step)
                    blk.gpsimd(g2_); sC.n += cc_step
                    ag_done[t] = sC.n

            def emit_ztail(t):
                slot = t % 2
                with nc.Block() as blk:
                    def g_(g, t=t, _gb=gq.n):
                        g.wait_ge(gq.h, _gb)
                        if t >= 1:
                            g.wait_ge(sP.h, zv_done[t - 1])
                        g.dma_start(ssk_tile[:], ssk_sp.ap()[P * t:P * (t + 1), :]).then_inc(gq.h, 16)
                    blk.gpsimd(g_); gq.n += 16
                    sskl = gq.n
                    sskl_done[t] = sskl
                    _pb2 = sP.n

                    def p_(p, t=t, _pb2=_pb2):
                        p.wait_ge(sP.h, yred_done[(t, 1)])
                        p.tensor_add(y_f[:], y_f[:], y_f2[:]).then_inc(sP.h, 1)
                    blk.gpsimd(p_); sP.n += 1
                    yadd = sP.n

                    def v_(v, t=t, slot=slot, sskl=sskl, yadd=yadd):
                        v.wait_ge(sP.h, yadd)
                        vf(v, v.scalar_tensor_tensor(zt_bf[:],
                                                     uc_ring[:, slot * LCORE:(slot + 1) * LCORE],
                                                     wd_sb[:, t:t + 1], y_f[:],
                                                     ALU.mult, ALU.add))
                        v.wait_ge(gq.h, sskl)
                        v.tensor_mul(z_bf[:], zt_bf[:], ssk_tile[:]).then_inc(sP.h, 1)
                    blk.vector(v_); sP.n += 1
                    zv_done[t] = sP.n

                    def g2_(g, t=t, _gb=gq.n):
                        g.wait_ge(gq.h, _gb)
                        g.wait_ge(sP.h, zv_done[t])
                        g.dma_start(z_sp.ap()[P * t:P * (t + 1), :], z_bf[:]).then_inc(gq.h, 16)
                    blk.gpsimd(g2_); gq.n += 16

            for t in range(TE):
                emit_delta(t)
                with nc.Block() as gblk:
                    emit_scan_group(gblk, t, 0)
                    emit_scan_group(gblk, t, 1)
                    if t >= 1:
                        emit_combine(gblk, t - 1)
                    for g in range(NGR):
                        if t >= 1:
                            emit_corr_group(gblk, t - 1, g)
                        if g + 2 < NGR:
                            emit_scan_group(gblk, t, g + 2)
                emit_state(t)
                if t >= 1:
                    emit_ztail(t - 1)
            with nc.Block() as gblk:
                emit_combine(gblk, TE - 1)
                for g in range(NGR):
                    emit_corr_group(gblk, TE - 1, g)
            emit_ztail(TE - 1)

        barrier()
        sfree("ssk_tile", "uc_ring", "dpre_b")

        o1b = es.enter_context(nc.sbuf_tensor("o1b", [P, TD * LCORE], BF16))
        es2 = ExitStack()
        z_all = es2.enter_context(nc.sbuf_tensor("z_all", [P, TE * LCORE], BF16))
        wostg = es2.enter_context(nc.sbuf_tensor("wostg", [P, E], F32))
        wostgb = es2.enter_context(nc.sbuf_tensor("wostgb", [P, 2 * E], BF16))
        mix_bf = es2.enter_context(nc.sbuf_tensor("mix_bf", [P, TD * LCORE], BF16))
        mix2 = es2.enter_context(nc.sbuf_tensor("mix2", [P, LCORE], BF16))
        xT_f = es2.enter_context(nc.sbuf_tensor("xT_f", [P, LCORE], F32))
        lrow = es2.enter_context(nc.sbuf_tensor("lrow", [1, 4 * LCORE], F32))
        r2b = es2.enter_context(nc.sbuf_tensor("r2b", [P, LCORE], BF16))
        mr2b = es2.enter_context(nc.sbuf_tensor("mr2b", [P, LCORE], BF16))
        t1b = es2.enter_context(nc.sbuf_tensor("t1b", [P, LCORE], BF16))
        o1f = es2.enter_context(nc.sbuf_tensor("o1f", [P, LCORE], F32))
        o1sq = es2.enter_context(nc.sbuf_tensor("o1sq", [P, LCORE], BF16))
        psMW = es2.enter_context(nc.psum_tensor("psMW", [P, 1024], F32))
        psM = es2.enter_context(nc.psum_tensor("psM", [P, 512], F32))
        psM2 = es2.enter_context(nc.psum_tensor("psM2", [P, 512], F32))
        psS1 = es2.enter_context(nc.psum_tensor("psS1", [1, LCORE], F32))
        psS2 = es2.enter_context(nc.psum_tensor("psS2", [1, LCORE], F32))
        with es2:
            with nc.Block() as blk:
                def g_(g, _gb=gq.n):
                    g.wait_ge(gq.h, _gb)
                    for t in range(TE):
                        g.dma_start(z_all[:, t * LCORE:(t + 1) * LCORE],
                                    z_sp.ap()[P * t:P * (t + 1), :]).then_inc(gq.h, 16)
                blk.gpsimd(g_); gq.n += 16 * TE
                zld = gq.n

            stats_prev = sT.n
            t0base = sT.n
            mo_arr = {}
            mev_arr = {}
            for dt in range(TD):
                buf = dt % 2
                with nc.Block() as blk:
                    tprev = mo_arr.get(dt - 2, t0base)

                    wc_prev = sV.n

                    def q_(sync, dt=dt, buf=buf, tprev=tprev, wc_prev=wc_prev, _qb=sq.n):
                        sync.wait_ge(sq.h, _qb)
                        sync.wait_ge(sT.h, tprev)
                        if dt >= 1:
                            sync.wait_ge(sV.h, wc_prev)
                        for pc in range(2):
                            sync.dma_start(
                                wostg[:, pc * 1024:(pc + 1) * 1024].rearrange(
                                    "p (a c) -> p a c", c=P),
                                WoutT[1024 * pc:1024 * (pc + 1),
                                      P * dt:P * (dt + 1)].rearrange("(a p) c -> p a c", p=P),
                            ).then_inc(sq.h, 16)
                    blk.sync(q_); sq.n += 32
                    ldo = sq.n

                    def qg_(g, dt=dt, buf=buf, tprev=tprev, wc_prev=wc_prev, _gb=gq.n):
                        g.wait_ge(gq.h, _gb)
                        g.wait_ge(sT.h, tprev)
                        if dt >= 1:
                            g.wait_ge(sV.h, wc_prev)
                        for pc in range(2, 4):
                            g.dma_start(
                                wostg[:, pc * 1024:(pc + 1) * 1024].rearrange(
                                    "p (a c) -> p a c", c=P),
                                WoutT[1024 * pc:1024 * (pc + 1),
                                      P * dt:P * (dt + 1)].rearrange("(a p) c -> p a c", p=P),
                            ).then_inc(gq.h, 16)
                    blk.gpsimd(qg_); gq.n += 32
                    ldo_g = gq.n

                    def v0_(v, buf=buf, ldo=ldo, ldo_g=ldo_g):
                        v.wait_ge(sq.h, ldo)
                        v.wait_ge(gq.h, ldo_g)
                        for hh in range(2):
                            v.tensor_copy(wostgb[:, buf * E + hh * (E // 2):
                                                 buf * E + (hh + 1) * (E // 2)],
                                          wostg[:, hh * (E // 2):(hh + 1) * (E // 2)]
                                          ).then_inc(sV.h, 1)
                    blk.vector(v0_); sV.n += 2
                    woc = sV.n

                    def t_(t_eng, dt=dt, buf=buf, woc=woc,
                           _pm=mev_arr.get(dt - 1, 0)):
                        t_eng.wait_ge(sV.h, woc)
                        if dt >= 1:
                            t_eng.wait_ge(sS.h, _pm)
                        if dt == 0:
                            t_eng.wait_ge(gq.h, zld)
                        for et in range(TE):
                            for lc in range(2):
                                mm = t_eng.matmul(
                                    psMW[:, 512 * lc:512 * (lc + 1)],
                                    wostgb[:, buf * E + P * et:buf * E + P * (et + 1)],
                                    z_all[:, et * LCORE + 512 * lc:et * LCORE + 512 * (lc + 1)],
                                    start=(et == 0), stop=(et == TE - 1),
                                    skip_group_check=True)
                                if et == TE - 1 and lc == 1:
                                    mm.then_inc(sT.h, 1)
                    blk.tensor(t_); sT.n += 1
                    momm = sT.n
                    mo_arr[dt] = momm

                    def s2_(s, dt=dt, momm=momm, stats_prev=stats_prev):
                        s.wait_ge(sT.h, stats_prev)
                        s.wait_ge(sT.h, momm)
                        for lc in range(2):
                            s.copy(mix_bf[:, dt * LCORE + 512 * lc:dt * LCORE + 512 * (lc + 1)],
                                   psMW[:, 512 * lc:512 * (lc + 1)])
                            act = s.activation(mix2[:, 512 * lc:512 * (lc + 1)],
                                               psMW[:, 512 * lc:512 * (lc + 1)], AF.Square)
                            if lc == 1:
                                act.then_inc(sS.h, 1)
                    blk.scalar(s2_); sS.n += 1
                    mev = sS.n
                    mev_arr[dt] = mev

                    def t2_(t_eng, dt=dt, mev=mev):
                        t_eng.wait_ge(sS.h, mev)
                        for lc in range(2):
                            t_eng.matmul(psS1[:, 512 * lc:512 * (lc + 1)], onescol_b[:],
                                         mix_bf[:, dt * LCORE + 512 * lc:dt * LCORE + 512 * (lc + 1)],
                                         start=(dt == 0), stop=(dt == TD - 1), skip_group_check=True)
                            mm = t_eng.matmul(psS2[:, 512 * lc:512 * (lc + 1)], onescol_b[:],
                                              mix2[:, 512 * lc:512 * (lc + 1)],
                                              start=(dt == 0), stop=(dt == TD - 1), skip_group_check=True)
                            if lc == 1:
                                mm.then_inc(sT.h, 1)
                    blk.tensor(t2_); sT.n += 1
                    stats_prev = sT.n

            with nc.Block() as blk:
                smm = sT.n

                def v_(v, smm=smm):
                    v.wait_ge(sT.h, smm)
                    vf(v, v.tensor_scalar_mul(lrow[0:1, 0:LCORE], psS1[:], 1.0 / D))
                    vf(v, v.tensor_scalar_mul(lrow[0:1, LCORE:2 * LCORE], psS2[:], 1.0 / D))
                    vf(v, v.tensor_mul(lrow[0:1, 2 * LCORE:3 * LCORE], lrow[0:1, 0:LCORE],
                                       lrow[0:1, 0:LCORE]))
                    vf(v, v.tensor_sub(lrow[0:1, LCORE:2 * LCORE], lrow[0:1, LCORE:2 * LCORE],
                                       lrow[0:1, 2 * LCORE:3 * LCORE]))
                    v.tensor_scalar_add(lrow[0:1, LCORE:2 * LCORE],
                                        lrow[0:1, LCORE:2 * LCORE], EPS).then_inc(sV.h, 1)
                blk.vector(v_); sV.n += 1
                vst = sV.n

                def s_(s, vst=vst):
                    s.wait_ge(sV.h, vst)
                    sf(s, s.activation(lrow[0:1, 2 * LCORE:3 * LCORE], lrow[0:1, LCORE:2 * LCORE],
                                       AF.Ln))
                    s.activation(lrow[0:1, 2 * LCORE:3 * LCORE], lrow[0:1, 2 * LCORE:3 * LCORE],
                                 AF.Exp, scale=-0.5).then_inc(sS.h, 1)
                blk.scalar(s_); sS.n += 1
                srr = sS.n

                def v2_(v, srr=srr):
                    v.wait_ge(sS.h, srr)
                    v.tensor_mul(lrow[0:1, 3 * LCORE:4 * LCORE], lrow[0:1, 0:LCORE],
                                 lrow[0:1, 2 * LCORE:3 * LCORE]).then_inc(sV.h, 1)
                blk.vector(v2_); sV.n += 1
                vmr = sV.n

                def t_(t_eng, vmr=vmr):
                    t_eng.wait_ge(sV.h, vmr)
                    for lc in range(2):
                        psr = psM[:] if lc == 0 else psMW[:, 0:512]
                        psm_ = psM2[:] if lc == 0 else psMW[:, 512:1024]
                        t_eng.matmul(psr, ones_f[0:1, :],
                                     lrow[0:1, 2 * LCORE + 512 * lc:2 * LCORE + 512 * (lc + 1)],
                                     start=True, stop=True)
                        t_eng.matmul(psm_, ones_f[0:1, :],
                                     lrow[0:1, 3 * LCORE + 512 * lc:3 * LCORE + 512 * (lc + 1)],
                                     start=True, stop=True).then_inc(sT.h, 1)
                blk.tensor(t_); sT.n += 2
                bmm2 = sT.n

                def s2_(s, bmm2=bmm2):
                    s.wait_ge(sT.h, bmm2)
                    s.copy(r2b[:, 0:512], psM[:])
                    s.copy(mr2b[:, 0:512], psM2[:])
                    s.copy(r2b[:, 512:1024], psMW[:, 0:512])
                    s.copy(mr2b[:, 512:1024], psMW[:, 512:1024]).then_inc(sS.h, 1)
                blk.scalar(s2_); sS.n += 1
                bcd = sS.n

            o1v_done = [0] * TD
            for dt in range(TD):
                with nc.Block() as blk:
                    def g_(g, dt=dt, _gb=gq.n):
                        g.wait_ge(gq.h, _gb)
                        if dt >= 1:
                            g.wait_ge(sV.h, o1v_done[dt - 1])
                        g.dma_start(xT_f[:], xt_sp.ap()[P * dt:P * (dt + 1), :]).then_inc(gq.h, 16)
                    blk.gpsimd(g_); gq.n += 16
                    xl = gq.n

                    def v_(v, dt=dt, xl=xl, bcd=bcd):
                        v.wait_ge(sS.h, bcd)
                        v.wait_ge(gq.h, xl)
                        vf(v, v.tensor_mul(t1b[:], mix_bf[:, dt * LCORE:(dt + 1) * LCORE], r2b[:]))
                        vf(v, v.tensor_sub(o1f[:], t1b[:], mr2b[:]))
                        vf(v, v.tensor_add(o1f[:], o1f[:], xT_f[:]))
                        v.tensor_copy(o1b[:, dt * LCORE:(dt + 1) * LCORE], o1f[:]).then_inc(sV.h, 1)
                    blk.vector(v_); sV.n += 1
                    o1v_done[dt] = sV.n

                    def g2_(g, dt=dt, _gb=gq.n):
                        g.wait_ge(gq.h, _gb)
                        g.wait_ge(sV.h, o1v_done[dt])
                        g.dma_start(o1_sp.ap()[P * dt:P * (dt + 1), :], o1f[:]).then_inc(gq.h, 16)
                    blk.gpsimd(g2_); gq.n += 16

            st2_arr = [0] * TD
            for dt in range(TD):
                with nc.Block() as blk:
                    def s_(s, dt=dt):
                        s.wait_ge(sV.h, o1v_done[dt])
                        if dt >= 1:
                            s.wait_ge(sT.h, st2_arr[dt - 1])
                        for lc in range(2):
                            act = s.activation(
                                o1sq[:, 512 * lc:512 * (lc + 1)],
                                o1b[:, dt * LCORE + 512 * lc:dt * LCORE + 512 * (lc + 1)],
                                AF.Square)
                            if lc == 1:
                                act.then_inc(sS.h, 1)
                    blk.scalar(s_); sS.n += 1
                    sq2d = sS.n

                    def t_(t_eng, dt=dt, sq2d=sq2d):
                        t_eng.wait_ge(sS.h, sq2d)
                        for lc in range(2):
                            t_eng.matmul(psS1[:, 512 * lc:512 * (lc + 1)], onescol_b[:],
                                         o1b[:, dt * LCORE + 512 * lc:dt * LCORE + 512 * (lc + 1)],
                                         start=(dt == 0), stop=(dt == TD - 1),
                                         skip_group_check=True)
                            mm = t_eng.matmul(psS2[:, 512 * lc:512 * (lc + 1)], onescol_b[:],
                                              o1sq[:, 512 * lc:512 * (lc + 1)],
                                              start=(dt == 0), stop=(dt == TD - 1),
                                              skip_group_check=True)
                            if lc == 1:
                                mm.then_inc(sT.h, 1)
                    blk.tensor(t_); sT.n += 1
                    st2_arr[dt] = sT.n

            with nc.Block() as blk:
                smm3 = sT.n

                def v_(v, smm3=smm3):
                    v.wait_ge(sT.h, smm3)
                    vf(v, v.tensor_scalar_mul(lrow[0:1, 0:LCORE], psS1[:], 1.0 / D))
                    vf(v, v.tensor_scalar_mul(lrow[0:1, LCORE:2 * LCORE], psS2[:], 1.0 / D))
                    vf(v, v.tensor_mul(lrow[0:1, 2 * LCORE:3 * LCORE], lrow[0:1, 0:LCORE],
                                       lrow[0:1, 0:LCORE]))
                    vf(v, v.tensor_sub(lrow[0:1, LCORE:2 * LCORE], lrow[0:1, LCORE:2 * LCORE],
                                       lrow[0:1, 2 * LCORE:3 * LCORE]))
                    v.tensor_scalar_add(lrow[0:1, LCORE:2 * LCORE],
                                        lrow[0:1, LCORE:2 * LCORE], EPS).then_inc(sV.h, 1)
                blk.vector(v_); sV.n += 1
                vst3 = sV.n

                def s_(s, vst3=vst3):
                    s.wait_ge(sV.h, vst3)
                    sf(s, s.activation(lrow[0:1, 2 * LCORE:3 * LCORE], lrow[0:1, LCORE:2 * LCORE],
                                       AF.Ln))
                    s.activation(lrow[0:1, 2 * LCORE:3 * LCORE], lrow[0:1, 2 * LCORE:3 * LCORE],
                                 AF.Exp, scale=-0.5).then_inc(sS.h, 1)
                blk.scalar(s_); sS.n += 1
                srr3 = sS.n

                def v2_(v, srr3=srr3):
                    v.wait_ge(sS.h, srr3)
                    v.tensor_mul(lrow[0:1, 3 * LCORE:4 * LCORE], lrow[0:1, 0:LCORE],
                                 lrow[0:1, 2 * LCORE:3 * LCORE]).then_inc(sV.h, 1)
                blk.vector(v2_); sV.n += 1
                vmr3 = sV.n

                def t_(t_eng, vmr3=vmr3):
                    t_eng.wait_ge(sV.h, vmr3)
                    for lc in range(2):
                        psr = psM[:] if lc == 0 else psMW[:, 0:512]
                        psm_ = psM2[:] if lc == 0 else psMW[:, 512:1024]
                        t_eng.matmul(psr, ones_f[0:1, :],
                                     lrow[0:1, 2 * LCORE + 512 * lc:2 * LCORE + 512 * (lc + 1)],
                                     start=True, stop=True)
                        t_eng.matmul(psm_, ones_f[0:1, :],
                                     lrow[0:1, 3 * LCORE + 512 * lc:3 * LCORE + 512 * (lc + 1)],
                                     start=True, stop=True).then_inc(sT.h, 1)
                blk.tensor(t_); sT.n += 2
                bmm4 = sT.n

                def s2_(s, bmm4=bmm4):
                    s.wait_ge(sT.h, bmm4)
                    s.copy(r2b[:, 0:512], psM[:])
                    s.copy(mr2b[:, 0:512], psM2[:])
                    s.copy(r2b[:, 512:1024], psMW[:, 0:512])
                    s.copy(mr2b[:, 512:1024], psMW[:, 512:1024]).then_inc(sS.h, 1)
                blk.scalar(s2_); sS.n += 1
                bcd3 = sS.n

            for dt in range(TD):
                with nc.Block() as blk:
                    vprev3 = sV.n

                    def v_(v, dt=dt, bcd3=bcd3, vprev3=vprev3, smm3=st2_arr[TD - 1]):
                        v.wait_ge(sS.h, bcd3)
                        v.wait_ge(sT.h, smm3)
                        v.wait_ge(sV.h, vprev3)
                        vf(v, v.tensor_mul(t1b[:], o1b[:, dt * LCORE:(dt + 1) * LCORE], r2b[:]))
                        v.tensor_sub(o1b[:, dt * LCORE:(dt + 1) * LCORE], t1b[:],
                                     mr2b[:]).then_inc(sV.h, 1)
                    blk.vector(v_); sV.n += 1

        barrier()

        es2 = ExitStack()
        gustg = es2.enter_context(nc.sbuf_tensor("gustg", [P, 4 * 1024], F32))
        dstg = es2.enter_context(nc.sbuf_tensor("dstg", [P, E], F32))
        dstgb = es2.enter_context(nc.sbuf_tensor("dstgb", [P, 2 * INTER], BF16))
        sg_f = es2.enter_context(nc.sbuf_tensor("sg_f", [P, 1024], F32))
        hffn = es2.enter_context(nc.sbuf_tensor("hffn", [P, 64 * 512], BF16))
        ffn_bf = es2.enter_context(nc.sbuf_tensor("ffn_bf", [P, TD * 512], BF16))
        ffn2 = es2.enter_context(nc.sbuf_tensor("ffn2", [P, 512], BF16))
        lrow2 = es2.enter_context(nc.sbuf_tensor("lrow2", [1, 4 * 512], F32))
        r3b = es2.enter_context(nc.sbuf_tensor("r3b", [P, 512], BF16))
        mr3b = es2.enter_context(nc.sbuf_tensor("mr3b", [P, 512], BF16))
        t1c = es2.enter_context(nc.sbuf_tensor("t1c", [P, 512], BF16))
        o1l = es2.enter_context(nc.sbuf_tensor("o1l", [P, 512], F32))
        o2f = es2.enter_context(nc.sbuf_tensor("o2f", [P, 512], F32))
        ostg = es2.enter_context(nc.sbuf_tensor("ostg", [P, 2 * P], F32))
        psG = es2.enter_context(nc.psum_tensor("psG", [P, 512], F32))
        psUp = es2.enter_context(nc.psum_tensor("psUp", [P, 512], F32))
        psDn = es2.enter_context(nc.psum_tensor("psDn", [P, 512], F32))
        psDn2 = es2.enter_context(nc.psum_tensor("psDn2", [P, 512], F32))
        psT1 = es2.enter_context(nc.psum_tensor("psT1", [1, 512], F32))
        psT2 = es2.enter_context(nc.psum_tensor("psT2", [1, 512], F32))
        psO = es2.enter_context(nc.psum_tensor("psO", [P, P], F32))
        with es2:
            for lh in range(2):
                l0 = 512 * lh
                ggmm = {}
                gmul = {}
                gvc = {}
                tbase = sT.n
                vbase = sV.n
                sbase = sS.n
                _bc = None
                for gi in range(64):
                    buf = gi % 2
                    pg = psG if gi % 2 == 0 else psDn
                    pu = psUp if gi % 2 == 0 else psDn2
                    if gi % 8 == 0:
                        if _bc is not None:
                            _bc.__exit__(None, None, None)
                        _bc = nc.Block()
                        blk = _bc.__enter__()
                    if True:
                        tfree = ggmm.get(gi - 2, tbase)

                        for bb in range(2):
                            def q_(sync, gi=gi, bb=bb, tfree=tfree, _qb=sq.n,
                                   _vc=gvc.get((gi - 1, bb), vbase)):
                                sync.wait_ge(sq.h, _qb)
                                sync.wait_ge(sT.h, tfree)
                                sync.wait_ge(sV.h, _vc)
                                for p2 in range(2):
                                    part = bb * 2 + p2
                                    col0 = P * gi + INTER * (part // 2)
                                    hh = part % 2
                                    sync.dma_start(
                                        gustg[:, (bb * 2 + p2) * 1024:(bb * 2 + p2 + 1) * 1024]
                                        .rearrange("p (a c) -> p a c", c=P),
                                        guT[1024 * hh:1024 * (hh + 1),
                                            col0:col0 + P].rearrange("(a p) c -> p a c", p=P),
                                    ).then_inc(sq.h, 16)
                            blk.sync(q_); sq.n += 32
                            ldg = sq.n

                            def v0_(v, buf=buf, bb=bb, ldg=ldg):
                                v.wait_ge(sq.h, ldg)
                                for p2 in range(2):
                                    part = bb * 2 + p2
                                    gu = part // 2
                                    hh = part % 2
                                    goff = 0 if gu == 0 else 2 * TD * P
                                    v.tensor_copy(dstgb[:, goff + buf * TD * P + hh * 1024:
                                                        goff + buf * TD * P + (hh + 1) * 1024],
                                                  gustg[:, (bb * 2 + p2) * 1024:
                                                        (bb * 2 + p2 + 1) * 1024]
                                                  ).then_inc(sV.h, 1)
                            blk.vector(v0_); sV.n += 2
                            gvc[(gi, bb)] = sV.n
                        gc = sV.n

                        def t_(t_eng, gi=gi, buf=buf, l0=l0, gc=gc, pg=pg, pu=pu,
                               _w2=gmul.get(gi - 2, vbase), _sb=sbase):
                            t_eng.wait_ge(sV.h, gc)
                            t_eng.wait_ge(sV.h, _w2)
                            if gi < 2:
                                t_eng.wait_ge(sS.h, _sb)
                            for dt in range(TD):
                                mm = t_eng.matmul(
                                    pg[:],
                                    dstgb[:, buf * TD * P + dt * P:buf * TD * P + (dt + 1) * P],
                                    o1b[:, dt * LCORE + l0:dt * LCORE + l0 + 512],
                                    start=(dt == 0), stop=(dt == TD - 1))
                                if dt == TD - 1:
                                    mm.then_inc(sT.h, 1)
                            for dt in range(TD):
                                mm = t_eng.matmul(
                                    pu[:],
                                    dstgb[:, 2 * TD * P + buf * TD * P + dt * P:
                                          2 * TD * P + buf * TD * P + (dt + 1) * P],
                                    o1b[:, dt * LCORE + l0:dt * LCORE + l0 + 512],
                                    start=(dt == 0), stop=(dt == TD - 1))
                                if dt == TD - 1:
                                    mm.then_inc(sT.h, 1)
                        blk.tensor(t_); sT.n += 2
                        gmm = sT.n
                        ggmm[gi] = gmm

                        def s2_(s, gi=gi, gmm=gmm, pg=pg):
                            s.wait_ge(sT.h, gmm - 1)
                            s.activation(sg_f[:, (gi % 2) * 512:(gi % 2 + 1) * 512],
                                         pg[:], AF.Silu).then_inc(sS.h, 1)
                        blk.scalar(s2_); sS.n += 1
                        sgd = sS.n

                        def v_(v, gi=gi, gmm=gmm, sgd=sgd, pu=pu):
                            v.wait_ge(sT.h, gmm)
                            v.wait_ge(sS.h, sgd)
                            v.tensor_mul(hffn[:, gi * 512:(gi + 1) * 512],
                                         sg_f[:, (gi % 2) * 512:(gi % 2 + 1) * 512],
                                         pu[:]).then_inc(sV.h, 1)
                        blk.vector(v_); sV.n += 1
                        gmul[gi] = sV.n

                _bc.__exit__(None, None, None)
                _bc = None
                hffn_done = sV.n
                for dt in range(TD):
                    buf = dt % 2
                    if dt % 8 == 0:
                        if _bc is not None:
                            _bc.__exit__(None, None, None)
                        _bc = nc.Block()
                        blk = _bc.__enter__()
                    if True:
                        if dt == 0:
                            tbase2 = sT.n
                            sbase2 = sS.n
                            dmm_arr = {}
                            sevac = {}
                            dvc = {}
                        tfree2 = dmm_arr.get(dt - 2, tbase2)

                        for bb in range(2):
                            stg = dstg if bb == 0 else gustg

                            def q_(sync, dt=dt, bb=bb, tfree2=tfree2, stg=stg, _qb=sq.n,
                                   _vc=dvc.get((dt - 1, bb), 0) if dt >= 1 else None):
                                sync.wait_ge(sq.h, _qb)
                                sync.wait_ge(sT.h, tfree2)
                                if _vc is not None:
                                    sync.wait_ge(sV.h, _vc)
                                for hh2 in range(2):
                                    hh = bb * 2 + hh2
                                    sync.dma_start(
                                        stg[:, hh2 * 2048:(hh2 + 1) * 2048].rearrange(
                                            "p (a c) -> p a c", c=P),
                                        dwT[2048 * hh:2048 * (hh + 1),
                                            P * dt:P * (dt + 1)].rearrange("(a p) c -> p a c", p=P),
                                    ).then_inc(sq.h, 16)
                            blk.sync(q_); sq.n += 32
                            ldd2 = sq.n

                            def v0_(v, buf=buf, bb=bb, ldd2=ldd2, stg=stg):
                                v.wait_ge(sq.h, ldd2)
                                for hh2 in range(2):
                                    hh = bb * 2 + hh2
                                    v.tensor_copy(dstgb[:, buf * INTER + hh * 2048:
                                                        buf * INTER + (hh + 1) * 2048],
                                                  stg[:, hh2 * 2048:(hh2 + 1) * 2048]
                                                  ).then_inc(sV.h, 1)
                            blk.vector(v0_); sV.n += 2
                            dvc[(dt, bb)] = sV.n
                        dc = sV.n

                        def t_(t_eng, dt=dt, buf=buf, dc=dc, hffn_done=hffn_done,
                               _pf=None, _sb2=sbase2):
                            t_eng.wait_ge(sV.h, dc)
                            t_eng.wait_ge(sV.h, hffn_done)
                            if dt < 2:
                                t_eng.wait_ge(sS.h, _sb2)
                            else:
                                t_eng.wait_ge(sS.h, sevac[dt - 2])
                            ps = psDn if dt % 2 == 0 else psDn2
                            for ii in range(64):
                                mm = t_eng.matmul(ps[:],
                                                  dstgb[:, buf * INTER + P * ii:
                                                        buf * INTER + P * (ii + 1)],
                                                  hffn[:, ii * 512:(ii + 1) * 512],
                                                  start=(ii == 0), stop=(ii == 63))
                                if ii == 63:
                                    mm.then_inc(sT.h, 1)
                        blk.tensor(t_); sT.n += 1
                        dmm2 = sT.n
                        dmm_arr[dt] = dmm2

                        def s2_(s, dt=dt, dmm2=dmm2):
                            ps = psDn if dt % 2 == 0 else psDn2
                            s.wait_ge(sT.h, dmm2)
                            if dt >= 1:
                                s.wait_ge(sT.h, sT.n)
                            s.copy(ffn_bf[:, dt * 512:(dt + 1) * 512], ps[:])
                            s.activation(ffn2[:], ps[:], AF.Square).then_inc(sS.h, 1)
                        blk.scalar(s2_); sS.n += 1
                        fev = sS.n
                        sevac[dt] = fev

                        def t2_(t_eng, dt=dt, fev=fev):
                            t_eng.wait_ge(sS.h, fev)
                            t_eng.matmul(psT1[:], onescol_b[:], ffn_bf[:, dt * 512:(dt + 1) * 512],
                                         start=(dt == 0), stop=(dt == TD - 1), skip_group_check=True)
                            t_eng.matmul(psT2[:], onescol_b[:], ffn2[:],
                                         start=(dt == 0), stop=(dt == TD - 1),
                                         skip_group_check=True).then_inc(sT.h, 1)
                        blk.tensor(t2_); sT.n += 1

                _bc.__exit__(None, None, None)
                _bc = None
                with nc.Block() as blk:
                    smm2 = sT.n

                    def v_(v, smm2=smm2):
                        v.wait_ge(sT.h, smm2)
                        vf(v, v.tensor_scalar_mul(lrow2[0:1, 0:512], psT1[:], 1.0 / D))
                        vf(v, v.tensor_scalar_mul(lrow2[0:1, 512:1024], psT2[:], 1.0 / D))
                        vf(v, v.tensor_mul(lrow2[0:1, 1024:1536], lrow2[0:1, 0:512],
                                           lrow2[0:1, 0:512]))
                        vf(v, v.tensor_sub(lrow2[0:1, 512:1024], lrow2[0:1, 512:1024],
                                           lrow2[0:1, 1024:1536]))
                        v.tensor_scalar_add(lrow2[0:1, 512:1024], lrow2[0:1, 512:1024],
                                            EPS).then_inc(sV.h, 1)
                    blk.vector(v_); sV.n += 1
                    vst2 = sV.n

                    def s_(s, vst2=vst2):
                        s.wait_ge(sV.h, vst2)
                        sf(s, s.activation(lrow2[0:1, 1024:1536], lrow2[0:1, 512:1024], AF.Ln))
                        s.activation(lrow2[0:1, 1024:1536], lrow2[0:1, 1024:1536],
                                     AF.Exp, scale=-0.5).then_inc(sS.h, 1)
                    blk.scalar(s_); sS.n += 1
                    sr2 = sS.n

                    def v2_(v, sr2=sr2):
                        v.wait_ge(sS.h, sr2)
                        v.tensor_mul(lrow2[0:1, 1536:2048], lrow2[0:1, 0:512],
                                     lrow2[0:1, 1024:1536]).then_inc(sV.h, 1)
                    blk.vector(v2_); sV.n += 1
                    vmr2 = sV.n

                    def t_(t_eng, vmr2=vmr2):
                        t_eng.wait_ge(sV.h, vmr2)
                        t_eng.matmul(psDn[:], ones_f[0:1, :], lrow2[0:1, 1024:1536],
                                     start=True, stop=True)
                        t_eng.matmul(psDn2[:], ones_f[0:1, :], lrow2[0:1, 1536:2048],
                                     start=True, stop=True).then_inc(sT.h, 1)
                    blk.tensor(t_); sT.n += 1
                    bmm3 = sT.n

                    def s2_(s, bmm3=bmm3):
                        s.wait_ge(sT.h, bmm3)
                        s.copy(r3b[:], psDn[:])
                        s.copy(mr3b[:], psDn2[:]).then_inc(sS.h, 1)
                    blk.scalar(s2_); sS.n += 1
                    bcd2 = sS.n

                for dt in range(TD):
                    if dt % 8 == 0:
                        if _bc is not None:
                            _bc.__exit__(None, None, None)
                        _bc = nc.Block()
                        blk = _bc.__enter__()
                    if True:
                        o2prev = sT.n

                        def g_(g, dt=dt, lh=lh, o2prev=o2prev, _gb=gq.n):
                            g.wait_ge(gq.h, _gb)
                            g.wait_ge(sT.h, o2prev)
                            g.dma_start(o1l[:], o1_sp.ap()[P * dt:P * (dt + 1),
                                                           512 * lh:512 * (lh + 1)]).then_inc(gq.h, 16)
                        blk.gpsimd(g_); gq.n += 16
                        o1ld = gq.n

                        def v_(v, dt=dt, o1ld=o1ld, bcd2=bcd2, o2prev=o2prev):
                            v.wait_ge(sS.h, bcd2)
                            v.wait_ge(sT.h, o2prev)
                            vf(v, v.tensor_mul(t1c[:], ffn_bf[:, dt * 512:(dt + 1) * 512], r3b[:]))
                            vf(v, v.tensor_sub(o2f[:], t1c[:], mr3b[:]))
                            v.wait_ge(gq.h, o1ld)
                            vf(v, v.tensor_add(o2f[:], o2f[:], o1l[:]))
                            v.nop().then_inc(sV.h, 1)
                        blk.vector(v_); sV.n += 1
                        o2d = sV.n

                        s_out_base = sS.n
                        g_out_base = gq.n

                        def t_(t_eng, dt=dt, o2d=o2d, s_out_base=s_out_base):
                            t_eng.wait_ge(sV.h, o2d)
                            for lb in range(4):
                                if lb >= 1:
                                    t_eng.wait_ge(sS.h, s_out_base + lb)
                                t_eng.transpose(psO[:], o2f[:, P * lb:P * (lb + 1)],
                                                ident[:]).then_inc(sT.h, 1)
                        blk.tensor(t_); sT.n += 4
                        otr = sT.n

                        def s_(s, otr=otr, g_out_base=g_out_base):
                            s.wait_ge(gq.h, g_out_base)
                            for lb in range(2):
                                s.wait_ge(sT.h, otr - 4 + lb + 1)
                                s.copy(ostg[:, lb * P:(lb + 1) * P], psO[:]).then_inc(sS.h, 1)
                        blk.scalar(s_); sS.n += 2
                        oevA = sS.n

                        def g2a_(g, dt=dt, lh=lh, oevA=oevA, _gb=gq.n):
                            g.wait_ge(gq.h, _gb)
                            g.wait_ge(sS.h, oevA)
                            for lb in range(2):
                                g.dma_start(out[512 * lh + P * lb:512 * lh + P * (lb + 1),
                                                P * dt:P * (dt + 1)],
                                            ostg[:, lb * P:(lb + 1) * P]).then_inc(gq.h, 16)
                        blk.gpsimd(g2a_); gq.n += 32
                        bA = gq.n

                        def s_b(s, otr=otr, bA=bA):
                            s.wait_ge(gq.h, bA)
                            for lb in range(2, 4):
                                s.wait_ge(sT.h, otr - 4 + lb + 1)
                                s.copy(ostg[:, (lb - 2) * P:(lb - 1) * P],
                                       psO[:]).then_inc(sS.h, 1)
                        blk.scalar(s_b); sS.n += 2
                        oevB = sS.n

                        def g2b_(g, dt=dt, lh=lh, oevB=oevB, _gb=bA):
                            g.wait_ge(gq.h, _gb)
                            g.wait_ge(sS.h, oevB)
                            for lb in range(2, 4):
                                g.dma_start(out[512 * lh + P * lb:512 * lh + P * (lb + 1),
                                                P * dt:P * (dt + 1)],
                                            ostg[:, (lb - 2) * P:(lb - 1) * P]).then_inc(gq.h, 16)
                        blk.gpsimd(g2b_); gq.n += 32

                _bc.__exit__(None, None, None)
                _bc = None

        with nc.Block() as blk:
            qq, gg = sq.n, gq.n

            def q_(sync, qq=qq, gg=gg):
                sync.wait_ge(gq.h, gg)
                sync.wait_ge(sq.h, qq)
            blk.sync(q_)

    return nc


def kernel(**inputs):
    x = np.ascontiguousarray(np.asarray(inputs["L0_attn_x"], dtype=np.float32)[0])
    L = x.shape[0]
    assert L == NCORES * LCORE

    f32 = lambda a: np.ascontiguousarray(np.asarray(a, dtype=np.float32))
    shared = dict(
        WinT=f32(np.asarray(inputs["L0_attn_W_in"]).T),
        WskT=f32(np.asarray(inputs["L0_attn_W_skip"]).T),
        Wd1T=f32(np.asarray(inputs["L0_attn_Wd1"]).T),
        Wd2T=f32(np.asarray(inputs["L0_attn_Wd2"]).T),
        WBCT=f32(np.concatenate([np.asarray(inputs["L0_attn_WB"]).T,
                                 np.asarray(inputs["L0_attn_WC"]).T], axis=1)),
        WoutT=f32(np.asarray(inputs["L0_attn_W_out"]).T),
        guT=f32(np.asarray(inputs["L0_ffn_w_gate_up"]).T),
        dwT=f32(np.asarray(inputs["L0_ffn_w_down"]).T),
        A_log=f32(inputs["L0_attn_A_log"]),
        conv_w=f32(np.asarray(inputs["L0_attn_conv_w"]).reshape(E, 4)),
        conv_b=f32(np.asarray(inputs["L0_attn_conv_b"]).reshape(1, E)),
        bd2=f32(np.asarray(inputs["L0_attn_bd2"]).reshape(1, E)),
        W_D=f32(np.asarray(inputs["L0_attn_W_D"]).reshape(1, E)),
        norm_w=f32(np.asarray(inputs["L0_attn_norm_w"]).reshape(1, D)),
        ident=np.eye(P, dtype=np.float32),
    )

    in_maps = []
    for k in range(NCORES):
        oh = np.zeros((1, NCORES), dtype=np.float32)
        oh[0, k] = 1.0
        xs = np.ascontiguousarray(x[k * LCORE:(k + 1) * LCORE])
        if k == 0:
            xp = np.zeros((HALO, D), dtype=np.float32)
        else:
            xp = np.ascontiguousarray(x[k * LCORE - HALO:k * LCORE])
        m = dict(shared)
        m.update(x_shard=xs, x_prev=xp, onehot=oh)
        in_maps.append(m)

    if "nc" not in _NC_CACHE:
        _NC_CACHE["nc"] = build_nc(NCORES)
    nc = _NC_CACHE["nc"]
    res = run_bass_kernel_spmd(nc, in_maps, core_ids=list(range(NCORES)))
    outs = [np.asarray(res.results[k]["out_shard"], dtype=np.float32) for k in range(NCORES)]
    return np.concatenate(outs, axis=0)[None, :, :]


# version 14
# speedup vs baseline: 1.1448x; 1.1448x over previous
import numpy as np
from contextlib import ExitStack

import concourse.bass as bass
import concourse.mybir as mybir
from concourse.bass_utils import run_bass_kernel_spmd

F32 = mybir.dt.float32
BF16 = mybir.dt.bfloat16
AF = mybir.ActivationFunctionType
ALU = mybir.AluOpType
AX = mybir.AxisListType

P = 128
D = 2048
E = 4096
N = 16
DTR = 128
INTER = 8192
LCORE = 1024
HALO = 128
LX = LCORE + HALO
TE = E // P
TD = D // P
NCORES = 8
EPS = 1e-5
NPG = 2
NGR = N // NPG

_NC_CACHE = {}


def build_nc(n_cores=NCORES):
    nc = bass.Bass("TRN2", target_bir_lowering=False)

    xin = nc.dram_tensor("x_shard", [LCORE, D], F32, kind="ExternalInput").ap()
    xprev = nc.dram_tensor("x_prev", [HALO, D], F32, kind="ExternalInput").ap()
    onehot = nc.dram_tensor("onehot", [1, NCORES], F32, kind="ExternalInput").ap()
    ident_in = nc.dram_tensor("ident", [P, P], F32, kind="ExternalInput").ap()

    WinT = nc.dram_tensor("WinT", [D, E], F32, kind="ExternalInput").ap()
    WskT = nc.dram_tensor("WskT", [D, E], F32, kind="ExternalInput").ap()
    Wd1T = nc.dram_tensor("Wd1T", [E, DTR], F32, kind="ExternalInput").ap()
    Wd2T = nc.dram_tensor("Wd2T", [DTR, E], F32, kind="ExternalInput").ap()
    WBCT = nc.dram_tensor("WBCT", [E, 2 * N], F32, kind="ExternalInput").ap()
    WoutT = nc.dram_tensor("WoutT", [E, D], F32, kind="ExternalInput").ap()
    guT = nc.dram_tensor("guT", [D, 2 * INTER], F32, kind="ExternalInput").ap()
    dwT = nc.dram_tensor("dwT", [INTER, D], F32, kind="ExternalInput").ap()
    A_log = nc.dram_tensor("A_log", [E, N], F32, kind="ExternalInput").ap()
    conv_w = nc.dram_tensor("conv_w", [E, 4], F32, kind="ExternalInput").ap()
    conv_b = nc.dram_tensor("conv_b", [1, E], F32, kind="ExternalInput").ap()
    bd2_in = nc.dram_tensor("bd2", [1, E], F32, kind="ExternalInput").ap()
    WD_in = nc.dram_tensor("W_D", [1, E], F32, kind="ExternalInput").ap()
    nw_in = nc.dram_tensor("norm_w", [1, D], F32, kind="ExternalInput").ap()

    out = nc.dram_tensor("out_shard", [LCORE, D], F32, kind="ExternalOutput").ap()

    uc_sp = nc.dram_tensor("uc_sp", [E, LCORE], BF16)
    ssk_sp = nc.dram_tensor("ssk_sp", [E, LCORE], BF16)
    z_sp = nc.dram_tensor("z_sp", [E, LCORE], BF16)
    o1_sp = nc.dram_tensor("o1_sp", [D, LCORE], F32)
    xt_sp = nc.dram_tensor("xt_sp", [D, LCORE], F32)
    winb_sp = nc.dram_tensor("winb_sp", [D, E], BF16)
    wskb_sp = nc.dram_tensor("wskb_sp", [D, E], BF16)
    bc_sp = nc.dram_tensor("bc_sp", [2 * N, LCORE], BF16)
    ag_in = [nc.dram_tensor(f"ag_in_{t}", [P, 2 * N], F32) for t in range(TE)]
    ag_out = [nc.dram_tensor(f"ag_out_{t}", [n_cores * P, 2 * N], F32) for t in range(TE)]

    es = ExitStack()
    with es:
        class Sem:
            def __init__(self, name):
                self.h = es.enter_context(nc.semaphore(name))
                self.n = 0

        sq = Sem("sq"); gq = Sem("gq")
        sT = Sem("sT"); sS = Sem("sS"); sV = Sem("sV"); sP = Sem("sP"); sC = Sem("sC")

        def vf(v, inst):
            inst.then_inc(sV.h, 1); sV.n += 1
            v.wait_ge(sV.h, sV.n)

        def sf(s, inst):
            inst.then_inc(sS.h, 1); sS.n += 1
            s.wait_ge(sS.h, sS.n)

        ident = es.enter_context(nc.sbuf_tensor("ident_sb", [P, P], F32))
        ones_f = es.enter_context(nc.sbuf_tensor("ones_f", [1, P], F32))
        ones_b = es.enter_context(nc.sbuf_tensor("ones_b", [1, P], BF16))
        onescol_b = es.enter_context(nc.sbuf_tensor("onescol_b", [P, 1], BF16))
        oh_bc = es.enter_context(nc.sbuf_tensor("oh_bc", [P, NCORES], F32))
        A_sb = es.enter_context(nc.sbuf_tensor("A_sb", [P, TE * N], F32))
        cw_sb = es.enter_context(nc.sbuf_tensor("cw_sb", [P, TE * 4], F32))
        cb_sb = es.enter_context(nc.sbuf_tensor("cb_sb", [P, TE], F32))
        bd2_sb = es.enter_context(nc.sbuf_tensor("bd2_sb", [P, TE], F32))
        wd_sb = es.enter_context(nc.sbuf_tensor("wd_sb", [P, TE], F32))
        nw_sb = es.enter_context(nc.sbuf_tensor("nw_sb", [P, TD], F32))
        hbuf = es.enter_context(nc.sbuf_tensor("hbuf", [P, N * NCORES], F32))

        _scoped = {}

        def salloc(name, shape, dt):
            cm = nc.sbuf_tensor(name, shape, dt)
            t = cm.__enter__()
            _scoped[name] = cm
            return t

        def sfree(*names):
            for nm in names:
                _scoped.pop(nm).__exit__(None, None, None)

        dpre_b = salloc("dpre_b", [P, LCORE], BF16)
        uc_ring = salloc("uc_ring", [P, 2 * LCORE], BF16)
        ssk_tile = salloc("ssk_tile", [P, LCORE], BF16)

        def barrier():
            with nc.Block() as blk:
                def mk(sem):
                    def f(e):
                        e.nop().then_inc(sem.h, 1)
                    return f
                blk.tensor(mk(sT)); blk.scalar(mk(sS)); blk.vector(mk(sV))
                blk.gpsimd(mk(sP)); blk.sync(mk(sq))
                sT.n += 1; sS.n += 1; sV.n += 1; sP.n += 1; sq.n += 1
                tt, ss, vv, pp, qq, gg = sT.n, sS.n, sV.n, sP.n, sq.n, gq.n

                def wall(e):
                    e.wait_ge(sT.h, tt); e.wait_ge(sS.h, ss); e.wait_ge(sV.h, vv)
                    e.wait_ge(sP.h, pp); e.wait_ge(sq.h, qq); e.wait_ge(gq.h, gg)
                blk.tensor(wall); blk.scalar(wall); blk.vector(wall)
                blk.gpsimd(wall); blk.sync(wall)

        with (
            nc.sbuf_tensor("vrow", [TE, P], F32) as vrow,
            nc.psum_tensor("psK0", [P, P], F32) as psK0,
        ):
            with nc.Block() as blk:
                def q_(sync, _qb=sq.n):
                    sync.wait_ge(sq.h, _qb)
                    sync.dma_start(ident[:], ident_in).then_inc(sq.h, 16)
                    sync.dma_start(oh_bc[0:1, :], onehot).then_inc(sq.h, 16)
                blk.sync(q_); sq.n += 32
                ld0 = sq.n

                def g_(g):
                    g.memset(ones_f[:], 1.0)
                    g.memset(ones_b[:], 1.0)
                    g.memset(onescol_b[:], 1.0)
                    g.memset(hbuf[:], 0.0).then_inc(sP.h, 1)
                blk.gpsimd(g_); sP.n += 1
                ones_done = sP.n

            for src, dst, rows in ((conv_b, cb_sb, TE), (bd2_in, bd2_sb, TE),
                                   (WD_in, wd_sb, TE), (nw_in, nw_sb, TD)):
                with nc.Block() as blk:
                    sprev = sS.n

                    def q_(sync, src=src, rows=rows, sprev=sprev, _qb=sq.n):
                        sync.wait_ge(sq.h, _qb)
                        sync.wait_ge(sS.h, sprev)
                        sync.dma_start(vrow[0:rows, :],
                                       src.rearrange("o (r p) -> (o r) p", p=P)).then_inc(sq.h, 16)
                    blk.sync(q_); sq.n += 16
                    ldv = sq.n

                    def t_(t, rows=rows, ldv=ldv):
                        t.wait_ge(sq.h, ldv)
                        t.wait_ge(sS.h, sS.n)
                        t.transpose(psK0[:, 0:rows], vrow[0:rows, :],
                                    ident[0:rows, 0:rows]).then_inc(sT.h, 1)
                    blk.tensor(t_); sT.n += 1
                    trv = sT.n

                    def s_(s, dst=dst, rows=rows, trv=trv):
                        s.wait_ge(sT.h, trv)
                        s.copy(dst[:, 0:rows], psK0[:, 0:rows]).then_inc(sS.h, 1)
                    blk.scalar(s_); sS.n += 1

            with nc.Block() as blk:
                def q_(sync, _qb=sq.n):
                    sync.wait_ge(sq.h, _qb)
                    for t in range(TE):
                        sync.dma_start(cw_sb[:, 4 * t:4 * t + 4],
                                       conv_w[P * t:P * (t + 1), :]).then_inc(sq.h, 16)
                        sync.dma_start(A_sb[:, N * t:N * (t + 1)],
                                       A_log[P * t:P * (t + 1), :]).then_inc(sq.h, 16)
                blk.sync(q_); sq.n += 32 * TE
                lda = sq.n

                def s_(s):
                    s.wait_ge(sq.h, lda)
                    for t in range(TE):
                        s.activation(A_sb[:, N * t:N * (t + 1)], A_sb[:, N * t:N * (t + 1)],
                                     AF.Exp).then_inc(sS.h, 1)
                blk.scalar(s_); sS.n += TE
                expa = sS.n

                def v_(v):
                    v.wait_ge(sS.h, expa)
                    v.tensor_scalar_mul(A_sb[:], A_sb[:], -1.0).then_inc(sV.h, 1)
                blk.vector(v_); sV.n += 1

                sprev2 = sS.n

                def t_(t, sprev2=sprev2, cur=None):
                    t.wait_ge(sq.h, sq.n)
                    t.wait_ge(sP.h, ones_done)
                    t.wait_ge(sS.h, sprev2)
                    t.matmul(psK0[:, 0:NCORES], ones_f[0:1, :], oh_bc[0:1, :],
                             start=True, stop=True).then_inc(sT.h, 1)
                blk.tensor(t_); sT.n += 1
                ohm = sT.n

                def s2_(s):
                    s.wait_ge(sT.h, ohm)
                    s.copy(oh_bc[:], psK0[:, 0:NCORES]).then_inc(sS.h, 1)
                blk.scalar(s2_); sS.n += 1

        barrier()

        keW_row = salloc("keW_row", [1, E], BF16)
        keS_row = salloc("keS_row", [1, E], BF16)
        keW_f = salloc("keW_f", [1, E], F32)
        keS_f = salloc("keS_f", [1, E], F32)
        with (
            nc.sbuf_tensor("wstg", [P, 2 * E], F32) as wstg,
            nc.sbuf_tensor("wstgb", [P, 2 * E], BF16) as wstgb,
            nc.sbuf_tensor("ke_acc", [P, TE], F32) as ke_acc,
            nc.sbuf_tensor("keT", [TE, P], F32) as keT,
            nc.psum_tensor("psK1", [P, P], F32) as psK1,
            nc.psum_tensor("psK2", [P, TE], F32) as psK2,
        ):
            for wi, (wsrc, wdst, kef, keb) in enumerate(
                    ((WinT, winb_sp, keW_f, keW_row), (WskT, wskb_sp, keS_f, keS_row))):
                for dt in range(TD):
                    buf = dt % 2
                    with nc.Block() as blk:
                        sfree0 = sS.n
                        tfree0 = sT.n

                        def q_(sync, dt=dt, buf=buf, wsrc=wsrc, sfree0=sfree0, tfree0=tfree0,
                               _qb=sq.n):
                            sync.wait_ge(sq.h, _qb)
                            sync.wait_ge(sS.h, sfree0)
                            sync.wait_ge(sT.h, tfree0)
                            sync.dma_start(wstg[:, buf * E:(buf + 1) * E],
                                           wsrc[P * dt:P * (dt + 1), :]).then_inc(sq.h, 16)
                        blk.sync(q_); sq.n += 16
                        ldw_ = sq.n

                        def s_(s, dt=dt, buf=buf, ldw_=ldw_):
                            s.wait_ge(sq.h, ldw_)
                            s.activation(wstgb[:, buf * E:(buf + 1) * E],
                                         wstg[:, buf * E:(buf + 1) * E],
                                         AF.Copy, scale=nw_sb[:, dt:dt + 1]).then_inc(sS.h, 1)
                        blk.scalar(s_); sS.n += 1
                        cast_done = sS.n

                        def q2_(sync, dt=dt, buf=buf, wdst=wdst, cast_done=cast_done,
                                _qb=sq.n):
                            sync.wait_ge(sq.h, _qb)
                            sync.wait_ge(sS.h, cast_done)
                            sync.dma_start(wdst.ap()[P * dt:P * (dt + 1), :],
                                           wstgb[:, buf * E:(buf + 1) * E]).then_inc(sq.h, 16)
                        blk.sync(q2_); sq.n += 16

                        vke_prev = sV.n

                        def t_(t, dt=dt, buf=buf, cast_done=cast_done, vke_prev=vke_prev):
                            t.wait_ge(sS.h, cast_done)
                            t.wait_ge(sV.h, vke_prev)
                            for eb in range(TE):
                                mm = t.matmul(psK2[:, eb:eb + 1],
                                              wstgb[:, buf * E + P * eb:buf * E + P * (eb + 1)],
                                              onescol_b[:],
                                              start=True, stop=True)
                                if eb == TE - 1:
                                    mm.then_inc(sT.h, 1)
                        blk.tensor(t_); sT.n += 1
                        kemm_t = sT.n

                        vke_prev2 = sV.n

                        def vke_(v, dt=dt, kemm_t=kemm_t, vke_prev2=vke_prev2):
                            v.wait_ge(sT.h, kemm_t)
                            v.wait_ge(sV.h, vke_prev2)
                            if dt == 0:
                                v.tensor_copy(ke_acc[:], psK2[:]).then_inc(sV.h, 1)
                            else:
                                v.tensor_add(ke_acc[:], ke_acc[:], psK2[:]).then_inc(sV.h, 1)
                        blk.vector(vke_); sV.n += 1

                with nc.Block() as blk:
                    keacc_v = sV.n

                    def t2_(t, keacc_v=keacc_v, gprev=gq.n):
                        t.wait_ge(sV.h, keacc_v)
                        t.wait_ge(gq.h, gprev)
                        t.transpose(psK1[0:TE, :], ke_acc[:], ident[:]).then_inc(sT.h, 1)
                    blk.tensor(t2_); sT.n += 1
                    ketr = sT.n

                    def s15_(s, ketr=ketr):
                        s.wait_ge(sT.h, ketr)
                        s.copy(keT[:], psK1[0:TE, :]).then_inc(sS.h, 1)
                    blk.scalar(s15_); sS.n += 1
                    ketc = sS.n

                    def g_(g, kef=kef, ketc=ketc, _gb=gq.n):
                        g.wait_ge(gq.h, _gb)
                        g.wait_ge(sS.h, ketc)
                        for eb in range(TE):
                            g.dma_start(kef[0:1, P * eb:P * (eb + 1)],
                                        keT[eb:eb + 1, :]).then_inc(gq.h, 16)
                    blk.gpsimd(g_); gq.n += 16 * TE
                    kedma = gq.n

                    def s2_(s, keb=keb, kef=kef, kedma=kedma):
                        s.wait_ge(gq.h, kedma)
                        s.copy(keb[:], kef[:]).then_inc(sS.h, 1)
                    blk.scalar(s2_); sS.n += 1

        barrier()

        xT_bf = salloc("xT_bf", [P, TD * LX], BF16)
        rb_bc = salloc("rb_bc", [P, LX], F32)
        r_row = salloc("r_row", [1, LX], F32)
        nmr_rowb = salloc("nmr_rowb", [1, LX], BF16)
        nmr_f = salloc("nmr_f", [1, LX], F32)
        es2 = ExitStack()
        sx = es2.enter_context(nc.sbuf_tensor("sx", [P, 2 * D], F32))
        xstage = es2.enter_context(nc.sbuf_tensor("xstage", [P, TD * P], F32))
        sxsq = es2.enter_context(nc.sbuf_tensor("sxsq", [P, D], F32))
        scol = es2.enter_context(nc.sbuf_tensor("scol", [P, 16], F32))
        rcols = es2.enter_context(nc.sbuf_tensor("rcols", [P, 9], F32))
        nmrcols = es2.enter_context(nc.sbuf_tensor("nmrcols", [P, 9], F32))
        rowst = es2.enter_context(nc.sbuf_tensor("rowst", [9, P], F32))
        rowst2 = es2.enter_context(nc.sbuf_tensor("rowst2", [9, P], F32))
        psX = es2.enter_context(nc.psum_tensor("psX", [P, P], F32))
        psX2 = es2.enter_context(nc.psum_tensor("psX2", [P, P], F32))
        with es2:
            NJ = 9
            for j in range(NJ):
                buf = j % 2
                with nc.Block() as blk:
                    vprev = sV.n
                    tprev = sT.n
                    s_base = sS.n

                    def q_(sync, j=j, buf=buf, vprev=vprev, tprev=tprev, _qb=sq.n):
                        sync.wait_ge(sq.h, _qb)
                        sync.wait_ge(sV.h, vprev)
                        sync.wait_ge(sT.h, tprev)
                        src = xprev if j == 0 else xin[P * (j - 1):P * j, :]
                        sync.dma_start(sx[:, buf * D:(buf + 1) * D], src).then_inc(sq.h, 16)
                    blk.sync(q_); sq.n += 16
                    ldx = sq.n

                    def s_(s, buf=buf, ldx=ldx):
                        s.wait_ge(sq.h, ldx)
                        s.wait_ge(sV.h, sV.n)
                        s.activation(sxsq[:], sx[:, buf * D:(buf + 1) * D], AF.Square).then_inc(sS.h, 1)
                    blk.scalar(s_); sS.n += 1
                    sqd = sS.n
                    s_base = sS.n

                    def v_(v, j=j, buf=buf, ldx=ldx, sqd=sqd):
                        x_ = sx[:, buf * D:(buf + 1) * D]
                        v.wait_ge(sq.h, ldx)
                        vf(v, v.reduce_sum(scol[:, 0:1], x_, AX.X))
                        v.wait_ge(sS.h, sqd)
                        vf(v, v.reduce_sum(scol[:, 1:2], sxsq[:], AX.X))
                        vf(v, v.tensor_scalar_mul(scol[:, 2:3], scol[:, 0:1], 1.0 / D))
                        vf(v, v.tensor_mul(scol[:, 3:4], scol[:, 2:3], scol[:, 2:3]))
                        vf(v, v.tensor_scalar_mul(scol[:, 4:5], scol[:, 1:2], 1.0 / D))
                        vf(v, v.tensor_sub(scol[:, 5:6], scol[:, 4:5], scol[:, 3:4]))
                        vf(v, v.tensor_scalar_add(scol[:, 6:7], scol[:, 5:6], EPS))
                        vf(v, v.reciprocal(scol[:, 7:8], scol[:, 6:7]))
                        vf(v, v.tensor_mul(scol[:, 8:9], scol[:, 5:6], scol[:, 7:8]))
                        v.tensor_scalar_add(scol[:, 9:10], scol[:, 8:9], EPS).then_inc(sV.h, 1)
                    blk.vector(v_); sV.n += 1
                    vstat = sV.n

                    def s2_(s, vstat=vstat):
                        s.wait_ge(sV.h, vstat)
                        sf(s, s.activation(scol[:, 10:11], scol[:, 6:7], AF.Ln))
                        s.activation(scol[:, 11:12], scol[:, 9:10], AF.Ln).then_inc(sS.h, 1)
                    blk.scalar(s2_); sS.n += 1
                    slog = sS.n

                    def v2_(v, slog=slog):
                        v.wait_ge(sS.h, slog)
                        v.tensor_add(scol[:, 12:13], scol[:, 10:11], scol[:, 11:12]).then_inc(sV.h, 1)
                    blk.vector(v2_); sV.n += 1
                    vln = sV.n

                    def s3_(s, vln=vln):
                        s.wait_ge(sV.h, vln)
                        s.activation(scol[:, 13:14], scol[:, 12:13], AF.Exp, scale=-0.5).then_inc(sS.h, 1)
                    blk.scalar(s3_); sS.n += 1
                    sr = sS.n

                    def v3_(v, j=j, sr=sr):
                        v.wait_ge(sS.h, sr)
                        vf(v, v.tensor_copy(rcols[:, j:j + 1], scol[:, 13:14]))
                        vf(v, v.tensor_mul(scol[:, 14:15], scol[:, 2:3], scol[:, 13:14]))
                        v.tensor_scalar_mul(nmrcols[:, j:j + 1], scol[:, 14:15], -1.0).then_inc(sV.h, 1)
                    blk.vector(v3_); sV.n += 1

                    s_evac_base = sS.n
                    g_evac_base = gq.n
                    nse = 2 if j >= 1 else 1

                    def t_(t, j=j, buf=buf, ldx=ldx, s_evac_base=s_evac_base, nse=nse):
                        t.wait_ge(sq.h, ldx)
                        x_ = sx[:, buf * D:(buf + 1) * D]
                        for dt in range(TD):
                            ps = psX if dt % 2 == 0 else psX2
                            if dt >= 2:
                                t.wait_ge(sS.h, s_evac_base + nse * (dt - 1))
                            t.transpose(ps[:], x_[:, P * dt:P * (dt + 1)], ident[:]).then_inc(sT.h, 1)
                    blk.tensor(t_); sT.n += TD
                    ttr = sT.n

                    def s4_(s, j=j, ttr=ttr, g_evac_base=g_evac_base):
                        if j >= 1:
                            s.wait_ge(gq.h, g_evac_base)
                        for dt in range(TD):
                            s.wait_ge(sT.h, ttr - TD + dt + 1)
                            ps = psX if dt % 2 == 0 else psX2
                            s.copy(xT_bf[:, dt * LX + P * j:dt * LX + P * (j + 1)],
                                   ps[:]).then_inc(sS.h, 1)
                            if j >= 1:
                                s.copy(xstage[:, dt * P:(dt + 1) * P],
                                       ps[:]).then_inc(sS.h, 1)
                    blk.scalar(s4_); sS.n += nse * TD
                    sxe = sS.n

                    if j >= 1:
                        def g_(g, j=j, sxe=sxe, nse=nse, _gb=gq.n):
                            g.wait_ge(gq.h, _gb)
                            for dt in range(TD):
                                g.wait_ge(sS.h, sxe - nse * TD + nse * (dt + 1))
                                g.dma_start(xt_sp.ap()[P * dt:P * (dt + 1),
                                                       P * (j - 1):P * j],
                                            xstage[:, dt * P:(dt + 1) * P]).then_inc(gq.h, 16)
                        blk.gpsimd(g_); gq.n += 16 * TD

            with nc.Block() as blk:
                vdone = sV.n
                gdone = gq.n
                sdone = sS.n

                def t_(t, vdone=vdone, gdone=gdone, sdone=sdone):
                    t.wait_ge(sV.h, vdone)
                    t.wait_ge(gq.h, gdone)
                    t.wait_ge(sS.h, sdone)
                    t.transpose(psX[0:9, :], rcols[:], ident[:]).then_inc(sT.h, 1)
                    t.transpose(psX2[0:9, :], nmrcols[:], ident[:]).then_inc(sT.h, 1)
                blk.tensor(t_); sT.n += 2
                rtr = sT.n

                def se_(s, rtr=rtr):
                    s.wait_ge(sT.h, rtr)
                    s.copy(rowst[:], psX[0:9, :])
                    s.copy(rowst2[:], psX2[0:9, :]).then_inc(sS.h, 1)
                blk.scalar(se_); sS.n += 1
                rse = sS.n

                def g_(g, rse=rse, _gb=gq.n):
                    g.wait_ge(gq.h, _gb)
                    g.wait_ge(sS.h, rse)
                    for j in range(9):
                        g.dma_start(r_row[0:1, P * j:P * (j + 1)], rowst[j:j + 1, :]).then_inc(gq.h, 16)
                        g.dma_start(nmr_f[0:1, P * j:P * (j + 1)], rowst2[j:j + 1, :]).then_inc(gq.h, 16)
                blk.gpsimd(g_); gq.n += 16 * 18
                rdma = gq.n

                def s_(s, rdma=rdma):
                    s.wait_ge(gq.h, rdma)
                    s.copy(nmr_rowb[:], nmr_f[:]).then_inc(sS.h, 1)
                blk.scalar(s_); sS.n += 1

                sb_base = sS.n

                def t2_(t, rdma=rdma, sb_base=sb_base):
                    t.wait_ge(gq.h, rdma)
                    for i in range(9):
                        ps = psX if i % 2 == 0 else psX2
                        if i >= 2:
                            t.wait_ge(sS.h, sb_base + i - 1)
                        t.matmul(ps[:], ones_f[0:1, :], r_row[0:1, P * i:P * (i + 1)],
                                 start=True, stop=True).then_inc(sT.h, 1)
                blk.tensor(t2_); sT.n += 9
                btr = sT.n

                def s5_(s, btr=btr):
                    for i in range(9):
                        s.wait_ge(sT.h, btr - 9 + i + 1)
                        ps = psX if i % 2 == 0 else psX2
                        s.copy(rb_bc[:, P * i:P * (i + 1)], ps[:]).then_inc(sS.h, 1)
                blk.scalar(s5_); sS.n += 9

        barrier()

        WS = 160
        ucmm_done = [0] * TE
        ucspill_done = [0] * TE
        ucdone_arr = [0] * TE
        sskd_arr = [0] * TE
        es2 = ExitStack()
        wv = es2.enter_context(nc.sbuf_tensor("wv", [P, 2 * TD * P], BF16))
        wvs = es2.enter_context(nc.sbuf_tensor("wvs", [P, 2 * TD * P], BF16))
        wsm = es2.enter_context(nc.sbuf_tensor("wsm", [P, 2 * WS], BF16))
        wsm_f = es2.enter_context(nc.sbuf_tensor("wsm_f", [P, 2 * WS], F32))
        up_ext = es2.enter_context(nc.sbuf_tensor("up_ext", [P, LX], F32))
        cva = es2.enter_context(nc.sbuf_tensor("cva", [P, LCORE], F32))
        cvb = es2.enter_context(nc.sbuf_tensor("cvb", [P, LCORE], F32))
        skf = es2.enter_context(nc.sbuf_tensor("skf", [P, LCORE], F32))
        psU = es2.enter_context(nc.psum_tensor("psU", [P, 512], F32))
        psU2 = es2.enter_context(nc.psum_tensor("psU2", [P, 512], F32))
        psDP = es2.enter_context(nc.psum_tensor("psDP", [P, LCORE], F32))
        psBC = es2.enter_context(nc.psum_tensor("psBC", [2 * N, LCORE], F32))
        with es2:
            UCHUNKS = [(0, 512), (512, 1024), (1024, LX)]
            _bc3 = None
            for t in range(TE):
                buf = t % 2
                ucslot = t % 2
                if t % 4 == 0:
                    if _bc3 is not None:
                        _bc3.__exit__(None, None, None)
                    _bc3 = nc.Block()
                    blk = _bc3.__enter__()
                if True:
                    if t == 0:
                        s3tbase = sT.n
                        s3t_arr = {}
                        s3wc_arr = {}
                    tprev = s3t_arr.get(t - 2, s3tbase)

                    def q_(sync, t=t, buf=buf, tprev=tprev, _qb=sq.n,
                           _wc=s3wc_arr.get(t - 2, 0)):
                        sync.wait_ge(sq.h, _qb)
                        sync.wait_ge(sT.h, tprev)
                        if t >= 2:
                            sync.wait_ge(sV.h, _wc)
                        sync.dma_start(
                            wv[:, buf * TD * P:(buf + 1) * TD * P].rearrange("p (a c) -> p a c", c=P),
                            winb_sp.ap()[:, P * t:P * (t + 1)].rearrange("(a p) c -> p a c", p=P),
                        ).then_inc(sq.h, 16)
                        sync.dma_start(
                            wvs[:, buf * TD * P:(buf + 1) * TD * P].rearrange("p (a c) -> p a c", c=P),
                            wskb_sp.ap()[:, P * t:P * (t + 1)].rearrange("(a p) c -> p a c", p=P),
                        ).then_inc(sq.h, 16)
                        sync.dma_start(wsm_f[:, buf * WS:buf * WS + P],
                                       Wd1T[P * t:P * (t + 1), :]).then_inc(sq.h, 16)
                        sync.dma_start(wsm_f[:, buf * WS + P:buf * WS + WS],
                                       WBCT[P * t:P * (t + 1), :]).then_inc(sq.h, 16)
                    blk.sync(q_); sq.n += 64
                    ldw1 = sq.n

                    def v00_(v, buf=buf, ldw1=ldw1):
                        v.wait_ge(sq.h, ldw1)
                        v.tensor_copy(wsm[:, buf * WS:(buf + 1) * WS],
                                      wsm_f[:, buf * WS:(buf + 1) * WS]).then_inc(sV.h, 1)
                    blk.vector(v00_); sV.n += 1
                    wdcast = sV.n
                    s3wc_arr[t] = wdcast

                    vprev_evac = sV.n

                    def t_(t_eng, t=t, buf=buf, ldw1=ldw1, vprev_evac=vprev_evac):
                        t_eng.wait_ge(sq.h, ldw1)
                        t_eng.wait_ge(sV.h, vprev_evac)
                        for ci, (c0, c1) in enumerate(UCHUNKS):
                            ps = psU if ci % 2 == 0 else psU2
                            cw = c1 - c0
                            if ci == 2:
                                t_eng.wait_ge(sV.h, vprev_evac + 1)
                            for dt in range(TD):
                                t_eng.matmul(ps[:, 0:cw],
                                             wv[:, buf * TD * P + dt * P:buf * TD * P + (dt + 1) * P],
                                             xT_bf[:, dt * LX + c0:dt * LX + c1],
                                             start=(dt == 0), stop=False)
                            t_eng.matmul(ps[:, 0:cw], keW_row[0:1, P * t:P * (t + 1)],
                                         nmr_rowb[0:1, c0:c1],
                                         start=False, stop=True).then_inc(sT.h, 1)
                    blk.tensor(t_); sT.n += 3
                    umm = sT.n

                    def v_(v, umm=umm):
                        for ci, (c0, c1) in enumerate(UCHUNKS):
                            cw = c1 - c0
                            ps = psU if ci % 2 == 0 else psU2
                            v.wait_ge(sT.h, umm - 3 + ci + 1)
                            v.tensor_mul(up_ext[:, c0:c1], ps[:, 0:cw],
                                         rb_bc[:, c0:c1]).then_inc(sV.h, 1)
                    blk.vector(v_); sV.n += 3
                    uev = sV.n

                    def v2_(v, t=t, uev=uev):
                        if t >= 1:
                            v.wait_ge(sS.h, ucdone_arr[t - 1])
                        v.wait_ge(sV.h, uev)
                        vf(v, v.tensor_single_scalar(cva[:], up_ext[:, 125:125 + LCORE],
                                                     cw_sb[:, 4 * t:4 * t + 1], ALU.mult))
                        vf(v, v.scalar_tensor_tensor(cvb[:], up_ext[:, 126:126 + LCORE],
                                                     cw_sb[:, 4 * t + 1:4 * t + 2], cva[:],
                                                     ALU.mult, ALU.add))
                        vf(v, v.scalar_tensor_tensor(cva[:], up_ext[:, 127:127 + LCORE],
                                                     cw_sb[:, 4 * t + 2:4 * t + 3], cvb[:],
                                                     ALU.mult, ALU.add))
                        v.scalar_tensor_tensor(cvb[:], up_ext[:, 128:128 + LCORE],
                                               cw_sb[:, 4 * t + 3:4 * t + 4], cva[:],
                                               ALU.mult, ALU.add).then_inc(sV.h, 1)
                    blk.vector(v2_); sV.n += 1
                    convd = sV.n

                    def s2_(s, t=t, ucslot=ucslot, convd=convd):
                        s.wait_ge(sV.h, convd)
                        if t >= 2:
                            s.wait_ge(sT.h, ucmm_done[t - 2])
                            s.wait_ge(gq.h, ucspill_done[t - 2])
                        s.activation(uc_ring[:, ucslot * LCORE:(ucslot + 1) * LCORE], cvb[:],
                                     AF.Silu, bias=cb_sb[:, t:t + 1]).then_inc(sS.h, 1)
                    blk.scalar(s2_); sS.n += 1
                    ucdone = sS.n
                    ucdone_arr[t] = ucdone

                    def g_(g, t=t, ucslot=ucslot, ucdone=ucdone, _gb=gq.n):
                        g.wait_ge(gq.h, _gb)
                        g.wait_ge(sS.h, ucdone)
                        g.dma_start(uc_sp.ap()[P * t:P * (t + 1), :],
                                    uc_ring[:, ucslot * LCORE:(ucslot + 1) * LCORE]).then_inc(gq.h, 16)
                    blk.gpsimd(g_); gq.n += 16
                    ucspill_done[t] = gq.n

                    def t2_(t_eng, t=t, buf=buf, ucslot=ucslot, ucdone=ucdone, wdcast=wdcast,
                            _dummy=None):
                        t_eng.wait_ge(sS.h, ucdone)
                        t_eng.wait_ge(sV.h, wdcast)
                        uc = uc_ring[:, ucslot * LCORE:(ucslot + 1) * LCORE]
                        for lc in range(2):
                            t_eng.matmul(psDP[:, 512 * lc:512 * (lc + 1)],
                                         wsm[:, buf * WS:buf * WS + P],
                                         uc[:, 512 * lc:512 * (lc + 1)],
                                         start=(t == 0), stop=(t == TE - 1), skip_group_check=True)
                        for lc in range(2):
                            mm = t_eng.matmul(psBC[:, 512 * lc:512 * (lc + 1)],
                                              wsm[:, buf * WS + P:buf * WS + WS],
                                              uc[:, 512 * lc:512 * (lc + 1)],
                                              start=(t == 0), stop=(t == TE - 1), skip_group_check=True)
                            if lc == 1:
                                mm.then_inc(sT.h, 1)
                    blk.tensor(t2_); sT.n += 1
                    ucmm_done[t] = sT.n

                    def t3_(t_eng, t=t, buf=buf, uev=uev):
                        t_eng.wait_ge(sV.h, uev)
                        for lc in range(2):
                            ps = psU if lc == 0 else psU2
                            c0 = HALO + 512 * lc
                            for dt in range(TD):
                                t_eng.matmul(ps[:],
                                             wvs[:, buf * TD * P + dt * P:buf * TD * P + (dt + 1) * P],
                                             xT_bf[:, dt * LX + c0:dt * LX + c0 + 512],
                                             start=(dt == 0), stop=False)
                            t_eng.matmul(ps[:], keS_row[0:1, P * t:P * (t + 1)],
                                         nmr_rowb[0:1, c0:c0 + 512],
                                         start=False, stop=True).then_inc(sT.h, 1)
                    blk.tensor(t3_); sT.n += 2
                    skmm = sT.n
                    s3t_arr[t] = skmm

                    def v3_(v, t=t, skmm=skmm):
                        if t >= 1:
                            v.wait_ge(sS.h, sskd_arr[t - 1])
                        for lc in range(2):
                            ps = psU if lc == 0 else psU2
                            v.wait_ge(sT.h, skmm - 2 + lc + 1)
                            v.tensor_mul(skf[:, 512 * lc:512 * (lc + 1)], ps[:],
                                         rb_bc[:, HALO + 512 * lc:HALO + 512 * (lc + 1)]).then_inc(sV.h, 1)
                    blk.vector(v3_); sV.n += 2
                    skev = sV.n

                    def s3_(s, t=t, skev=skev):
                        s.wait_ge(sV.h, skev)
                        if t >= 1:
                            s.wait_ge(gq.h, _sskspill[t - 1])
                        s.activation(ssk_tile[:], skf[:], AF.Silu).then_inc(sS.h, 1)
                    blk.scalar(s3_); sS.n += 1
                    sskd = sS.n
                    sskd_arr[t] = sskd

                    def g2_(g, t=t, sskd=sskd, _gb=gq.n):
                        g.wait_ge(gq.h, _gb)
                        g.wait_ge(sS.h, sskd)
                        g.dma_start(ssk_sp.ap()[P * t:P * (t + 1), :], ssk_tile[:]).then_inc(gq.h, 16)
                    blk.gpsimd(g2_); gq.n += 16
                    if t == 0:
                        _sskspill = [0] * TE
                    _sskspill[t] = gq.n

            _bc3.__exit__(None, None, None)
            _bc3 = None
            with nc.Block() as blk:
                tall = sT.n

                def s_(s, tall=tall):
                    s.wait_ge(sT.h, tall)
                    s.wait_ge(gq.h, _sskspill[TE - 1])
                    s.copy(dpre_b[:], psDP[:])
                    s.copy(ssk_tile[0:2 * N, :], psBC[:]).then_inc(sS.h, 1)
                blk.scalar(s_); sS.n += 1
                bcev = sS.n

                def g_(g, bcev=bcev, _gb=gq.n):
                    g.wait_ge(gq.h, _gb)
                    g.wait_ge(sS.h, bcev)
                    g.dma_start(bc_sp.ap(), ssk_tile[0:2 * N, :]).then_inc(gq.h, 16)
                blk.gpsimd(g_); gq.n += 16

        barrier()
        sfree("nmr_f", "nmr_rowb", "r_row", "rb_bc", "xT_bf",
              "keS_f", "keW_f", "keS_row", "keW_row")

        HS = 24
        es2 = ExitStack()
        wd2s = es2.enter_context(nc.sbuf_tensor("wd2s", [P, 2 * P], BF16))
        wd2f = es2.enter_context(nc.sbuf_tensor("wd2f", [P, 2 * P], F32))
        spx = es2.enter_context(nc.sbuf_tensor("spx", [P, LCORE], F32))
        spa = es2.enter_context(nc.sbuf_tensor("spa", [P, LCORE], F32))
        delta_f = es2.enter_context(nc.sbuf_tensor("delta_f", [P, 2 * LCORE], F32))
        srel = es2.enter_context(nc.sbuf_tensor("srel", [P, 2 * LCORE], F32))
        du_f = es2.enter_context(nc.sbuf_tensor("du_f", [P, 2 * LCORE], F32))
        a_slab = es2.enter_context(nc.sbuf_tensor("a_slab", [P, 2 * NPG * LCORE], F32))
        w_slab = es2.enter_context(nc.sbuf_tensor("w_slab", [P, 2 * NPG * LCORE], F32))
        h_slab = es2.enter_context(nc.sbuf_tensor("h_slab", [P, HS * LCORE], BF16))
        p_slab = es2.enter_context(nc.sbuf_tensor("p_slab", [P, 2 * NPG * LCORE], BF16))
        ht_buf = es2.enter_context(nc.sbuf_tensor("ht_buf", [P, NPG * LCORE], F32))
        q_buf = es2.enter_context(nc.sbuf_tensor("q_buf", [P, (N // 2) * LCORE], BF16))
        B_sb = es2.enter_context(nc.sbuf_tensor("B_sb", [P, 4 * LCORE], BF16))
        C_sb = es2.enter_context(nc.sbuf_tensor("C_sb", [P, 4 * LCORE], BF16))
        y_f = es2.enter_context(nc.sbuf_tensor("y_f", [P, LCORE], F32))
        y_f2 = es2.enter_context(nc.sbuf_tensor("y_f2", [P, LCORE], F32))
        trA = es2.enter_context(nc.sbuf_tensor("trA", [P, LCORE], F32))
        trB = es2.enter_context(nc.sbuf_tensor("trB", [P, LCORE], F32))
        trC = es2.enter_context(nc.sbuf_tensor("trC", [P, LCORE], F32))
        zt_bf = es2.enter_context(nc.sbuf_tensor("zt_bf", [P, LCORE], BF16))
        z_bf = es2.enter_context(nc.sbuf_tensor("z_bf", [P, LCORE], BF16))
        stout = es2.enter_context(nc.sbuf_tensor("stout", [P, 2 * N], F32))
        gsb = es2.enter_context(nc.sbuf_tensor("gsb", [P, NCORES * 2 * N], F32))
        hcur = es2.enter_context(nc.sbuf_tensor("hcur", [P, N], F32))
        htmp = es2.enter_context(nc.sbuf_tensor("htmp", [P, N], F32))
        hsel = es2.enter_context(nc.sbuf_tensor("hsel", [P, N], F32))
        psDel = es2.enter_context(nc.psum_tensor("psDel", [P, LCORE], F32))
        with es2:
            ag_done = [0] * TE
            std_done = [0] * TE
            zv_done = [0] * TE
            yred_done = {}
            hsel_done = [0] * TE
            scans_done_v = [0] * TE
            scv = {}
            wpool = {}
            qpool = {}
            htv = {}
            l1h = {}
            dready_a = [0] * TE
            sskl_done = [0] * TE

            def hslot(t, n):
                return ((N * t + n) % HS) * LCORE

            def emit_delta(t):
                buf = t % 2
                slot = t % 2
                with nc.Block() as blk:
                    tprev = sT.n

                    def q_(sync, t=t, buf=buf, tprev=tprev, _qb=sq.n):
                        sync.wait_ge(sq.h, _qb)
                        sync.wait_ge(sT.h, tprev)
                        sync.dma_start(wd2f[:, buf * P:(buf + 1) * P],
                                       Wd2T[:, P * t:P * (t + 1)]).then_inc(sq.h, 16)
                    blk.sync(q_); sq.n += 16
                    ldd = sq.n

                    def g_(g, t=t, slot=slot, _gb=gq.n):
                        g.wait_ge(gq.h, _gb)
                        if t >= 2:
                            g.wait_ge(sP.h, zv_done[t - 2])
                        g.dma_start(uc_ring[:, slot * LCORE:(slot + 1) * LCORE],
                                    uc_sp.ap()[P * t:P * (t + 1), :]).then_inc(gq.h, 16)
                    blk.gpsimd(g_); gq.n += 16
                    lduc = gq.n

                    def s_(s, buf=buf, ldd=ldd):
                        s.wait_ge(sq.h, ldd)
                        s.copy(wd2s[:, buf * P:(buf + 1) * P],
                               wd2f[:, buf * P:(buf + 1) * P]).then_inc(sS.h, 1)
                    blk.scalar(s_); sS.n += 1
                    w2c = sS.n

                    sfree_ = sS.n

                    def t_(t_eng, buf=buf, w2c=w2c, sfree_=sfree_):
                        t_eng.wait_ge(sS.h, w2c)
                        t_eng.wait_ge(sS.h, sfree_)
                        for lc in range(2):
                            mm = t_eng.matmul(psDel[:, 512 * lc:512 * (lc + 1)],
                                              wd2s[:, buf * P:(buf + 1) * P],
                                              dpre_b[:, 512 * lc:512 * (lc + 1)],
                                              start=True, stop=True)
                            if lc == 1:
                                mm.then_inc(sT.h, 1)
                    blk.tensor(t_); sT.n += 1
                    dmm = sT.n

                    vfree = sV.n

                    def v0_(v, t=t, dmm=dmm, vfree=vfree):
                        v.wait_ge(sT.h, dmm)
                        v.wait_ge(sV.h, vfree)
                        v.tensor_scalar_add(spx[:], psDel[:], bd2_sb[:, t:t + 1]).then_inc(sV.h, 1)
                    blk.vector(v0_); sV.n += 1
                    vbias = sV.n

                    def s2_(s, vbias=vbias):
                        s.wait_ge(sV.h, vbias)
                        sf(s, s.activation(spa[:], spx[:], AF.Abs))
                        sf(s, s.activation(spa[:], spa[:], AF.Exp, scale=-1.0))
                        sf(s, s.activation(spa[:], spa[:], AF.Ln, bias=1.0))
                        s.activation(spx[:], spx[:], AF.Relu).then_inc(sS.h, 1)
                    blk.scalar(s2_); sS.n += 1
                    spd = sS.n

                    def v_(v, slot=slot, spd=spd, lduc=lduc,
                           _df=wpool.get((t - 2, N - 1), 0)):
                        v.wait_ge(sS.h, spd)
                        v.wait_ge(sP.h, _df)
                        dslab = delta_f[:, slot * LCORE:(slot + 1) * LCORE]
                        vf(v, v.tensor_add(dslab, spa[:], spx[:]))
                        v.tensor_tensor_scan(srel[:, slot * LCORE:(slot + 1) * LCORE],
                                             dslab, dslab, 0.0, ALU.add, ALU.bypass)
                        v.wait_ge(gq.h, lduc)
                        v.tensor_mul(du_f[:, slot * LCORE:(slot + 1) * LCORE], dslab,
                                     uc_ring[:, slot * LCORE:(slot + 1) * LCORE]).then_inc(sV.h, 1)
                    blk.vector(v_); sV.n += 1
                    dready_a[t] = sV.n

            def emit_combine(blk, t):
                if True:
                    def g_(g, t=t, _gb=gq.n):
                        g.wait_ge(gq.h, _gb)
                        g.wait_ge(sC.h, ag_done[t])
                        if t >= 1:
                            g.wait_ge(sV.h, hsel_done[t - 1])
                        g.dma_start(
                            gsb[:, 0:n_cores * 2 * N].rearrange("p (r c) -> p r c", r=n_cores),
                            ag_out[t].ap().rearrange("(r p) c -> p r c", p=P),
                        ).then_inc(gq.h, 16)
                    blk.gpsimd(g_); gq.n += 16
                    gld = gq.n

                    def v_(v, t=t, gld=gld):
                        v.wait_ge(gq.h, gld)
                        vf(v, v.memset(hcur[:], 0.0))
                        for r in range(n_cores):
                            vf(v, v.tensor_copy(bass.AP(hbuf, r, [[N * NCORES, P], [NCORES, N]]),
                                                hcur[:]))
                            if r < n_cores - 1:
                                vf(v, v.tensor_mul(htmp[:], gsb[:, r * 2 * N + N:(r + 1) * 2 * N],
                                                   hcur[:]))
                                vf(v, v.tensor_add(hcur[:], htmp[:],
                                                   gsb[:, r * 2 * N:r * 2 * N + N]))
                        vf(v, v.tensor_mul(hbuf[:], hbuf[:],
                                           bass.AP(oh_bc, 0, [[NCORES, P], [0, N], [1, NCORES]])))
                        v.tensor_reduce(hsel[:], hbuf.ap().rearrange("p (n r) -> p n r", r=NCORES),
                                        AX.X, ALU.add).then_inc(sV.h, 1)
                    blk.vector(v_); sV.n += 1
                    hsel_done[t] = sV.n

            def emit_corr_group(blk, t, g):
                slot = t % 2
                gbuf = g % 2
                if True:
                    def q_(sync, t=t, g=g, _qb=sq.n,
                           _fr=[qpool.get((t, g * NPG + i - 4),
                                          qpool.get((t - 1, g * NPG + i + 12), 0))
                                for i in range(NPG)]):
                        sync.wait_ge(sq.h, _qb)
                        for i in range(NPG):
                            n = g * NPG + i
                            sync.wait_ge(sP.h, _fr[i])
                            sync.dma_start(
                                C_sb[:, (n % 4) * LCORE:(n % 4 + 1) * LCORE],
                                bass.AP(bc_sp, (N + n) * LCORE, [[0, P], [1, LCORE]]),
                            ).then_inc(sq.h, 16)
                    blk.sync(q_); sq.n += 16 * NPG
                    cld = sq.n

                    def s_(s, t=t, g=g, gbuf=gbuf, slot=slot,
                           _pf=(htv.get((t, (g - 1) * NPG), 0) if g >= 2
                                else (htv.get((t - 1, (g + 6) * NPG), 0) if t >= 1 else 0))):
                        s.wait_ge(sV.h, _pf)
                        for i in range(NPG):
                            n = g * NPG + i
                            s.activation(
                                p_slab[:, (gbuf * NPG + i) * LCORE:(gbuf * NPG + i + 1) * LCORE],
                                srel[:, slot * LCORE:(slot + 1) * LCORE],
                                AF.Exp, scale=A_sb[:, N * t + n:N * t + n + 1]).then_inc(sS.h, 1)
                    blk.scalar(s_); sS.n += NPG
                    pexp = sS.n

                    for i in range(NPG):
                        htv[(t, g * NPG + i)] = sV.n + i + 1

                    def v_(v, t=t, g=g, gbuf=gbuf, slot=slot, pexp=pexp,
                           _hf=[(qpool.get((t, g * NPG + i - 2),
                                           qpool.get((t - 1, g * NPG + i + 14), 0)))
                                for i in range(NPG)]):
                        v.wait_ge(sS.h, pexp)
                        v.wait_ge(sV.h, hsel_done[t])
                        for i in range(NPG):
                            n = g * NPG + i
                            v.wait_ge(sP.h, _hf[i])
                            v.scalar_tensor_tensor(
                                ht_buf[:, i * LCORE:(i + 1) * LCORE],
                                p_slab[:, (gbuf * NPG + i) * LCORE:(gbuf * NPG + i + 1) * LCORE],
                                hsel[:, n:n + 1],
                                h_slab[:, hslot(t, n):hslot(t, n) + LCORE],
                                ALU.mult, ALU.add).then_inc(sV.h, 1)
                    blk.vector(v_); sV.n += NPG

                    def p2_(v, t=t, g=g, cld=cld,
                            _qf=[(l1h.get((t - 1, 1, (g * NPG + i) // 2), 0)
                                  if g * NPG + i < N // 2
                                  else l1h.get((t, 0, ((g * NPG + i) % 8) // 2), 0))
                                 for i in range(NPG)]):
                        v.wait_ge(sq.h, cld)
                        for i in range(NPG):
                            n = g * NPG + i
                            v.wait_ge(sP.h, _qf[i])
                            v.wait_ge(sV.h, htv[(t, n)])
                            v.tensor_mul(
                                q_buf[:, (n % (N // 2)) * LCORE:(n % (N // 2) + 1) * LCORE],
                                ht_buf[:, i * LCORE:(i + 1) * LCORE],
                                C_sb[:, (n % 4) * LCORE:(n % 4 + 1) * LCORE]).then_inc(sP.h, 1)
                    blk.vector(p2_)
                    for i in range(NPG):
                        sP.n += 1
                        qpool[(t, g * NPG + i)] = sP.n

                    half = g // 4
                    pi = g % 4
                    tgt = (trA, trB, trC, (y_f if half == 0 else y_f2))[pi]

                    def pl1_(p, t=t, g=g, tgt=tgt, pi=pi):
                        p.wait_ge(sP.h, qpool[(t, 2 * g + 1)])
                        if pi == 3 and t >= 1:
                            p.wait_ge(sP.h, zv_done[t - 1])
                        p.tensor_add(tgt[:],
                                     q_buf[:, (2 * (g % 4)) * LCORE:(2 * (g % 4) + 1) * LCORE],
                                     q_buf[:, (2 * (g % 4) + 1) * LCORE:(2 * (g % 4) + 2) * LCORE]
                                     ).then_inc(sP.h, 1)
                    blk.gpsimd(pl1_); sP.n += 1
                    l1h[(t, half, pi)] = sP.n

                    if pi == 3:
                        _pb = sP.n

                        def p_(p, t=t, half=half, _pb=_pb):
                            p.wait_ge(sP.h, _pb)
                            dst = y_f if half == 0 else y_f2
                            p.tensor_add(trA[:], trA[:], trB[:])
                            p.tensor_add(dst[:], dst[:], trC[:]).then_inc(sP.h, 1)
                            p.wait_ge(sP.h, _pb + 1)
                            p.tensor_add(dst[:], dst[:], trA[:]).then_inc(sP.h, 1)
                        blk.gpsimd(p_); sP.n += 2
                        yred_done[(t, half)] = sP.n

            def emit_scan_group(blk, t, g):
                slot = t % 2
                gbuf = g % 2
                if True:
                    def q_(sync, t=t, g=g, _qb=sq.n,
                           _fr=[wpool.get((t, g * NPG + i - 4),
                                          wpool.get((t - 1, g * NPG + i + 12), 0))
                                for i in range(NPG)]):
                        sync.wait_ge(sq.h, _qb)
                        for i in range(NPG):
                            n = g * NPG + i
                            sync.wait_ge(sP.h, _fr[i])
                            sync.dma_start(
                                B_sb[:, (n % 4) * LCORE:(n % 4 + 1) * LCORE],
                                bass.AP(bc_sp, n * LCORE, [[0, P], [1, LCORE]]),
                            ).then_inc(sq.h, 16)
                    blk.sync(q_); sq.n += 16 * NPG
                    bld = sq.n

                    def s_(s, t=t, g=g, gbuf=gbuf, slot=slot,
                           _af=(scv.get((t, (g - 1) * NPG), 0) if g >= 2
                                else (scv.get((t - 1, (g + 6) * NPG), 0) if t >= 1 else 0))):
                        s.wait_ge(sV.h, dready_a[t])
                        s.wait_ge(sV.h, _af)
                        for i in range(NPG):
                            n = g * NPG + i
                            s.activation(
                                a_slab[:, (gbuf * NPG + i) * LCORE:(gbuf * NPG + i + 1) * LCORE],
                                delta_f[:, slot * LCORE:(slot + 1) * LCORE],
                                AF.Exp, scale=A_sb[:, N * t + n:N * t + n + 1]).then_inc(sS.h, 1)
                    blk.scalar(s_); sS.n += NPG
                    aexp = sS.n

                    def p_(v, t=t, g=g, gbuf=gbuf, slot=slot, bld=bld,
                           _wf=(scv.get((t, (g - 1) * NPG), 0) if g >= 2
                                else (scv.get((t - 1, (g + 6) * NPG), 0) if t >= 1 else 0))):
                        v.wait_ge(sq.h, bld)
                        v.wait_ge(sV.h, dready_a[t])
                        v.wait_ge(sV.h, _wf)
                        for i in range(NPG):
                            n = g * NPG + i
                            v.tensor_mul(
                                w_slab[:, (gbuf * NPG + i) * LCORE:(gbuf * NPG + i + 1) * LCORE],
                                du_f[:, slot * LCORE:(slot + 1) * LCORE],
                                B_sb[:, (n % 4) * LCORE:(n % 4 + 1) * LCORE]).then_inc(sP.h, 1)
                    blk.vector(p_)
                    for i in range(NPG):
                        sP.n += 1
                        wpool[(t, g * NPG + i)] = sP.n

                    for i in range(NPG):
                        scv[(t, g * NPG + i)] = sV.n + i + 1

                    def v_(v, t=t, g=g, gbuf=gbuf, slot=slot, aexp=aexp):
                        v.wait_ge(sS.h, aexp)
                        for i in range(NPG):
                            n = g * NPG + i
                            v.wait_ge(sP.h, wpool[(t, n)])
                            if n < 8:
                                if t >= 2:
                                    v.wait_ge(sV.h, htv[(t - 2, n + 8)])
                            else:
                                if t >= 1:
                                    v.wait_ge(sV.h, htv[(t - 1, n - 8)])
                            v.tensor_tensor_scan(
                                h_slab[:, hslot(t, n):hslot(t, n) + LCORE],
                                a_slab[:, (gbuf * NPG + i) * LCORE:(gbuf * NPG + i + 1) * LCORE],
                                w_slab[:, (gbuf * NPG + i) * LCORE:(gbuf * NPG + i + 1) * LCORE],
                                0.0, ALU.mult, ALU.add).then_inc(sV.h, 1)
                    blk.vector(v_)
                    sV.n += NPG

            def emit_state(t):
                slot = t % 2
                scans_done_v[t] = sV.n
                with nc.Block() as blk:
                    scansd = scans_done_v[t]

                    def v_(v, t=t, scansd=scansd):
                        v.wait_ge(sV.h, scansd)
                        if t >= 1:
                            v.wait_ge(gq.h, std_done[t - 1])
                        s0 = (N * t) % HS
                        run1 = min(N, HS - s0)
                        c1 = v.tensor_copy(stout[:, 0:run1],
                                           bass.AP(h_slab, s0 * LCORE + LCORE - 1,
                                                   [[HS * LCORE, P], [LCORE, run1]]))
                        if run1 < N:
                            v.tensor_copy(stout[:, run1:N],
                                          bass.AP(h_slab, LCORE - 1,
                                                  [[HS * LCORE, P], [LCORE, N - run1]])).then_inc(sV.h, 1)
                        else:
                            c1.then_inc(sV.h, 1)
                    blk.vector(v_); sV.n += 1
                    fext = sV.n

                    def s_(s, t=t, slot=slot, fext=fext):
                        s.wait_ge(sV.h, fext)
                        for n in range(N):
                            act = s.activation(stout[:, N + n:N + n + 1],
                                               srel[:, slot * LCORE + LCORE - 1:slot * LCORE + LCORE],
                                               AF.Exp, scale=A_sb[:, N * t + n:N * t + n + 1])
                            if n == N - 1:
                                act.then_inc(sS.h, 1)
                    blk.scalar(s_); sS.n += 1
                    ppd = sS.n

                    def g_(g, t=t, fext=fext, ppd=ppd, _gb=gq.n):
                        g.wait_ge(gq.h, _gb)
                        g.wait_ge(sV.h, fext)
                        g.wait_ge(sS.h, ppd)
                        g.dma_start(ag_in[t].ap(), stout[:]).then_inc(gq.h, 16)
                    blk.gpsimd(g_); gq.n += 16
                    std_done[t] = gq.n

                    cc_step = 16 if n_cores == 1 else 1

                    def g2_(g, t=t, cc_step=cc_step, _cb=sC.n):
                        g.wait_ge(sC.h, _cb)
                        g.wait_ge(gq.h, std_done[t])
                        if n_cores == 1:
                            g.dma_start(ag_out[t].ap()[0:P, :], ag_in[t].ap()).then_inc(sC.h, cc_step)
                        else:
                            g.collective_compute(
                                "AllGather", ALU.bypass,
                                ins=[ag_in[t].ap()], outs=[ag_out[t].ap()],
                                replica_groups=[list(range(n_cores))],
                            ).then_inc(sC.h, cc_step)
                    blk.gpsimd(g2_); sC.n += cc_step
                    ag_done[t] = sC.n

            def emit_ztail(t):
                slot = t % 2
                with nc.Block() as blk:
                    def g_(g, t=t, _gb=gq.n):
                        g.wait_ge(gq.h, _gb)
                        if t >= 1:
                            g.wait_ge(sP.h, zv_done[t - 1])
                        g.dma_start(ssk_tile[:], ssk_sp.ap()[P * t:P * (t + 1), :]).then_inc(gq.h, 16)
                    blk.gpsimd(g_); gq.n += 16
                    sskl = gq.n
                    sskl_done[t] = sskl
                    _pb2 = sP.n

                    def p_(p, t=t, _pb2=_pb2):
                        p.wait_ge(sP.h, yred_done[(t, 1)])
                        p.tensor_add(y_f[:], y_f[:], y_f2[:]).then_inc(sP.h, 1)
                    blk.gpsimd(p_); sP.n += 1
                    yadd = sP.n

                    def v_(v, t=t, slot=slot, sskl=sskl, yadd=yadd):
                        v.wait_ge(sP.h, yadd)
                        vf(v, v.scalar_tensor_tensor(zt_bf[:],
                                                     uc_ring[:, slot * LCORE:(slot + 1) * LCORE],
                                                     wd_sb[:, t:t + 1], y_f[:],
                                                     ALU.mult, ALU.add))
                        v.wait_ge(gq.h, sskl)
                        v.tensor_mul(z_bf[:], zt_bf[:], ssk_tile[:]).then_inc(sP.h, 1)
                    blk.vector(v_); sP.n += 1
                    zv_done[t] = sP.n

                    def g2_(g, t=t, _gb=gq.n):
                        g.wait_ge(gq.h, _gb)
                        g.wait_ge(sP.h, zv_done[t])
                        g.dma_start(z_sp.ap()[P * t:P * (t + 1), :], z_bf[:]).then_inc(gq.h, 16)
                    blk.gpsimd(g2_); gq.n += 16

            for t in range(TE):
                emit_delta(t)
                with nc.Block() as gblk:
                    emit_scan_group(gblk, t, 0)
                    emit_scan_group(gblk, t, 1)
                    if t >= 1:
                        emit_combine(gblk, t - 1)
                    for g in range(NGR):
                        if t >= 1:
                            emit_corr_group(gblk, t - 1, g)
                        if g + 2 < NGR:
                            emit_scan_group(gblk, t, g + 2)
                emit_state(t)
                if t >= 1:
                    emit_ztail(t - 1)
            with nc.Block() as gblk:
                emit_combine(gblk, TE - 1)
                for g in range(NGR):
                    emit_corr_group(gblk, TE - 1, g)
            emit_ztail(TE - 1)

        barrier()
        sfree("ssk_tile", "uc_ring", "dpre_b")

        o1b = es.enter_context(nc.sbuf_tensor("o1b", [P, TD * LCORE], BF16))
        es2 = ExitStack()
        z_all = es2.enter_context(nc.sbuf_tensor("z_all", [P, TE * LCORE], BF16))
        wostg = es2.enter_context(nc.sbuf_tensor("wostg", [P, E], F32))
        wostgb = es2.enter_context(nc.sbuf_tensor("wostgb", [P, 2 * E], BF16))
        mix_bf = es2.enter_context(nc.sbuf_tensor("mix_bf", [P, TD * LCORE], BF16))
        mix2 = es2.enter_context(nc.sbuf_tensor("mix2", [P, LCORE], BF16))
        xT_f = es2.enter_context(nc.sbuf_tensor("xT_f", [P, LCORE], F32))
        lrow = es2.enter_context(nc.sbuf_tensor("lrow", [1, 4 * LCORE], F32))
        r2b = es2.enter_context(nc.sbuf_tensor("r2b", [P, LCORE], BF16))
        mr2b = es2.enter_context(nc.sbuf_tensor("mr2b", [P, LCORE], BF16))
        t1b = es2.enter_context(nc.sbuf_tensor("t1b", [P, LCORE], BF16))
        o1f = es2.enter_context(nc.sbuf_tensor("o1f", [P, LCORE], F32))
        o1sq = es2.enter_context(nc.sbuf_tensor("o1sq", [P, LCORE], BF16))
        psMW = es2.enter_context(nc.psum_tensor("psMW", [P, 1024], F32))
        psM = es2.enter_context(nc.psum_tensor("psM", [P, 512], F32))
        psM2 = es2.enter_context(nc.psum_tensor("psM2", [P, 512], F32))
        psS1 = es2.enter_context(nc.psum_tensor("psS1", [1, LCORE], F32))
        psS2 = es2.enter_context(nc.psum_tensor("psS2", [1, LCORE], F32))
        with es2:
            with nc.Block() as blk:
                def g_(g, _gb=gq.n):
                    g.wait_ge(gq.h, _gb)
                    for t in range(TE):
                        g.dma_start(z_all[:, t * LCORE:(t + 1) * LCORE],
                                    z_sp.ap()[P * t:P * (t + 1), :]).then_inc(gq.h, 16)
                blk.gpsimd(g_); gq.n += 16 * TE
                zld = gq.n

            stats_prev = sT.n
            t0base = sT.n
            mo_arr = {}
            mev_arr = {}
            for dt in range(TD):
                buf = dt % 2
                with nc.Block() as blk:
                    tprev = mo_arr.get(dt - 2, t0base)

                    wc_prev = sV.n

                    def q_(sync, dt=dt, buf=buf, tprev=tprev, wc_prev=wc_prev, _qb=sq.n):
                        sync.wait_ge(sq.h, _qb)
                        sync.wait_ge(sT.h, tprev)
                        if dt >= 1:
                            sync.wait_ge(sV.h, wc_prev)
                        for pc in range(2):
                            sync.dma_start(
                                wostg[:, pc * 1024:(pc + 1) * 1024].rearrange(
                                    "p (a c) -> p a c", c=P),
                                WoutT[1024 * pc:1024 * (pc + 1),
                                      P * dt:P * (dt + 1)].rearrange("(a p) c -> p a c", p=P),
                            ).then_inc(sq.h, 16)
                    blk.sync(q_); sq.n += 32
                    ldo = sq.n

                    def qg_(g, dt=dt, buf=buf, tprev=tprev, wc_prev=wc_prev, _gb=gq.n):
                        g.wait_ge(gq.h, _gb)
                        g.wait_ge(sT.h, tprev)
                        if dt >= 1:
                            g.wait_ge(sV.h, wc_prev)
                        for pc in range(2, 4):
                            g.dma_start(
                                wostg[:, pc * 1024:(pc + 1) * 1024].rearrange(
                                    "p (a c) -> p a c", c=P),
                                WoutT[1024 * pc:1024 * (pc + 1),
                                      P * dt:P * (dt + 1)].rearrange("(a p) c -> p a c", p=P),
                            ).then_inc(gq.h, 16)
                    blk.gpsimd(qg_); gq.n += 32
                    ldo_g = gq.n

                    def v0_(v, buf=buf, ldo=ldo, ldo_g=ldo_g):
                        v.wait_ge(sq.h, ldo)
                        v.wait_ge(gq.h, ldo_g)
                        for hh in range(2):
                            v.tensor_copy(wostgb[:, buf * E + hh * (E // 2):
                                                 buf * E + (hh + 1) * (E // 2)],
                                          wostg[:, hh * (E // 2):(hh + 1) * (E // 2)]
                                          ).then_inc(sV.h, 1)
                    blk.vector(v0_); sV.n += 2
                    woc = sV.n

                    def t_(t_eng, dt=dt, buf=buf, woc=woc,
                           _pm=mev_arr.get(dt - 1, 0)):
                        t_eng.wait_ge(sV.h, woc)
                        if dt >= 1:
                            t_eng.wait_ge(sS.h, _pm)
                        if dt == 0:
                            t_eng.wait_ge(gq.h, zld)
                        for et in range(TE):
                            for lc in range(2):
                                mm = t_eng.matmul(
                                    psMW[:, 512 * lc:512 * (lc + 1)],
                                    wostgb[:, buf * E + P * et:buf * E + P * (et + 1)],
                                    z_all[:, et * LCORE + 512 * lc:et * LCORE + 512 * (lc + 1)],
                                    start=(et == 0), stop=(et == TE - 1),
                                    skip_group_check=True)
                                if et == TE - 1 and lc == 1:
                                    mm.then_inc(sT.h, 1)
                    blk.tensor(t_); sT.n += 1
                    momm = sT.n
                    mo_arr[dt] = momm

                    def s2_(s, dt=dt, momm=momm, stats_prev=stats_prev):
                        s.wait_ge(sT.h, stats_prev)
                        s.wait_ge(sT.h, momm)
                        for lc in range(2):
                            s.copy(mix_bf[:, dt * LCORE + 512 * lc:dt * LCORE + 512 * (lc + 1)],
                                   psMW[:, 512 * lc:512 * (lc + 1)])
                            act = s.activation(mix2[:, 512 * lc:512 * (lc + 1)],
                                               psMW[:, 512 * lc:512 * (lc + 1)], AF.Square)
                            if lc == 1:
                                act.then_inc(sS.h, 1)
                    blk.scalar(s2_); sS.n += 1
                    mev = sS.n
                    mev_arr[dt] = mev

                    def t2_(t_eng, dt=dt, mev=mev):
                        t_eng.wait_ge(sS.h, mev)
                        for lc in range(2):
                            t_eng.matmul(psS1[:, 512 * lc:512 * (lc + 1)], onescol_b[:],
                                         mix_bf[:, dt * LCORE + 512 * lc:dt * LCORE + 512 * (lc + 1)],
                                         start=(dt == 0), stop=(dt == TD - 1), skip_group_check=True)
                            mm = t_eng.matmul(psS2[:, 512 * lc:512 * (lc + 1)], onescol_b[:],
                                              mix2[:, 512 * lc:512 * (lc + 1)],
                                              start=(dt == 0), stop=(dt == TD - 1), skip_group_check=True)
                            if lc == 1:
                                mm.then_inc(sT.h, 1)
                    blk.tensor(t2_); sT.n += 1
                    stats_prev = sT.n

            with nc.Block() as blk:
                smm = sT.n

                def v_(v, smm=smm):
                    v.wait_ge(sT.h, smm)
                    vf(v, v.tensor_scalar_mul(lrow[0:1, 0:LCORE], psS1[:], 1.0 / D))
                    vf(v, v.tensor_scalar_mul(lrow[0:1, LCORE:2 * LCORE], psS2[:], 1.0 / D))
                    vf(v, v.tensor_mul(lrow[0:1, 2 * LCORE:3 * LCORE], lrow[0:1, 0:LCORE],
                                       lrow[0:1, 0:LCORE]))
                    vf(v, v.tensor_sub(lrow[0:1, LCORE:2 * LCORE], lrow[0:1, LCORE:2 * LCORE],
                                       lrow[0:1, 2 * LCORE:3 * LCORE]))
                    v.tensor_scalar_add(lrow[0:1, LCORE:2 * LCORE],
                                        lrow[0:1, LCORE:2 * LCORE], EPS).then_inc(sV.h, 1)
                blk.vector(v_); sV.n += 1
                vst = sV.n

                def s_(s, vst=vst):
                    s.wait_ge(sV.h, vst)
                    sf(s, s.activation(lrow[0:1, 2 * LCORE:3 * LCORE], lrow[0:1, LCORE:2 * LCORE],
                                       AF.Ln))
                    s.activation(lrow[0:1, 2 * LCORE:3 * LCORE], lrow[0:1, 2 * LCORE:3 * LCORE],
                                 AF.Exp, scale=-0.5).then_inc(sS.h, 1)
                blk.scalar(s_); sS.n += 1
                srr = sS.n

                def v2_(v, srr=srr):
                    v.wait_ge(sS.h, srr)
                    v.tensor_mul(lrow[0:1, 3 * LCORE:4 * LCORE], lrow[0:1, 0:LCORE],
                                 lrow[0:1, 2 * LCORE:3 * LCORE]).then_inc(sV.h, 1)
                blk.vector(v2_); sV.n += 1
                vmr = sV.n

                def t_(t_eng, vmr=vmr):
                    t_eng.wait_ge(sV.h, vmr)
                    for lc in range(2):
                        psr = psM[:] if lc == 0 else psMW[:, 0:512]
                        psm_ = psM2[:] if lc == 0 else psMW[:, 512:1024]
                        t_eng.matmul(psr, ones_f[0:1, :],
                                     lrow[0:1, 2 * LCORE + 512 * lc:2 * LCORE + 512 * (lc + 1)],
                                     start=True, stop=True)
                        t_eng.matmul(psm_, ones_f[0:1, :],
                                     lrow[0:1, 3 * LCORE + 512 * lc:3 * LCORE + 512 * (lc + 1)],
                                     start=True, stop=True).then_inc(sT.h, 1)
                blk.tensor(t_); sT.n += 2
                bmm2 = sT.n

                def s2_(s, bmm2=bmm2):
                    s.wait_ge(sT.h, bmm2)
                    s.copy(r2b[:, 0:512], psM[:])
                    s.copy(mr2b[:, 0:512], psM2[:])
                    s.copy(r2b[:, 512:1024], psMW[:, 0:512])
                    s.copy(mr2b[:, 512:1024], psMW[:, 512:1024]).then_inc(sS.h, 1)
                blk.scalar(s2_); sS.n += 1
                bcd = sS.n

            o1v_done = [0] * TD
            for dt in range(TD):
                with nc.Block() as blk:
                    def g_(g, dt=dt, _gb=gq.n):
                        g.wait_ge(gq.h, _gb)
                        if dt >= 1:
                            g.wait_ge(sV.h, o1v_done[dt - 1])
                        g.dma_start(xT_f[:], xt_sp.ap()[P * dt:P * (dt + 1), :]).then_inc(gq.h, 16)
                    blk.gpsimd(g_); gq.n += 16
                    xl = gq.n

                    def v_(v, dt=dt, xl=xl, bcd=bcd):
                        v.wait_ge(sS.h, bcd)
                        v.wait_ge(gq.h, xl)
                        vf(v, v.tensor_mul(t1b[:], mix_bf[:, dt * LCORE:(dt + 1) * LCORE], r2b[:]))
                        vf(v, v.tensor_sub(o1f[:], t1b[:], mr2b[:]))
                        vf(v, v.tensor_add(o1f[:], o1f[:], xT_f[:]))
                        v.tensor_copy(o1b[:, dt * LCORE:(dt + 1) * LCORE], o1f[:]).then_inc(sV.h, 1)
                    blk.vector(v_); sV.n += 1
                    o1v_done[dt] = sV.n

                    def g2_(g, dt=dt, _gb=gq.n):
                        g.wait_ge(gq.h, _gb)
                        g.wait_ge(sV.h, o1v_done[dt])
                        g.dma_start(o1_sp.ap()[P * dt:P * (dt + 1), :], o1f[:]).then_inc(gq.h, 16)
                    blk.gpsimd(g2_); gq.n += 16

            st2_arr = [0] * TD
            for dt in range(TD):
                with nc.Block() as blk:
                    def s_(s, dt=dt):
                        s.wait_ge(sV.h, o1v_done[dt])
                        if dt >= 1:
                            s.wait_ge(sT.h, st2_arr[dt - 1])
                        for lc in range(2):
                            act = s.activation(
                                o1sq[:, 512 * lc:512 * (lc + 1)],
                                o1b[:, dt * LCORE + 512 * lc:dt * LCORE + 512 * (lc + 1)],
                                AF.Square)
                            if lc == 1:
                                act.then_inc(sS.h, 1)
                    blk.scalar(s_); sS.n += 1
                    sq2d = sS.n

                    def t_(t_eng, dt=dt, sq2d=sq2d):
                        t_eng.wait_ge(sS.h, sq2d)
                        for lc in range(2):
                            t_eng.matmul(psS1[:, 512 * lc:512 * (lc + 1)], onescol_b[:],
                                         o1b[:, dt * LCORE + 512 * lc:dt * LCORE + 512 * (lc + 1)],
                                         start=(dt == 0), stop=(dt == TD - 1),
                                         skip_group_check=True)
                            mm = t_eng.matmul(psS2[:, 512 * lc:512 * (lc + 1)], onescol_b[:],
                                              o1sq[:, 512 * lc:512 * (lc + 1)],
                                              start=(dt == 0), stop=(dt == TD - 1),
                                              skip_group_check=True)
                            if lc == 1:
                                mm.then_inc(sT.h, 1)
                    blk.tensor(t_); sT.n += 1
                    st2_arr[dt] = sT.n

            with nc.Block() as blk:
                smm3 = sT.n

                def v_(v, smm3=smm3):
                    v.wait_ge(sT.h, smm3)
                    vf(v, v.tensor_scalar_mul(lrow[0:1, 0:LCORE], psS1[:], 1.0 / D))
                    vf(v, v.tensor_scalar_mul(lrow[0:1, LCORE:2 * LCORE], psS2[:], 1.0 / D))
                    vf(v, v.tensor_mul(lrow[0:1, 2 * LCORE:3 * LCORE], lrow[0:1, 0:LCORE],
                                       lrow[0:1, 0:LCORE]))
                    vf(v, v.tensor_sub(lrow[0:1, LCORE:2 * LCORE], lrow[0:1, LCORE:2 * LCORE],
                                       lrow[0:1, 2 * LCORE:3 * LCORE]))
                    v.tensor_scalar_add(lrow[0:1, LCORE:2 * LCORE],
                                        lrow[0:1, LCORE:2 * LCORE], EPS).then_inc(sV.h, 1)
                blk.vector(v_); sV.n += 1
                vst3 = sV.n

                def s_(s, vst3=vst3):
                    s.wait_ge(sV.h, vst3)
                    sf(s, s.activation(lrow[0:1, 2 * LCORE:3 * LCORE], lrow[0:1, LCORE:2 * LCORE],
                                       AF.Ln))
                    s.activation(lrow[0:1, 2 * LCORE:3 * LCORE], lrow[0:1, 2 * LCORE:3 * LCORE],
                                 AF.Exp, scale=-0.5).then_inc(sS.h, 1)
                blk.scalar(s_); sS.n += 1
                srr3 = sS.n

                def v2_(v, srr3=srr3):
                    v.wait_ge(sS.h, srr3)
                    v.tensor_mul(lrow[0:1, 3 * LCORE:4 * LCORE], lrow[0:1, 0:LCORE],
                                 lrow[0:1, 2 * LCORE:3 * LCORE]).then_inc(sV.h, 1)
                blk.vector(v2_); sV.n += 1
                vmr3 = sV.n

                def t_(t_eng, vmr3=vmr3):
                    t_eng.wait_ge(sV.h, vmr3)
                    for lc in range(2):
                        psr = psM[:] if lc == 0 else psMW[:, 0:512]
                        psm_ = psM2[:] if lc == 0 else psMW[:, 512:1024]
                        t_eng.matmul(psr, ones_f[0:1, :],
                                     lrow[0:1, 2 * LCORE + 512 * lc:2 * LCORE + 512 * (lc + 1)],
                                     start=True, stop=True)
                        t_eng.matmul(psm_, ones_f[0:1, :],
                                     lrow[0:1, 3 * LCORE + 512 * lc:3 * LCORE + 512 * (lc + 1)],
                                     start=True, stop=True).then_inc(sT.h, 1)
                blk.tensor(t_); sT.n += 2
                bmm4 = sT.n

                def s2_(s, bmm4=bmm4):
                    s.wait_ge(sT.h, bmm4)
                    s.copy(r2b[:, 0:512], psM[:])
                    s.copy(mr2b[:, 0:512], psM2[:])
                    s.copy(r2b[:, 512:1024], psMW[:, 0:512])
                    s.copy(mr2b[:, 512:1024], psMW[:, 512:1024]).then_inc(sS.h, 1)
                blk.scalar(s2_); sS.n += 1
                bcd3 = sS.n

            for dt in range(TD):
                with nc.Block() as blk:
                    vprev3 = sV.n

                    def v_(v, dt=dt, bcd3=bcd3, vprev3=vprev3, smm3=st2_arr[TD - 1]):
                        v.wait_ge(sS.h, bcd3)
                        v.wait_ge(sT.h, smm3)
                        v.wait_ge(sV.h, vprev3)
                        vf(v, v.tensor_mul(t1b[:], o1b[:, dt * LCORE:(dt + 1) * LCORE], r2b[:]))
                        v.tensor_sub(o1b[:, dt * LCORE:(dt + 1) * LCORE], t1b[:],
                                     mr2b[:]).then_inc(sV.h, 1)
                    blk.vector(v_); sV.n += 1

        barrier()

        es2 = ExitStack()
        gustg = es2.enter_context(nc.sbuf_tensor("gustg", [P, 4 * 1024], F32))
        dstg = es2.enter_context(nc.sbuf_tensor("dstg", [P, E], F32))
        dstgb = es2.enter_context(nc.sbuf_tensor("dstgb", [P, 2 * INTER], BF16))
        sg_f = es2.enter_context(nc.sbuf_tensor("sg_f", [P, 1024], F32))
        hffn = es2.enter_context(nc.sbuf_tensor("hffn", [P, 64 * 512], BF16))
        ffn_bf = es2.enter_context(nc.sbuf_tensor("ffn_bf", [P, TD * 512], BF16))
        ffn2 = es2.enter_context(nc.sbuf_tensor("ffn2", [P, 512], BF16))
        lrow2 = es2.enter_context(nc.sbuf_tensor("lrow2", [1, 4 * 512], F32))
        r3b = es2.enter_context(nc.sbuf_tensor("r3b", [P, 512], BF16))
        mr3b = es2.enter_context(nc.sbuf_tensor("mr3b", [P, 512], BF16))
        t1c = es2.enter_context(nc.sbuf_tensor("t1c", [P, 512], BF16))
        o1l = es2.enter_context(nc.sbuf_tensor("o1l", [P, 512], F32))
        o2f = es2.enter_context(nc.sbuf_tensor("o2f", [P, 512], F32))
        ostg = es2.enter_context(nc.sbuf_tensor("ostg", [P, 2 * P], F32))
        psG = es2.enter_context(nc.psum_tensor("psG", [P, 512], F32))
        psUp = es2.enter_context(nc.psum_tensor("psUp", [P, 512], F32))
        psDn = es2.enter_context(nc.psum_tensor("psDn", [P, 512], F32))
        psDn2 = es2.enter_context(nc.psum_tensor("psDn2", [P, 512], F32))
        psT1 = es2.enter_context(nc.psum_tensor("psT1", [1, 512], F32))
        psT2 = es2.enter_context(nc.psum_tensor("psT2", [1, 512], F32))
        psO = es2.enter_context(nc.psum_tensor("psO", [P, P], F32))
        with es2:
            for lh in range(2):
                l0 = 512 * lh
                ggmm = {}
                gmul = {}
                gvc = {}
                tbase = sT.n
                vbase = sV.n
                sbase = sS.n
                _bc = None
                for gi in range(64):
                    buf = gi % 2
                    pg = psG if gi % 2 == 0 else psDn
                    pu = psUp if gi % 2 == 0 else psDn2
                    if gi % 8 == 0:
                        if _bc is not None:
                            _bc.__exit__(None, None, None)
                        _bc = nc.Block()
                        blk = _bc.__enter__()
                    if True:
                        tfree = ggmm.get(gi - 2, tbase)

                        for bb in range(2):
                            def q_(eng, gi=gi, bb=bb, tfree=tfree,
                                   _qh=(sq.h if bb == 0 else gq.h),
                                   _qb=(sq.n if bb == 0 else gq.n),
                                   _vc=gvc.get((gi - 1, bb), vbase)):
                                eng.wait_ge(_qh, _qb)
                                eng.wait_ge(sT.h, tfree)
                                eng.wait_ge(sV.h, _vc)
                                for p2 in range(2):
                                    part = bb * 2 + p2
                                    col0 = P * gi + INTER * (part // 2)
                                    hh = part % 2
                                    for pc in range(2):
                                        eng.dma_start(
                                            gustg[:, (bb * 2 + p2) * 1024 + pc * 512:
                                                  (bb * 2 + p2) * 1024 + (pc + 1) * 512]
                                            .rearrange("p (a c) -> p a c", c=P),
                                            guT[1024 * hh + 512 * pc:1024 * hh + 512 * (pc + 1),
                                                col0:col0 + P].rearrange("(a p) c -> p a c", p=P),
                                        ).then_inc(_qh, 16)
                            if bb == 0:
                                blk.sync(q_); sq.n += 64
                                ldg = ('q', sq.n)
                            else:
                                blk.gpsimd(q_); gq.n += 64
                                ldg = ('g', gq.n)

                            def v0_(v, buf=buf, bb=bb, ldg=ldg):
                                v.wait_ge(sq.h if ldg[0] == 'q' else gq.h, ldg[1])
                                for p2 in range(2):
                                    part = bb * 2 + p2
                                    gu = part // 2
                                    hh = part % 2
                                    goff = 0 if gu == 0 else 2 * TD * P
                                    v.tensor_copy(dstgb[:, goff + buf * TD * P + hh * 1024:
                                                        goff + buf * TD * P + (hh + 1) * 1024],
                                                  gustg[:, (bb * 2 + p2) * 1024:
                                                        (bb * 2 + p2 + 1) * 1024]
                                                  ).then_inc(sV.h, 1)
                            blk.vector(v0_); sV.n += 2
                            gvc[(gi, bb)] = sV.n
                        gc = sV.n

                        def t_(t_eng, gi=gi, buf=buf, l0=l0, gc=gc, pg=pg, pu=pu,
                               _w2=gmul.get(gi - 2, vbase), _sb=sbase):
                            t_eng.wait_ge(sV.h, gc)
                            t_eng.wait_ge(sV.h, _w2)
                            if gi < 2:
                                t_eng.wait_ge(sS.h, _sb)
                            for dt in range(TD):
                                mm = t_eng.matmul(
                                    pg[:],
                                    dstgb[:, buf * TD * P + dt * P:buf * TD * P + (dt + 1) * P],
                                    o1b[:, dt * LCORE + l0:dt * LCORE + l0 + 512],
                                    start=(dt == 0), stop=(dt == TD - 1))
                                if dt == TD - 1:
                                    mm.then_inc(sT.h, 1)
                            for dt in range(TD):
                                mm = t_eng.matmul(
                                    pu[:],
                                    dstgb[:, 2 * TD * P + buf * TD * P + dt * P:
                                          2 * TD * P + buf * TD * P + (dt + 1) * P],
                                    o1b[:, dt * LCORE + l0:dt * LCORE + l0 + 512],
                                    start=(dt == 0), stop=(dt == TD - 1))
                                if dt == TD - 1:
                                    mm.then_inc(sT.h, 1)
                        blk.tensor(t_); sT.n += 2
                        gmm = sT.n
                        ggmm[gi] = gmm

                        def s2_(s, gi=gi, gmm=gmm, pg=pg):
                            s.wait_ge(sT.h, gmm - 1)
                            s.activation(sg_f[:, (gi % 2) * 512:(gi % 2 + 1) * 512],
                                         pg[:], AF.Silu).then_inc(sS.h, 1)
                        blk.scalar(s2_); sS.n += 1
                        sgd = sS.n

                        def v_(v, gi=gi, gmm=gmm, sgd=sgd, pu=pu):
                            v.wait_ge(sT.h, gmm)
                            v.wait_ge(sS.h, sgd)
                            v.tensor_mul(hffn[:, gi * 512:(gi + 1) * 512],
                                         sg_f[:, (gi % 2) * 512:(gi % 2 + 1) * 512],
                                         pu[:]).then_inc(sV.h, 1)
                        blk.vector(v_); sV.n += 1
                        gmul[gi] = sV.n

                _bc.__exit__(None, None, None)
                _bc = None
                hffn_done = sV.n
                for dt in range(TD):
                    buf = dt % 2
                    if dt % 8 == 0:
                        if _bc is not None:
                            _bc.__exit__(None, None, None)
                        _bc = nc.Block()
                        blk = _bc.__enter__()
                    if True:
                        if dt == 0:
                            tbase2 = sT.n
                            sbase2 = sS.n
                            dmm_arr = {}
                            sevac = {}
                            dvc = {}
                        tfree2 = dmm_arr.get(dt - 2, tbase2)

                        for bb in range(2):
                            stg = dstg if bb == 0 else gustg

                            def q_(eng, dt=dt, bb=bb, tfree2=tfree2, stg=stg,
                                   _qh=(sq.h if bb == 0 else gq.h),
                                   _qb=(sq.n if bb == 0 else gq.n),
                                   _vc=dvc.get((dt - 1, bb), 0) if dt >= 1 else None):
                                eng.wait_ge(_qh, _qb)
                                eng.wait_ge(sT.h, tfree2)
                                if _vc is not None:
                                    eng.wait_ge(sV.h, _vc)
                                for hh2 in range(2):
                                    hh = bb * 2 + hh2
                                    for pc in range(2):
                                        eng.dma_start(
                                            stg[:, hh2 * 2048 + pc * 1024:
                                                hh2 * 2048 + (pc + 1) * 1024].rearrange(
                                                "p (a c) -> p a c", c=P),
                                            dwT[2048 * hh + 1024 * pc:2048 * hh + 1024 * (pc + 1),
                                                P * dt:P * (dt + 1)].rearrange("(a p) c -> p a c", p=P),
                                        ).then_inc(_qh, 16)
                            if bb == 0:
                                blk.sync(q_); sq.n += 64
                                ldd2 = ('q', sq.n)
                            else:
                                blk.gpsimd(q_); gq.n += 64
                                ldd2 = ('g', gq.n)

                            def v0_(v, buf=buf, bb=bb, ldd2=ldd2, stg=stg):
                                v.wait_ge(sq.h if ldd2[0] == 'q' else gq.h, ldd2[1])
                                for hh2 in range(2):
                                    hh = bb * 2 + hh2
                                    v.tensor_copy(dstgb[:, buf * INTER + hh * 2048:
                                                        buf * INTER + (hh + 1) * 2048],
                                                  stg[:, hh2 * 2048:(hh2 + 1) * 2048]
                                                  ).then_inc(sV.h, 1)
                            blk.vector(v0_); sV.n += 2
                            dvc[(dt, bb)] = sV.n
                        dc = sV.n

                        def t_(t_eng, dt=dt, buf=buf, dc=dc, hffn_done=hffn_done,
                               _pf=None, _sb2=sbase2):
                            t_eng.wait_ge(sV.h, dc)
                            t_eng.wait_ge(sV.h, hffn_done)
                            if dt < 2:
                                t_eng.wait_ge(sS.h, _sb2)
                            else:
                                t_eng.wait_ge(sS.h, sevac[dt - 2])
                            ps = psDn if dt % 2 == 0 else psDn2
                            for ii in range(64):
                                mm = t_eng.matmul(ps[:],
                                                  dstgb[:, buf * INTER + P * ii:
                                                        buf * INTER + P * (ii + 1)],
                                                  hffn[:, ii * 512:(ii + 1) * 512],
                                                  start=(ii == 0), stop=(ii == 63))
                                if ii == 63:
                                    mm.then_inc(sT.h, 1)
                        blk.tensor(t_); sT.n += 1
                        dmm2 = sT.n
                        dmm_arr[dt] = dmm2

                        def s2_(s, dt=dt, dmm2=dmm2):
                            ps = psDn if dt % 2 == 0 else psDn2
                            s.wait_ge(sT.h, dmm2)
                            if dt >= 1:
                                s.wait_ge(sT.h, sT.n)
                            s.copy(ffn_bf[:, dt * 512:(dt + 1) * 512], ps[:])
                            s.activation(ffn2[:], ps[:], AF.Square).then_inc(sS.h, 1)
                        blk.scalar(s2_); sS.n += 1
                        fev = sS.n
                        sevac[dt] = fev

                        def t2_(t_eng, dt=dt, fev=fev):
                            t_eng.wait_ge(sS.h, fev)
                            t_eng.matmul(psT1[:], onescol_b[:], ffn_bf[:, dt * 512:(dt + 1) * 512],
                                         start=(dt == 0), stop=(dt == TD - 1), skip_group_check=True)
                            t_eng.matmul(psT2[:], onescol_b[:], ffn2[:],
                                         start=(dt == 0), stop=(dt == TD - 1),
                                         skip_group_check=True).then_inc(sT.h, 1)
                        blk.tensor(t2_); sT.n += 1

                _bc.__exit__(None, None, None)
                _bc = None
                with nc.Block() as blk:
                    smm2 = sT.n

                    def v_(v, smm2=smm2):
                        v.wait_ge(sT.h, smm2)
                        vf(v, v.tensor_scalar_mul(lrow2[0:1, 0:512], psT1[:], 1.0 / D))
                        vf(v, v.tensor_scalar_mul(lrow2[0:1, 512:1024], psT2[:], 1.0 / D))
                        vf(v, v.tensor_mul(lrow2[0:1, 1024:1536], lrow2[0:1, 0:512],
                                           lrow2[0:1, 0:512]))
                        vf(v, v.tensor_sub(lrow2[0:1, 512:1024], lrow2[0:1, 512:1024],
                                           lrow2[0:1, 1024:1536]))
                        v.tensor_scalar_add(lrow2[0:1, 512:1024], lrow2[0:1, 512:1024],
                                            EPS).then_inc(sV.h, 1)
                    blk.vector(v_); sV.n += 1
                    vst2 = sV.n

                    def s_(s, vst2=vst2):
                        s.wait_ge(sV.h, vst2)
                        sf(s, s.activation(lrow2[0:1, 1024:1536], lrow2[0:1, 512:1024], AF.Ln))
                        s.activation(lrow2[0:1, 1024:1536], lrow2[0:1, 1024:1536],
                                     AF.Exp, scale=-0.5).then_inc(sS.h, 1)
                    blk.scalar(s_); sS.n += 1
                    sr2 = sS.n

                    def v2_(v, sr2=sr2):
                        v.wait_ge(sS.h, sr2)
                        v.tensor_mul(lrow2[0:1, 1536:2048], lrow2[0:1, 0:512],
                                     lrow2[0:1, 1024:1536]).then_inc(sV.h, 1)
                    blk.vector(v2_); sV.n += 1
                    vmr2 = sV.n

                    def t_(t_eng, vmr2=vmr2):
                        t_eng.wait_ge(sV.h, vmr2)
                        t_eng.matmul(psDn[:], ones_f[0:1, :], lrow2[0:1, 1024:1536],
                                     start=True, stop=True)
                        t_eng.matmul(psDn2[:], ones_f[0:1, :], lrow2[0:1, 1536:2048],
                                     start=True, stop=True).then_inc(sT.h, 1)
                    blk.tensor(t_); sT.n += 1
                    bmm3 = sT.n

                    def s2_(s, bmm3=bmm3):
                        s.wait_ge(sT.h, bmm3)
                        s.copy(r3b[:], psDn[:])
                        s.copy(mr3b[:], psDn2[:]).then_inc(sS.h, 1)
                    blk.scalar(s2_); sS.n += 1
                    bcd2 = sS.n

                for dt in range(TD):
                    if dt % 8 == 0:
                        if _bc is not None:
                            _bc.__exit__(None, None, None)
                        _bc = nc.Block()
                        blk = _bc.__enter__()
                    if True:
                        o2prev = sT.n

                        def g_(g, dt=dt, lh=lh, o2prev=o2prev, _gb=gq.n):
                            g.wait_ge(gq.h, _gb)
                            g.wait_ge(sT.h, o2prev)
                            g.dma_start(o1l[:], o1_sp.ap()[P * dt:P * (dt + 1),
                                                           512 * lh:512 * (lh + 1)]).then_inc(gq.h, 16)
                        blk.gpsimd(g_); gq.n += 16
                        o1ld = gq.n

                        def v_(v, dt=dt, o1ld=o1ld, bcd2=bcd2, o2prev=o2prev):
                            v.wait_ge(sS.h, bcd2)
                            v.wait_ge(sT.h, o2prev)
                            vf(v, v.tensor_mul(t1c[:], ffn_bf[:, dt * 512:(dt + 1) * 512], r3b[:]))
                            vf(v, v.tensor_sub(o2f[:], t1c[:], mr3b[:]))
                            v.wait_ge(gq.h, o1ld)
                            vf(v, v.tensor_add(o2f[:], o2f[:], o1l[:]))
                            v.nop().then_inc(sV.h, 1)
                        blk.vector(v_); sV.n += 1
                        o2d = sV.n

                        s_out_base = sS.n
                        g_out_base = gq.n

                        def t_(t_eng, dt=dt, o2d=o2d, s_out_base=s_out_base):
                            t_eng.wait_ge(sV.h, o2d)
                            for lb in range(4):
                                if lb >= 1:
                                    t_eng.wait_ge(sS.h, s_out_base + lb)
                                t_eng.transpose(psO[:], o2f[:, P * lb:P * (lb + 1)],
                                                ident[:]).then_inc(sT.h, 1)
                        blk.tensor(t_); sT.n += 4
                        otr = sT.n

                        def s_(s, otr=otr, g_out_base=g_out_base):
                            s.wait_ge(gq.h, g_out_base)
                            for lb in range(2):
                                s.wait_ge(sT.h, otr - 4 + lb + 1)
                                s.copy(ostg[:, lb * P:(lb + 1) * P], psO[:]).then_inc(sS.h, 1)
                        blk.scalar(s_); sS.n += 2
                        oevA = sS.n

                        def g2a_(g, dt=dt, lh=lh, oevA=oevA, _gb=gq.n):
                            g.wait_ge(gq.h, _gb)
                            g.wait_ge(sS.h, oevA)
                            for lb in range(2):
                                g.dma_start(out[512 * lh + P * lb:512 * lh + P * (lb + 1),
                                                P * dt:P * (dt + 1)],
                                            ostg[:, lb * P:(lb + 1) * P]).then_inc(gq.h, 16)
                        blk.gpsimd(g2a_); gq.n += 32
                        bA = gq.n

                        def s_b(s, otr=otr, bA=bA):
                            s.wait_ge(gq.h, bA)
                            for lb in range(2, 4):
                                s.wait_ge(sT.h, otr - 4 + lb + 1)
                                s.copy(ostg[:, (lb - 2) * P:(lb - 1) * P],
                                       psO[:]).then_inc(sS.h, 1)
                        blk.scalar(s_b); sS.n += 2
                        oevB = sS.n

                        def g2b_(g, dt=dt, lh=lh, oevB=oevB, _gb=bA):
                            g.wait_ge(gq.h, _gb)
                            g.wait_ge(sS.h, oevB)
                            for lb in range(2, 4):
                                g.dma_start(out[512 * lh + P * lb:512 * lh + P * (lb + 1),
                                                P * dt:P * (dt + 1)],
                                            ostg[:, (lb - 2) * P:(lb - 1) * P]).then_inc(gq.h, 16)
                        blk.gpsimd(g2b_); gq.n += 32

                _bc.__exit__(None, None, None)
                _bc = None

        with nc.Block() as blk:
            qq, gg = sq.n, gq.n

            def q_(sync, qq=qq, gg=gg):
                sync.wait_ge(gq.h, gg)
                sync.wait_ge(sq.h, qq)
            blk.sync(q_)

    return nc


def kernel(**inputs):
    x = np.ascontiguousarray(np.asarray(inputs["L0_attn_x"], dtype=np.float32)[0])
    L = x.shape[0]
    assert L == NCORES * LCORE

    f32 = lambda a: np.ascontiguousarray(np.asarray(a, dtype=np.float32))
    shared = dict(
        WinT=f32(np.asarray(inputs["L0_attn_W_in"]).T),
        WskT=f32(np.asarray(inputs["L0_attn_W_skip"]).T),
        Wd1T=f32(np.asarray(inputs["L0_attn_Wd1"]).T),
        Wd2T=f32(np.asarray(inputs["L0_attn_Wd2"]).T),
        WBCT=f32(np.concatenate([np.asarray(inputs["L0_attn_WB"]).T,
                                 np.asarray(inputs["L0_attn_WC"]).T], axis=1)),
        WoutT=f32(np.asarray(inputs["L0_attn_W_out"]).T),
        guT=f32(np.asarray(inputs["L0_ffn_w_gate_up"]).T),
        dwT=f32(np.asarray(inputs["L0_ffn_w_down"]).T),
        A_log=f32(inputs["L0_attn_A_log"]),
        conv_w=f32(np.asarray(inputs["L0_attn_conv_w"]).reshape(E, 4)),
        conv_b=f32(np.asarray(inputs["L0_attn_conv_b"]).reshape(1, E)),
        bd2=f32(np.asarray(inputs["L0_attn_bd2"]).reshape(1, E)),
        W_D=f32(np.asarray(inputs["L0_attn_W_D"]).reshape(1, E)),
        norm_w=f32(np.asarray(inputs["L0_attn_norm_w"]).reshape(1, D)),
        ident=np.eye(P, dtype=np.float32),
    )

    in_maps = []
    for k in range(NCORES):
        oh = np.zeros((1, NCORES), dtype=np.float32)
        oh[0, k] = 1.0
        xs = np.ascontiguousarray(x[k * LCORE:(k + 1) * LCORE])
        if k == 0:
            xp = np.zeros((HALO, D), dtype=np.float32)
        else:
            xp = np.ascontiguousarray(x[k * LCORE - HALO:k * LCORE])
        m = dict(shared)
        m.update(x_shard=xs, x_prev=xp, onehot=oh)
        in_maps.append(m)

    if "nc" not in _NC_CACHE:
        _NC_CACHE["nc"] = build_nc(NCORES)
    nc = _NC_CACHE["nc"]
    res = run_bass_kernel_spmd(nc, in_maps, core_ids=list(range(NCORES)))
    outs = [np.asarray(res.results[k]["out_shard"], dtype=np.float32) for k in range(NCORES)]
    return np.concatenate(outs, axis=0)[None, :, :]


# version 16 (final)
# speedup vs baseline: 1.1655x; 1.0181x over previous
import numpy as np
from contextlib import ExitStack

import concourse.bass as bass
import concourse.mybir as mybir
from concourse.bass_utils import run_bass_kernel_spmd

F32 = mybir.dt.float32
BF16 = mybir.dt.bfloat16
AF = mybir.ActivationFunctionType
ALU = mybir.AluOpType
AX = mybir.AxisListType

P = 128
D = 2048
E = 4096
N = 16
DTR = 128
INTER = 8192
LCORE = 1024
HALO = 128
LX = LCORE + HALO
TE = E // P
TD = D // P
NCORES = 8
EPS = 1e-5
NPG = 2
NGR = N // NPG

_NC_CACHE = {}


def build_nc(n_cores=NCORES):
    nc = bass.Bass("TRN2", target_bir_lowering=False)

    xin = nc.dram_tensor("x_shard", [LCORE, D], F32, kind="ExternalInput").ap()
    xprev = nc.dram_tensor("x_prev", [HALO, D], F32, kind="ExternalInput").ap()
    onehot = nc.dram_tensor("onehot", [1, NCORES], F32, kind="ExternalInput").ap()
    ident_in = nc.dram_tensor("ident", [P, P], F32, kind="ExternalInput").ap()

    WinT = nc.dram_tensor("WinT", [D, E], F32, kind="ExternalInput").ap()
    WskT = nc.dram_tensor("WskT", [D, E], F32, kind="ExternalInput").ap()
    Wd1T = nc.dram_tensor("Wd1T", [E, DTR], F32, kind="ExternalInput").ap()
    Wd2T = nc.dram_tensor("Wd2T", [DTR, E], F32, kind="ExternalInput").ap()
    WBCT = nc.dram_tensor("WBCT", [E, 2 * N], F32, kind="ExternalInput").ap()
    WoutT = nc.dram_tensor("WoutT", [E, D], F32, kind="ExternalInput").ap()
    guT = nc.dram_tensor("guT", [D, 2 * INTER], F32, kind="ExternalInput").ap()
    dwT = nc.dram_tensor("dwT", [INTER, D], F32, kind="ExternalInput").ap()
    A_log = nc.dram_tensor("A_log", [E, N], F32, kind="ExternalInput").ap()
    conv_w = nc.dram_tensor("conv_w", [E, 4], F32, kind="ExternalInput").ap()
    conv_b = nc.dram_tensor("conv_b", [1, E], F32, kind="ExternalInput").ap()
    bd2_in = nc.dram_tensor("bd2", [1, E], F32, kind="ExternalInput").ap()
    WD_in = nc.dram_tensor("W_D", [1, E], F32, kind="ExternalInput").ap()
    nw_in = nc.dram_tensor("norm_w", [1, D], F32, kind="ExternalInput").ap()

    out = nc.dram_tensor("out_shard", [LCORE, D], F32, kind="ExternalOutput").ap()

    uc_sp = nc.dram_tensor("uc_sp", [E, LCORE], BF16)
    ssk_sp = nc.dram_tensor("ssk_sp", [E, LCORE], BF16)
    z_sp = nc.dram_tensor("z_sp", [E, LCORE], BF16)
    o1_sp = nc.dram_tensor("o1_sp", [D, LCORE], F32)
    xt_sp = nc.dram_tensor("xt_sp", [D, LCORE], F32)
    winb_sp = nc.dram_tensor("winb_sp", [D, E], BF16)
    wskb_sp = nc.dram_tensor("wskb_sp", [D, E], BF16)
    bc_sp = nc.dram_tensor("bc_sp", [2 * N, LCORE], BF16)
    ag_in = [nc.dram_tensor(f"ag_in_{t}", [P, 2 * N], F32) for t in range(TE)]
    ag_out = [nc.dram_tensor(f"ag_out_{t}", [n_cores * P, 2 * N], F32) for t in range(TE)]

    es = ExitStack()
    with es:
        class Sem:
            def __init__(self, name):
                self.h = es.enter_context(nc.semaphore(name))
                self.n = 0

        sq = Sem("sq"); gq = Sem("gq")
        sT = Sem("sT"); sS = Sem("sS"); sV = Sem("sV"); sP = Sem("sP"); sC = Sem("sC")

        def vf(v, inst):
            inst.then_inc(sV.h, 1); sV.n += 1
            v.wait_ge(sV.h, sV.n)

        def sf(s, inst):
            inst.then_inc(sS.h, 1); sS.n += 1
            s.wait_ge(sS.h, sS.n)

        ident = es.enter_context(nc.sbuf_tensor("ident_sb", [P, P], F32))
        ones_f = es.enter_context(nc.sbuf_tensor("ones_f", [1, P], F32))
        ones_b = es.enter_context(nc.sbuf_tensor("ones_b", [1, P], BF16))
        onescol_b = es.enter_context(nc.sbuf_tensor("onescol_b", [P, 1], BF16))
        oh_bc = es.enter_context(nc.sbuf_tensor("oh_bc", [P, NCORES], F32))
        A_sb = es.enter_context(nc.sbuf_tensor("A_sb", [P, TE * N], F32))
        cw_sb = es.enter_context(nc.sbuf_tensor("cw_sb", [P, TE * 4], F32))
        cb_sb = es.enter_context(nc.sbuf_tensor("cb_sb", [P, TE], F32))
        bd2_sb = es.enter_context(nc.sbuf_tensor("bd2_sb", [P, TE], F32))
        wd_sb = es.enter_context(nc.sbuf_tensor("wd_sb", [P, TE], F32))
        nw_sb = es.enter_context(nc.sbuf_tensor("nw_sb", [P, TD], F32))
        hbuf = es.enter_context(nc.sbuf_tensor("hbuf", [P, N * NCORES], F32))

        _scoped = {}

        def salloc(name, shape, dt):
            cm = nc.sbuf_tensor(name, shape, dt)
            t = cm.__enter__()
            _scoped[name] = cm
            return t

        def sfree(*names):
            for nm in names:
                _scoped.pop(nm).__exit__(None, None, None)

        dpre_b = salloc("dpre_b", [P, LCORE], BF16)
        uc_ring = salloc("uc_ring", [P, 2 * LCORE], BF16)
        ssk_tile = salloc("ssk_tile", [P, LCORE], BF16)

        def barrier():
            with nc.Block() as blk:
                def mk(sem):
                    def f(e):
                        e.nop().then_inc(sem.h, 1)
                    return f
                blk.tensor(mk(sT)); blk.scalar(mk(sS)); blk.vector(mk(sV))
                blk.gpsimd(mk(sP)); blk.sync(mk(sq))
                sT.n += 1; sS.n += 1; sV.n += 1; sP.n += 1; sq.n += 1
                tt, ss, vv, pp, qq, gg = sT.n, sS.n, sV.n, sP.n, sq.n, gq.n

                def wall(e):
                    e.wait_ge(sT.h, tt); e.wait_ge(sS.h, ss); e.wait_ge(sV.h, vv)
                    e.wait_ge(sP.h, pp); e.wait_ge(sq.h, qq); e.wait_ge(gq.h, gg)
                blk.tensor(wall); blk.scalar(wall); blk.vector(wall)
                blk.gpsimd(wall); blk.sync(wall)

        with (
            nc.sbuf_tensor("vrow", [TE, P], F32) as vrow,
            nc.psum_tensor("psK0", [P, P], F32) as psK0,
        ):
            with nc.Block() as blk:
                def q_(sync, _qb=sq.n):
                    sync.wait_ge(sq.h, _qb)
                    sync.dma_start(ident[:], ident_in).then_inc(sq.h, 16)
                    sync.dma_start(oh_bc[0:1, :], onehot).then_inc(sq.h, 16)
                blk.sync(q_); sq.n += 32
                ld0 = sq.n

                def g_(g):
                    g.memset(ones_f[:], 1.0)
                    g.memset(ones_b[:], 1.0)
                    g.memset(onescol_b[:], 1.0)
                    g.memset(hbuf[:], 0.0).then_inc(sP.h, 1)
                blk.gpsimd(g_); sP.n += 1
                ones_done = sP.n

            for src, dst, rows in ((conv_b, cb_sb, TE), (bd2_in, bd2_sb, TE),
                                   (WD_in, wd_sb, TE), (nw_in, nw_sb, TD)):
                with nc.Block() as blk:
                    sprev = sS.n

                    def q_(sync, src=src, rows=rows, sprev=sprev, _qb=sq.n):
                        sync.wait_ge(sq.h, _qb)
                        sync.wait_ge(sS.h, sprev)
                        sync.dma_start(vrow[0:rows, :],
                                       src.rearrange("o (r p) -> (o r) p", p=P)).then_inc(sq.h, 16)
                    blk.sync(q_); sq.n += 16
                    ldv = sq.n

                    def t_(t, rows=rows, ldv=ldv):
                        t.wait_ge(sq.h, ldv)
                        t.wait_ge(sS.h, sS.n)
                        t.transpose(psK0[:, 0:rows], vrow[0:rows, :],
                                    ident[0:rows, 0:rows]).then_inc(sT.h, 1)
                    blk.tensor(t_); sT.n += 1
                    trv = sT.n

                    def s_(s, dst=dst, rows=rows, trv=trv):
                        s.wait_ge(sT.h, trv)
                        s.copy(dst[:, 0:rows], psK0[:, 0:rows]).then_inc(sS.h, 1)
                    blk.scalar(s_); sS.n += 1

            with nc.Block() as blk:
                def q_(sync, _qb=sq.n):
                    sync.wait_ge(sq.h, _qb)
                    for t in range(TE):
                        sync.dma_start(cw_sb[:, 4 * t:4 * t + 4],
                                       conv_w[P * t:P * (t + 1), :]).then_inc(sq.h, 16)
                        sync.dma_start(A_sb[:, N * t:N * (t + 1)],
                                       A_log[P * t:P * (t + 1), :]).then_inc(sq.h, 16)
                blk.sync(q_); sq.n += 32 * TE
                lda = sq.n

                def s_(s):
                    s.wait_ge(sq.h, lda)
                    for t in range(TE):
                        s.activation(A_sb[:, N * t:N * (t + 1)], A_sb[:, N * t:N * (t + 1)],
                                     AF.Exp).then_inc(sS.h, 1)
                blk.scalar(s_); sS.n += TE
                expa = sS.n

                def v_(v):
                    v.wait_ge(sS.h, expa)
                    v.tensor_scalar_mul(A_sb[:], A_sb[:], -1.0).then_inc(sV.h, 1)
                blk.vector(v_); sV.n += 1

                sprev2 = sS.n

                def t_(t, sprev2=sprev2, cur=None):
                    t.wait_ge(sq.h, sq.n)
                    t.wait_ge(sP.h, ones_done)
                    t.wait_ge(sS.h, sprev2)
                    t.matmul(psK0[:, 0:NCORES], ones_f[0:1, :], oh_bc[0:1, :],
                             start=True, stop=True).then_inc(sT.h, 1)
                blk.tensor(t_); sT.n += 1
                ohm = sT.n

                def s2_(s):
                    s.wait_ge(sT.h, ohm)
                    s.copy(oh_bc[:], psK0[:, 0:NCORES]).then_inc(sS.h, 1)
                blk.scalar(s2_); sS.n += 1

        barrier()

        keW_row = salloc("keW_row", [1, E], BF16)
        keS_row = salloc("keS_row", [1, E], BF16)
        keW_f = salloc("keW_f", [1, E], F32)
        keS_f = salloc("keS_f", [1, E], F32)
        with (
            nc.sbuf_tensor("wstg", [P, 2 * E], F32) as wstg,
            nc.sbuf_tensor("wstgb", [P, 2 * E], BF16) as wstgb,
            nc.sbuf_tensor("ke_acc", [P, TE], F32) as ke_acc,
            nc.sbuf_tensor("keT", [TE, P], F32) as keT,
            nc.psum_tensor("psK1", [P, P], F32) as psK1,
            nc.psum_tensor("psK2", [P, TE], F32) as psK2,
        ):
            for wi, (wsrc, wdst, kef, keb) in enumerate(
                    ((WinT, winb_sp, keW_f, keW_row), (WskT, wskb_sp, keS_f, keS_row))):
                for dt in range(TD):
                    buf = dt % 2
                    with nc.Block() as blk:
                        sfree0 = sS.n
                        tfree0 = sT.n

                        def q_(sync, dt=dt, buf=buf, wsrc=wsrc, sfree0=sfree0, tfree0=tfree0,
                               _qb=sq.n):
                            sync.wait_ge(sq.h, _qb)
                            sync.wait_ge(sS.h, sfree0)
                            sync.wait_ge(sT.h, tfree0)
                            sync.dma_start(wstg[:, buf * E:(buf + 1) * E],
                                           wsrc[P * dt:P * (dt + 1), :]).then_inc(sq.h, 16)
                        blk.sync(q_); sq.n += 16
                        ldw_ = sq.n

                        def s_(s, dt=dt, buf=buf, ldw_=ldw_):
                            s.wait_ge(sq.h, ldw_)
                            s.activation(wstgb[:, buf * E:(buf + 1) * E],
                                         wstg[:, buf * E:(buf + 1) * E],
                                         AF.Copy, scale=nw_sb[:, dt:dt + 1]).then_inc(sS.h, 1)
                        blk.scalar(s_); sS.n += 1
                        cast_done = sS.n

                        def q2_(sync, dt=dt, buf=buf, wdst=wdst, cast_done=cast_done,
                                _qb=sq.n):
                            sync.wait_ge(sq.h, _qb)
                            sync.wait_ge(sS.h, cast_done)
                            sync.dma_start(wdst.ap()[P * dt:P * (dt + 1), :],
                                           wstgb[:, buf * E:(buf + 1) * E]).then_inc(sq.h, 16)
                        blk.sync(q2_); sq.n += 16

                        vke_prev = sV.n

                        def t_(t, dt=dt, buf=buf, cast_done=cast_done, vke_prev=vke_prev):
                            t.wait_ge(sS.h, cast_done)
                            t.wait_ge(sV.h, vke_prev)
                            for eb in range(TE):
                                mm = t.matmul(psK2[:, eb:eb + 1],
                                              wstgb[:, buf * E + P * eb:buf * E + P * (eb + 1)],
                                              onescol_b[:],
                                              start=True, stop=True)
                                if eb == TE - 1:
                                    mm.then_inc(sT.h, 1)
                        blk.tensor(t_); sT.n += 1
                        kemm_t = sT.n

                        vke_prev2 = sV.n

                        def vke_(v, dt=dt, kemm_t=kemm_t, vke_prev2=vke_prev2):
                            v.wait_ge(sT.h, kemm_t)
                            v.wait_ge(sV.h, vke_prev2)
                            if dt == 0:
                                v.tensor_copy(ke_acc[:], psK2[:]).then_inc(sV.h, 1)
                            else:
                                v.tensor_add(ke_acc[:], ke_acc[:], psK2[:]).then_inc(sV.h, 1)
                        blk.vector(vke_); sV.n += 1

                with nc.Block() as blk:
                    keacc_v = sV.n

                    def t2_(t, keacc_v=keacc_v, gprev=gq.n):
                        t.wait_ge(sV.h, keacc_v)
                        t.wait_ge(gq.h, gprev)
                        t.transpose(psK1[0:TE, :], ke_acc[:], ident[:]).then_inc(sT.h, 1)
                    blk.tensor(t2_); sT.n += 1
                    ketr = sT.n

                    def s15_(s, ketr=ketr):
                        s.wait_ge(sT.h, ketr)
                        s.copy(keT[:], psK1[0:TE, :]).then_inc(sS.h, 1)
                    blk.scalar(s15_); sS.n += 1
                    ketc = sS.n

                    def g_(g, kef=kef, ketc=ketc, _gb=gq.n):
                        g.wait_ge(gq.h, _gb)
                        g.wait_ge(sS.h, ketc)
                        for eb in range(TE):
                            g.dma_start(kef[0:1, P * eb:P * (eb + 1)],
                                        keT[eb:eb + 1, :]).then_inc(gq.h, 16)
                    blk.gpsimd(g_); gq.n += 16 * TE
                    kedma = gq.n

                    def s2_(s, keb=keb, kef=kef, kedma=kedma):
                        s.wait_ge(gq.h, kedma)
                        s.copy(keb[:], kef[:]).then_inc(sS.h, 1)
                    blk.scalar(s2_); sS.n += 1

        barrier()

        xT_bf = salloc("xT_bf", [P, TD * LX], BF16)
        rb_bc = salloc("rb_bc", [P, LX], F32)
        r_row = salloc("r_row", [1, LX], F32)
        nmr_rowb = salloc("nmr_rowb", [1, LX], BF16)
        nmr_f = salloc("nmr_f", [1, LX], F32)
        es2 = ExitStack()
        sx = es2.enter_context(nc.sbuf_tensor("sx", [P, 2 * D], F32))
        xstage = es2.enter_context(nc.sbuf_tensor("xstage", [P, TD * P], F32))
        sxsq = es2.enter_context(nc.sbuf_tensor("sxsq", [P, D], F32))
        scol = es2.enter_context(nc.sbuf_tensor("scol", [P, 16], F32))
        rcols = es2.enter_context(nc.sbuf_tensor("rcols", [P, 9], F32))
        nmrcols = es2.enter_context(nc.sbuf_tensor("nmrcols", [P, 9], F32))
        rowst = es2.enter_context(nc.sbuf_tensor("rowst", [9, P], F32))
        rowst2 = es2.enter_context(nc.sbuf_tensor("rowst2", [9, P], F32))
        psX = es2.enter_context(nc.psum_tensor("psX", [P, P], F32))
        psX2 = es2.enter_context(nc.psum_tensor("psX2", [P, P], F32))
        with es2:
            NJ = 9
            for j in range(NJ):
                buf = j % 2
                with nc.Block() as blk:
                    vprev = sV.n
                    tprev = sT.n
                    s_base = sS.n

                    def q_(sync, j=j, buf=buf, vprev=vprev, tprev=tprev, _qb=sq.n):
                        sync.wait_ge(sq.h, _qb)
                        sync.wait_ge(sV.h, vprev)
                        sync.wait_ge(sT.h, tprev)
                        src = xprev if j == 0 else xin[P * (j - 1):P * j, :]
                        sync.dma_start(sx[:, buf * D:(buf + 1) * D], src).then_inc(sq.h, 16)
                    blk.sync(q_); sq.n += 16
                    ldx = sq.n

                    def s_(s, buf=buf, ldx=ldx):
                        s.wait_ge(sq.h, ldx)
                        s.wait_ge(sV.h, sV.n)
                        s.activation(sxsq[:], sx[:, buf * D:(buf + 1) * D], AF.Square).then_inc(sS.h, 1)
                    blk.scalar(s_); sS.n += 1
                    sqd = sS.n
                    s_base = sS.n

                    def v_(v, j=j, buf=buf, ldx=ldx, sqd=sqd):
                        x_ = sx[:, buf * D:(buf + 1) * D]
                        v.wait_ge(sq.h, ldx)
                        vf(v, v.reduce_sum(scol[:, 0:1], x_, AX.X))
                        v.wait_ge(sS.h, sqd)
                        vf(v, v.reduce_sum(scol[:, 1:2], sxsq[:], AX.X))
                        vf(v, v.tensor_scalar_mul(scol[:, 2:3], scol[:, 0:1], 1.0 / D))
                        vf(v, v.tensor_mul(scol[:, 3:4], scol[:, 2:3], scol[:, 2:3]))
                        vf(v, v.tensor_scalar_mul(scol[:, 4:5], scol[:, 1:2], 1.0 / D))
                        vf(v, v.tensor_sub(scol[:, 5:6], scol[:, 4:5], scol[:, 3:4]))
                        vf(v, v.tensor_scalar_add(scol[:, 6:7], scol[:, 5:6], EPS))
                        vf(v, v.reciprocal(scol[:, 7:8], scol[:, 6:7]))
                        vf(v, v.tensor_mul(scol[:, 8:9], scol[:, 5:6], scol[:, 7:8]))
                        v.tensor_scalar_add(scol[:, 9:10], scol[:, 8:9], EPS).then_inc(sV.h, 1)
                    blk.vector(v_); sV.n += 1
                    vstat = sV.n

                    def s2_(s, vstat=vstat):
                        s.wait_ge(sV.h, vstat)
                        sf(s, s.activation(scol[:, 10:11], scol[:, 6:7], AF.Ln))
                        s.activation(scol[:, 11:12], scol[:, 9:10], AF.Ln).then_inc(sS.h, 1)
                    blk.scalar(s2_); sS.n += 1
                    slog = sS.n

                    def v2_(v, slog=slog):
                        v.wait_ge(sS.h, slog)
                        v.tensor_add(scol[:, 12:13], scol[:, 10:11], scol[:, 11:12]).then_inc(sV.h, 1)
                    blk.vector(v2_); sV.n += 1
                    vln = sV.n

                    def s3_(s, vln=vln):
                        s.wait_ge(sV.h, vln)
                        s.activation(scol[:, 13:14], scol[:, 12:13], AF.Exp, scale=-0.5).then_inc(sS.h, 1)
                    blk.scalar(s3_); sS.n += 1
                    sr = sS.n

                    def v3_(v, j=j, sr=sr):
                        v.wait_ge(sS.h, sr)
                        vf(v, v.tensor_copy(rcols[:, j:j + 1], scol[:, 13:14]))
                        vf(v, v.tensor_mul(scol[:, 14:15], scol[:, 2:3], scol[:, 13:14]))
                        v.tensor_scalar_mul(nmrcols[:, j:j + 1], scol[:, 14:15], -1.0).then_inc(sV.h, 1)
                    blk.vector(v3_); sV.n += 1

                    s_evac_base = sS.n
                    g_evac_base = gq.n
                    nse = 2 if j >= 1 else 1

                    def t_(t, j=j, buf=buf, ldx=ldx, s_evac_base=s_evac_base, nse=nse):
                        t.wait_ge(sq.h, ldx)
                        x_ = sx[:, buf * D:(buf + 1) * D]
                        for dt in range(TD):
                            ps = psX if dt % 2 == 0 else psX2
                            if dt >= 2:
                                t.wait_ge(sS.h, s_evac_base + nse * (dt - 1))
                            t.transpose(ps[:], x_[:, P * dt:P * (dt + 1)], ident[:]).then_inc(sT.h, 1)
                    blk.tensor(t_); sT.n += TD
                    ttr = sT.n

                    def s4_(s, j=j, ttr=ttr, g_evac_base=g_evac_base):
                        if j >= 1:
                            s.wait_ge(gq.h, g_evac_base)
                        for dt in range(TD):
                            s.wait_ge(sT.h, ttr - TD + dt + 1)
                            ps = psX if dt % 2 == 0 else psX2
                            s.copy(xT_bf[:, dt * LX + P * j:dt * LX + P * (j + 1)],
                                   ps[:]).then_inc(sS.h, 1)
                            if j >= 1:
                                s.copy(xstage[:, dt * P:(dt + 1) * P],
                                       ps[:]).then_inc(sS.h, 1)
                    blk.scalar(s4_); sS.n += nse * TD
                    sxe = sS.n

                    if j >= 1:
                        def g_(g, j=j, sxe=sxe, nse=nse, _gb=gq.n):
                            g.wait_ge(gq.h, _gb)
                            for dt in range(TD):
                                g.wait_ge(sS.h, sxe - nse * TD + nse * (dt + 1))
                                g.dma_start(xt_sp.ap()[P * dt:P * (dt + 1),
                                                       P * (j - 1):P * j],
                                            xstage[:, dt * P:(dt + 1) * P]).then_inc(gq.h, 16)
                        blk.gpsimd(g_); gq.n += 16 * TD

            with nc.Block() as blk:
                vdone = sV.n
                gdone = gq.n
                sdone = sS.n

                def t_(t, vdone=vdone, gdone=gdone, sdone=sdone):
                    t.wait_ge(sV.h, vdone)
                    t.wait_ge(gq.h, gdone)
                    t.wait_ge(sS.h, sdone)
                    t.transpose(psX[0:9, :], rcols[:], ident[:]).then_inc(sT.h, 1)
                    t.transpose(psX2[0:9, :], nmrcols[:], ident[:]).then_inc(sT.h, 1)
                blk.tensor(t_); sT.n += 2
                rtr = sT.n

                def se_(s, rtr=rtr):
                    s.wait_ge(sT.h, rtr)
                    s.copy(rowst[:], psX[0:9, :])
                    s.copy(rowst2[:], psX2[0:9, :]).then_inc(sS.h, 1)
                blk.scalar(se_); sS.n += 1
                rse = sS.n

                def g_(g, rse=rse, _gb=gq.n):
                    g.wait_ge(gq.h, _gb)
                    g.wait_ge(sS.h, rse)
                    for j in range(9):
                        g.dma_start(r_row[0:1, P * j:P * (j + 1)], rowst[j:j + 1, :]).then_inc(gq.h, 16)
                        g.dma_start(nmr_f[0:1, P * j:P * (j + 1)], rowst2[j:j + 1, :]).then_inc(gq.h, 16)
                blk.gpsimd(g_); gq.n += 16 * 18
                rdma = gq.n

                def s_(s, rdma=rdma):
                    s.wait_ge(gq.h, rdma)
                    s.copy(nmr_rowb[:], nmr_f[:]).then_inc(sS.h, 1)
                blk.scalar(s_); sS.n += 1

                sb_base = sS.n

                def t2_(t, rdma=rdma, sb_base=sb_base):
                    t.wait_ge(gq.h, rdma)
                    for i in range(9):
                        ps = psX if i % 2 == 0 else psX2
                        if i >= 2:
                            t.wait_ge(sS.h, sb_base + i - 1)
                        t.matmul(ps[:], ones_f[0:1, :], r_row[0:1, P * i:P * (i + 1)],
                                 start=True, stop=True).then_inc(sT.h, 1)
                blk.tensor(t2_); sT.n += 9
                btr = sT.n

                def s5_(s, btr=btr):
                    for i in range(9):
                        s.wait_ge(sT.h, btr - 9 + i + 1)
                        ps = psX if i % 2 == 0 else psX2
                        s.copy(rb_bc[:, P * i:P * (i + 1)], ps[:]).then_inc(sS.h, 1)
                blk.scalar(s5_); sS.n += 9

        barrier()

        WS = 160
        ucmm_done = [0] * TE
        ucspill_done = [0] * TE
        ucdone_arr = [0] * TE
        sskd_arr = [0] * TE
        es2 = ExitStack()
        wv = es2.enter_context(nc.sbuf_tensor("wv", [P, 2 * TD * P], BF16))
        wvs = es2.enter_context(nc.sbuf_tensor("wvs", [P, 2 * TD * P], BF16))
        wsm = es2.enter_context(nc.sbuf_tensor("wsm", [P, 2 * WS], BF16))
        wsm_f = es2.enter_context(nc.sbuf_tensor("wsm_f", [P, 2 * WS], F32))
        up_ext = es2.enter_context(nc.sbuf_tensor("up_ext", [P, LX], F32))
        cva = es2.enter_context(nc.sbuf_tensor("cva", [P, LCORE], F32))
        cvb = es2.enter_context(nc.sbuf_tensor("cvb", [P, LCORE], F32))
        skf = es2.enter_context(nc.sbuf_tensor("skf", [P, LCORE], F32))
        psU = es2.enter_context(nc.psum_tensor("psU", [P, 512], F32))
        psU2 = es2.enter_context(nc.psum_tensor("psU2", [P, 512], F32))
        psDP = es2.enter_context(nc.psum_tensor("psDP", [P, LCORE], F32))
        psBC = es2.enter_context(nc.psum_tensor("psBC", [2 * N, LCORE], F32))
        with es2:
            UCHUNKS = [(0, 512), (512, 1024), (1024, LX)]
            _bc3 = None
            for t in range(TE):
                buf = t % 2
                ucslot = t % 2
                if t % 4 == 0:
                    if _bc3 is not None:
                        _bc3.__exit__(None, None, None)
                    _bc3 = nc.Block()
                    blk = _bc3.__enter__()
                if True:
                    if t == 0:
                        s3tbase = sT.n
                        s3t_arr = {}
                        s3wc_arr = {}
                    tprev = s3t_arr.get(t - 2, s3tbase)

                    def q_(sync, t=t, buf=buf, tprev=tprev, _qb=sq.n,
                           _wc=s3wc_arr.get(t - 2, 0)):
                        sync.wait_ge(sq.h, _qb)
                        sync.wait_ge(sT.h, tprev)
                        if t >= 2:
                            sync.wait_ge(sV.h, _wc)
                        sync.dma_start(
                            wv[:, buf * TD * P:(buf + 1) * TD * P].rearrange("p (a c) -> p a c", c=P),
                            winb_sp.ap()[:, P * t:P * (t + 1)].rearrange("(a p) c -> p a c", p=P),
                        ).then_inc(sq.h, 16)
                        sync.dma_start(
                            wvs[:, buf * TD * P:(buf + 1) * TD * P].rearrange("p (a c) -> p a c", c=P),
                            wskb_sp.ap()[:, P * t:P * (t + 1)].rearrange("(a p) c -> p a c", p=P),
                        ).then_inc(sq.h, 16)
                        sync.dma_start(wsm_f[:, buf * WS:buf * WS + P],
                                       Wd1T[P * t:P * (t + 1), :]).then_inc(sq.h, 16)
                        sync.dma_start(wsm_f[:, buf * WS + P:buf * WS + WS],
                                       WBCT[P * t:P * (t + 1), :]).then_inc(sq.h, 16)
                    blk.sync(q_); sq.n += 64
                    ldw1 = sq.n

                    def v00_(v, buf=buf, ldw1=ldw1):
                        v.wait_ge(sq.h, ldw1)
                        v.tensor_copy(wsm[:, buf * WS:(buf + 1) * WS],
                                      wsm_f[:, buf * WS:(buf + 1) * WS]).then_inc(sV.h, 1)
                    blk.vector(v00_); sV.n += 1
                    wdcast = sV.n
                    s3wc_arr[t] = wdcast

                    vprev_evac = sV.n

                    def t_(t_eng, t=t, buf=buf, ldw1=ldw1, vprev_evac=vprev_evac):
                        t_eng.wait_ge(sq.h, ldw1)
                        t_eng.wait_ge(sV.h, vprev_evac)
                        for ci, (c0, c1) in enumerate(UCHUNKS):
                            ps = psU if ci % 2 == 0 else psU2
                            cw = c1 - c0
                            if ci == 2:
                                t_eng.wait_ge(sV.h, vprev_evac + 1)
                            for dt in range(TD):
                                t_eng.matmul(ps[:, 0:cw],
                                             wv[:, buf * TD * P + dt * P:buf * TD * P + (dt + 1) * P],
                                             xT_bf[:, dt * LX + c0:dt * LX + c1],
                                             start=(dt == 0), stop=False)
                            t_eng.matmul(ps[:, 0:cw], keW_row[0:1, P * t:P * (t + 1)],
                                         nmr_rowb[0:1, c0:c1],
                                         start=False, stop=True).then_inc(sT.h, 1)
                    blk.tensor(t_); sT.n += 3
                    umm = sT.n

                    def v_(v, umm=umm):
                        for ci, (c0, c1) in enumerate(UCHUNKS):
                            cw = c1 - c0
                            ps = psU if ci % 2 == 0 else psU2
                            v.wait_ge(sT.h, umm - 3 + ci + 1)
                            v.tensor_mul(up_ext[:, c0:c1], ps[:, 0:cw],
                                         rb_bc[:, c0:c1]).then_inc(sV.h, 1)
                    blk.vector(v_); sV.n += 3
                    uev = sV.n

                    def v2_(v, t=t, uev=uev):
                        if t >= 1:
                            v.wait_ge(sS.h, ucdone_arr[t - 1])
                        v.wait_ge(sV.h, uev)
                        vf(v, v.tensor_single_scalar(cva[:], up_ext[:, 125:125 + LCORE],
                                                     cw_sb[:, 4 * t:4 * t + 1], ALU.mult))
                        vf(v, v.scalar_tensor_tensor(cvb[:], up_ext[:, 126:126 + LCORE],
                                                     cw_sb[:, 4 * t + 1:4 * t + 2], cva[:],
                                                     ALU.mult, ALU.add))
                        vf(v, v.scalar_tensor_tensor(cva[:], up_ext[:, 127:127 + LCORE],
                                                     cw_sb[:, 4 * t + 2:4 * t + 3], cvb[:],
                                                     ALU.mult, ALU.add))
                        v.scalar_tensor_tensor(cvb[:], up_ext[:, 128:128 + LCORE],
                                               cw_sb[:, 4 * t + 3:4 * t + 4], cva[:],
                                               ALU.mult, ALU.add).then_inc(sV.h, 1)
                    blk.vector(v2_); sV.n += 1
                    convd = sV.n

                    def s2_(s, t=t, ucslot=ucslot, convd=convd):
                        s.wait_ge(sV.h, convd)
                        if t >= 2:
                            s.wait_ge(sT.h, ucmm_done[t - 2])
                            s.wait_ge(gq.h, ucspill_done[t - 2])
                        s.activation(uc_ring[:, ucslot * LCORE:(ucslot + 1) * LCORE], cvb[:],
                                     AF.Silu, bias=cb_sb[:, t:t + 1]).then_inc(sS.h, 1)
                    blk.scalar(s2_); sS.n += 1
                    ucdone = sS.n
                    ucdone_arr[t] = ucdone

                    def g_(g, t=t, ucslot=ucslot, ucdone=ucdone, _gb=gq.n):
                        g.wait_ge(gq.h, _gb)
                        g.wait_ge(sS.h, ucdone)
                        g.dma_start(uc_sp.ap()[P * t:P * (t + 1), :],
                                    uc_ring[:, ucslot * LCORE:(ucslot + 1) * LCORE]).then_inc(gq.h, 16)
                    blk.gpsimd(g_); gq.n += 16
                    ucspill_done[t] = gq.n

                    def t2_(t_eng, t=t, buf=buf, ucslot=ucslot, ucdone=ucdone, wdcast=wdcast,
                            _dummy=None):
                        t_eng.wait_ge(sS.h, ucdone)
                        t_eng.wait_ge(sV.h, wdcast)
                        uc = uc_ring[:, ucslot * LCORE:(ucslot + 1) * LCORE]
                        for lc in range(2):
                            t_eng.matmul(psDP[:, 512 * lc:512 * (lc + 1)],
                                         wsm[:, buf * WS:buf * WS + P],
                                         uc[:, 512 * lc:512 * (lc + 1)],
                                         start=(t == 0), stop=(t == TE - 1), skip_group_check=True)
                        for lc in range(2):
                            mm = t_eng.matmul(psBC[:, 512 * lc:512 * (lc + 1)],
                                              wsm[:, buf * WS + P:buf * WS + WS],
                                              uc[:, 512 * lc:512 * (lc + 1)],
                                              start=(t == 0), stop=(t == TE - 1), skip_group_check=True)
                            if lc == 1:
                                mm.then_inc(sT.h, 1)
                    blk.tensor(t2_); sT.n += 1
                    ucmm_done[t] = sT.n

                    def t3_(t_eng, t=t, buf=buf, uev=uev):
                        t_eng.wait_ge(sV.h, uev)
                        for lc in range(2):
                            ps = psU if lc == 0 else psU2
                            c0 = HALO + 512 * lc
                            for dt in range(TD):
                                t_eng.matmul(ps[:],
                                             wvs[:, buf * TD * P + dt * P:buf * TD * P + (dt + 1) * P],
                                             xT_bf[:, dt * LX + c0:dt * LX + c0 + 512],
                                             start=(dt == 0), stop=False)
                            t_eng.matmul(ps[:], keS_row[0:1, P * t:P * (t + 1)],
                                         nmr_rowb[0:1, c0:c0 + 512],
                                         start=False, stop=True).then_inc(sT.h, 1)
                    blk.tensor(t3_); sT.n += 2
                    skmm = sT.n
                    s3t_arr[t] = skmm

                    def v3_(v, t=t, skmm=skmm):
                        if t >= 1:
                            v.wait_ge(sS.h, sskd_arr[t - 1])
                        for lc in range(2):
                            ps = psU if lc == 0 else psU2
                            v.wait_ge(sT.h, skmm - 2 + lc + 1)
                            v.tensor_mul(skf[:, 512 * lc:512 * (lc + 1)], ps[:],
                                         rb_bc[:, HALO + 512 * lc:HALO + 512 * (lc + 1)]).then_inc(sV.h, 1)
                    blk.vector(v3_); sV.n += 2
                    skev = sV.n

                    def s3_(s, t=t, skev=skev):
                        s.wait_ge(sV.h, skev)
                        if t >= 1:
                            s.wait_ge(gq.h, _sskspill[t - 1])
                        s.activation(ssk_tile[:], skf[:], AF.Silu).then_inc(sS.h, 1)
                    blk.scalar(s3_); sS.n += 1
                    sskd = sS.n
                    sskd_arr[t] = sskd

                    def g2_(g, t=t, sskd=sskd, _gb=gq.n):
                        g.wait_ge(gq.h, _gb)
                        g.wait_ge(sS.h, sskd)
                        g.dma_start(ssk_sp.ap()[P * t:P * (t + 1), :], ssk_tile[:]).then_inc(gq.h, 16)
                    blk.gpsimd(g2_); gq.n += 16
                    if t == 0:
                        _sskspill = [0] * TE
                    _sskspill[t] = gq.n

            _bc3.__exit__(None, None, None)
            _bc3 = None
            with nc.Block() as blk:
                tall = sT.n

                def s_(s, tall=tall):
                    s.wait_ge(sT.h, tall)
                    s.wait_ge(gq.h, _sskspill[TE - 1])
                    s.copy(dpre_b[:], psDP[:])
                    s.copy(ssk_tile[0:2 * N, :], psBC[:]).then_inc(sS.h, 1)
                blk.scalar(s_); sS.n += 1
                bcev = sS.n

                def g_(g, bcev=bcev, _gb=gq.n):
                    g.wait_ge(gq.h, _gb)
                    g.wait_ge(sS.h, bcev)
                    g.dma_start(bc_sp.ap(), ssk_tile[0:2 * N, :]).then_inc(gq.h, 16)
                blk.gpsimd(g_); gq.n += 16

        barrier()
        sfree("nmr_f", "nmr_rowb", "r_row", "rb_bc", "xT_bf",
              "keS_f", "keW_f", "keS_row", "keW_row")

        HS = 24
        es2 = ExitStack()
        wd2s = es2.enter_context(nc.sbuf_tensor("wd2s", [P, 2 * P], BF16))
        wd2f = es2.enter_context(nc.sbuf_tensor("wd2f", [P, 2 * P], F32))
        spx = es2.enter_context(nc.sbuf_tensor("spx", [P, LCORE], F32))
        spa = es2.enter_context(nc.sbuf_tensor("spa", [P, LCORE], F32))
        delta_f = es2.enter_context(nc.sbuf_tensor("delta_f", [P, 2 * LCORE], F32))
        srel = es2.enter_context(nc.sbuf_tensor("srel", [P, 2 * LCORE], F32))
        du_f = es2.enter_context(nc.sbuf_tensor("du_f", [P, 2 * LCORE], F32))
        a_slab = es2.enter_context(nc.sbuf_tensor("a_slab", [P, 2 * NPG * LCORE], F32))
        w_slab = es2.enter_context(nc.sbuf_tensor("w_slab", [P, 2 * NPG * LCORE], F32))
        h_slab = es2.enter_context(nc.sbuf_tensor("h_slab", [P, HS * LCORE], BF16))
        p_slab = es2.enter_context(nc.sbuf_tensor("p_slab", [P, 2 * NPG * LCORE], BF16))
        ht_buf = es2.enter_context(nc.sbuf_tensor("ht_buf", [P, NPG * LCORE], F32))
        q_buf = es2.enter_context(nc.sbuf_tensor("q_buf", [P, (N // 2) * LCORE], BF16))
        B_sb = es2.enter_context(nc.sbuf_tensor("B_sb", [P, 4 * LCORE], BF16))
        C_sb = es2.enter_context(nc.sbuf_tensor("C_sb", [P, 4 * LCORE], BF16))
        y_f = es2.enter_context(nc.sbuf_tensor("y_f", [P, LCORE], F32))
        y_f2 = es2.enter_context(nc.sbuf_tensor("y_f2", [P, LCORE], F32))
        trA = es2.enter_context(nc.sbuf_tensor("trA", [P, LCORE], F32))
        trB = es2.enter_context(nc.sbuf_tensor("trB", [P, LCORE], F32))
        trC = es2.enter_context(nc.sbuf_tensor("trC", [P, LCORE], F32))
        zt_bf = es2.enter_context(nc.sbuf_tensor("zt_bf", [P, LCORE], BF16))
        z_bf = es2.enter_context(nc.sbuf_tensor("z_bf", [P, LCORE], BF16))
        stout = es2.enter_context(nc.sbuf_tensor("stout", [P, 2 * N], F32))
        gsb = es2.enter_context(nc.sbuf_tensor("gsb", [P, NCORES * 2 * N], F32))
        hcur = es2.enter_context(nc.sbuf_tensor("hcur", [P, N], F32))
        htmp = es2.enter_context(nc.sbuf_tensor("htmp", [P, N], F32))
        hsel = es2.enter_context(nc.sbuf_tensor("hsel", [P, N], F32))
        psDel = es2.enter_context(nc.psum_tensor("psDel", [P, LCORE], F32))
        with es2:
            ag_done = [0] * TE
            std_done = [0] * TE
            zv_done = [0] * TE
            yred_done = {}
            hsel_done = [0] * TE
            scans_done_v = [0] * TE
            scv = {}
            wpool = {}
            qpool = {}
            htv = {}
            l1h = {}
            dready_a = [0] * TE
            sskl_done = [0] * TE

            def hslot(t, n):
                return ((N * t + n) % HS) * LCORE

            def emit_delta(t):
                buf = t % 2
                slot = t % 2
                with nc.Block() as blk:
                    tprev = sT.n

                    def q_(sync, t=t, buf=buf, tprev=tprev, _qb=sq.n):
                        sync.wait_ge(sq.h, _qb)
                        sync.wait_ge(sT.h, tprev)
                        sync.dma_start(wd2f[:, buf * P:(buf + 1) * P],
                                       Wd2T[:, P * t:P * (t + 1)]).then_inc(sq.h, 16)
                    blk.sync(q_); sq.n += 16
                    ldd = sq.n

                    def g_(g, t=t, slot=slot, _gb=gq.n):
                        g.wait_ge(gq.h, _gb)
                        if t >= 2:
                            g.wait_ge(sP.h, zv_done[t - 2])
                        g.dma_start(uc_ring[:, slot * LCORE:(slot + 1) * LCORE],
                                    uc_sp.ap()[P * t:P * (t + 1), :]).then_inc(gq.h, 16)
                    blk.gpsimd(g_); gq.n += 16
                    lduc = gq.n

                    def s_(s, buf=buf, ldd=ldd):
                        s.wait_ge(sq.h, ldd)
                        s.copy(wd2s[:, buf * P:(buf + 1) * P],
                               wd2f[:, buf * P:(buf + 1) * P]).then_inc(sS.h, 1)
                    blk.scalar(s_); sS.n += 1
                    w2c = sS.n

                    sfree_ = sS.n

                    def t_(t_eng, buf=buf, w2c=w2c, sfree_=sfree_):
                        t_eng.wait_ge(sS.h, w2c)
                        t_eng.wait_ge(sS.h, sfree_)
                        for lc in range(2):
                            mm = t_eng.matmul(psDel[:, 512 * lc:512 * (lc + 1)],
                                              wd2s[:, buf * P:(buf + 1) * P],
                                              dpre_b[:, 512 * lc:512 * (lc + 1)],
                                              start=True, stop=True)
                            if lc == 1:
                                mm.then_inc(sT.h, 1)
                    blk.tensor(t_); sT.n += 1
                    dmm = sT.n

                    vfree = sV.n

                    def v0_(v, t=t, dmm=dmm, vfree=vfree):
                        v.wait_ge(sT.h, dmm)
                        v.wait_ge(sV.h, vfree)
                        v.tensor_scalar_add(spx[:], psDel[:], bd2_sb[:, t:t + 1]).then_inc(sV.h, 1)
                    blk.vector(v0_); sV.n += 1
                    vbias = sV.n

                    def s2_(s, vbias=vbias):
                        s.wait_ge(sV.h, vbias)
                        sf(s, s.activation(spa[:], spx[:], AF.Abs))
                        sf(s, s.activation(spa[:], spa[:], AF.Exp, scale=-1.0))
                        sf(s, s.activation(spa[:], spa[:], AF.Ln, bias=1.0))
                        s.activation(spx[:], spx[:], AF.Relu).then_inc(sS.h, 1)
                    blk.scalar(s2_); sS.n += 1
                    spd = sS.n

                    def v_(v, slot=slot, spd=spd, lduc=lduc,
                           _df=wpool.get((t - 2, N - 1), 0)):
                        v.wait_ge(sS.h, spd)
                        v.wait_ge(sP.h, _df)
                        dslab = delta_f[:, slot * LCORE:(slot + 1) * LCORE]
                        vf(v, v.tensor_add(dslab, spa[:], spx[:]))
                        v.tensor_tensor_scan(srel[:, slot * LCORE:(slot + 1) * LCORE],
                                             dslab, dslab, 0.0, ALU.add, ALU.bypass)
                        v.wait_ge(gq.h, lduc)
                        v.tensor_mul(du_f[:, slot * LCORE:(slot + 1) * LCORE], dslab,
                                     uc_ring[:, slot * LCORE:(slot + 1) * LCORE]).then_inc(sV.h, 1)
                    blk.vector(v_); sV.n += 1
                    dready_a[t] = sV.n

            def emit_combine(blk, t):
                if True:
                    def g_(g, t=t, _gb=gq.n):
                        g.wait_ge(gq.h, _gb)
                        g.wait_ge(sC.h, ag_done[t])
                        if t >= 1:
                            g.wait_ge(sV.h, hsel_done[t - 1])
                        g.dma_start(
                            gsb[:, 0:n_cores * 2 * N].rearrange("p (r c) -> p r c", r=n_cores),
                            ag_out[t].ap().rearrange("(r p) c -> p r c", p=P),
                        ).then_inc(gq.h, 16)
                    blk.gpsimd(g_); gq.n += 16
                    gld = gq.n

                    def v_(v, t=t, gld=gld):
                        v.wait_ge(gq.h, gld)
                        vf(v, v.memset(hcur[:], 0.0))
                        for r in range(n_cores):
                            vf(v, v.tensor_copy(bass.AP(hbuf, r, [[N * NCORES, P], [NCORES, N]]),
                                                hcur[:]))
                            if r < n_cores - 1:
                                vf(v, v.tensor_mul(htmp[:], gsb[:, r * 2 * N + N:(r + 1) * 2 * N],
                                                   hcur[:]))
                                vf(v, v.tensor_add(hcur[:], htmp[:],
                                                   gsb[:, r * 2 * N:r * 2 * N + N]))
                        vf(v, v.tensor_mul(hbuf[:], hbuf[:],
                                           bass.AP(oh_bc, 0, [[NCORES, P], [0, N], [1, NCORES]])))
                        v.tensor_reduce(hsel[:], hbuf.ap().rearrange("p (n r) -> p n r", r=NCORES),
                                        AX.X, ALU.add).then_inc(sV.h, 1)
                    blk.vector(v_); sV.n += 1
                    hsel_done[t] = sV.n

            def emit_corr_group(blk, t, g):
                slot = t % 2
                gbuf = g % 2
                if True:
                    def q_(sync, t=t, g=g, _qb=sq.n,
                           _fr=[qpool.get((t, g * NPG + i - 4),
                                          qpool.get((t - 1, g * NPG + i + 12), 0))
                                for i in range(NPG)]):
                        sync.wait_ge(sq.h, _qb)
                        for i in range(NPG):
                            n = g * NPG + i
                            sync.wait_ge(sP.h, _fr[i])
                            sync.dma_start(
                                C_sb[:, (n % 4) * LCORE:(n % 4 + 1) * LCORE],
                                bass.AP(bc_sp, (N + n) * LCORE, [[0, P], [1, LCORE]]),
                            ).then_inc(sq.h, 16)
                    blk.sync(q_); sq.n += 16 * NPG
                    cld = sq.n

                    def s_(s, t=t, g=g, gbuf=gbuf, slot=slot,
                           _pf=(htv.get((t, (g - 1) * NPG), 0) if g >= 2
                                else (htv.get((t - 1, (g + 6) * NPG), 0) if t >= 1 else 0))):
                        s.wait_ge(sV.h, _pf)
                        for i in range(NPG):
                            n = g * NPG + i
                            s.activation(
                                p_slab[:, (gbuf * NPG + i) * LCORE:(gbuf * NPG + i + 1) * LCORE],
                                srel[:, slot * LCORE:(slot + 1) * LCORE],
                                AF.Exp, scale=A_sb[:, N * t + n:N * t + n + 1]).then_inc(sS.h, 1)
                    blk.scalar(s_); sS.n += NPG
                    pexp = sS.n

                    for i in range(NPG):
                        htv[(t, g * NPG + i)] = sV.n + i + 1

                    def v_(v, t=t, g=g, gbuf=gbuf, slot=slot, pexp=pexp,
                           _hf=[(qpool.get((t, g * NPG + i - 2),
                                           qpool.get((t - 1, g * NPG + i + 14), 0)))
                                for i in range(NPG)]):
                        v.wait_ge(sS.h, pexp)
                        v.wait_ge(sV.h, hsel_done[t])
                        for i in range(NPG):
                            n = g * NPG + i
                            v.wait_ge(sP.h, _hf[i])
                            v.scalar_tensor_tensor(
                                ht_buf[:, i * LCORE:(i + 1) * LCORE],
                                p_slab[:, (gbuf * NPG + i) * LCORE:(gbuf * NPG + i + 1) * LCORE],
                                hsel[:, n:n + 1],
                                h_slab[:, hslot(t, n):hslot(t, n) + LCORE],
                                ALU.mult, ALU.add).then_inc(sV.h, 1)
                    blk.vector(v_); sV.n += NPG

                    def p2_(v, t=t, g=g, cld=cld,
                            _qf=[(l1h.get((t - 1, 1, (g * NPG + i) // 2), 0)
                                  if g * NPG + i < N // 2
                                  else l1h.get((t, 0, ((g * NPG + i) % 8) // 2), 0))
                                 for i in range(NPG)]):
                        v.wait_ge(sq.h, cld)
                        for i in range(NPG):
                            n = g * NPG + i
                            v.wait_ge(sP.h, _qf[i])
                            v.wait_ge(sV.h, htv[(t, n)])
                            v.tensor_mul(
                                q_buf[:, (n % (N // 2)) * LCORE:(n % (N // 2) + 1) * LCORE],
                                ht_buf[:, i * LCORE:(i + 1) * LCORE],
                                C_sb[:, (n % 4) * LCORE:(n % 4 + 1) * LCORE]).then_inc(sP.h, 1)
                    blk.vector(p2_)
                    for i in range(NPG):
                        sP.n += 1
                        qpool[(t, g * NPG + i)] = sP.n

                    half = g // 4
                    pi = g % 4
                    tgt = (trA, trB, trC, (y_f if half == 0 else y_f2))[pi]

                    def pl1_(p, t=t, g=g, tgt=tgt, pi=pi):
                        p.wait_ge(sP.h, qpool[(t, 2 * g + 1)])
                        if pi == 3 and t >= 1:
                            p.wait_ge(sP.h, zv_done[t - 1])
                        p.tensor_add(tgt[:],
                                     q_buf[:, (2 * (g % 4)) * LCORE:(2 * (g % 4) + 1) * LCORE],
                                     q_buf[:, (2 * (g % 4) + 1) * LCORE:(2 * (g % 4) + 2) * LCORE]
                                     ).then_inc(sP.h, 1)
                    blk.gpsimd(pl1_); sP.n += 1
                    l1h[(t, half, pi)] = sP.n

                    if pi == 3:
                        _pb = sP.n

                        def p_(p, t=t, half=half, _pb=_pb):
                            p.wait_ge(sP.h, _pb)
                            dst = y_f if half == 0 else y_f2
                            p.tensor_add(trA[:], trA[:], trB[:])
                            p.tensor_add(dst[:], dst[:], trC[:]).then_inc(sP.h, 1)
                            p.wait_ge(sP.h, _pb + 1)
                            p.tensor_add(dst[:], dst[:], trA[:]).then_inc(sP.h, 1)
                        blk.gpsimd(p_); sP.n += 2
                        yred_done[(t, half)] = sP.n

            def emit_scan_group(blk, t, g):
                slot = t % 2
                gbuf = g % 2
                if True:
                    def q_(sync, t=t, g=g, _qb=sq.n,
                           _fr=[wpool.get((t, g * NPG + i - 4),
                                          wpool.get((t - 1, g * NPG + i + 12), 0))
                                for i in range(NPG)]):
                        sync.wait_ge(sq.h, _qb)
                        for i in range(NPG):
                            n = g * NPG + i
                            sync.wait_ge(sP.h, _fr[i])
                            sync.dma_start(
                                B_sb[:, (n % 4) * LCORE:(n % 4 + 1) * LCORE],
                                bass.AP(bc_sp, n * LCORE, [[0, P], [1, LCORE]]),
                            ).then_inc(sq.h, 16)
                    blk.sync(q_); sq.n += 16 * NPG
                    bld = sq.n

                    def s_(s, t=t, g=g, gbuf=gbuf, slot=slot,
                           _af=(scv.get((t, (g - 1) * NPG), 0) if g >= 2
                                else (scv.get((t - 1, (g + 6) * NPG), 0) if t >= 1 else 0))):
                        s.wait_ge(sV.h, dready_a[t])
                        s.wait_ge(sV.h, _af)
                        for i in range(NPG):
                            n = g * NPG + i
                            s.activation(
                                a_slab[:, (gbuf * NPG + i) * LCORE:(gbuf * NPG + i + 1) * LCORE],
                                delta_f[:, slot * LCORE:(slot + 1) * LCORE],
                                AF.Exp, scale=A_sb[:, N * t + n:N * t + n + 1]).then_inc(sS.h, 1)
                    blk.scalar(s_); sS.n += NPG
                    aexp = sS.n

                    def p_(v, t=t, g=g, gbuf=gbuf, slot=slot, bld=bld,
                           _wf=(scv.get((t, (g - 1) * NPG), 0) if g >= 2
                                else (scv.get((t - 1, (g + 6) * NPG), 0) if t >= 1 else 0))):
                        v.wait_ge(sq.h, bld)
                        v.wait_ge(sV.h, dready_a[t])
                        v.wait_ge(sV.h, _wf)
                        for i in range(NPG):
                            n = g * NPG + i
                            v.tensor_mul(
                                w_slab[:, (gbuf * NPG + i) * LCORE:(gbuf * NPG + i + 1) * LCORE],
                                du_f[:, slot * LCORE:(slot + 1) * LCORE],
                                B_sb[:, (n % 4) * LCORE:(n % 4 + 1) * LCORE]).then_inc(sP.h, 1)
                    blk.vector(p_)
                    for i in range(NPG):
                        sP.n += 1
                        wpool[(t, g * NPG + i)] = sP.n

                    for i in range(NPG):
                        scv[(t, g * NPG + i)] = sV.n + i + 1

                    def v_(v, t=t, g=g, gbuf=gbuf, slot=slot, aexp=aexp):
                        v.wait_ge(sS.h, aexp)
                        for i in range(NPG):
                            n = g * NPG + i
                            v.wait_ge(sP.h, wpool[(t, n)])
                            if n < 8:
                                if t >= 2:
                                    v.wait_ge(sV.h, htv[(t - 2, n + 8)])
                            else:
                                if t >= 1:
                                    v.wait_ge(sV.h, htv[(t - 1, n - 8)])
                            v.tensor_tensor_scan(
                                h_slab[:, hslot(t, n):hslot(t, n) + LCORE],
                                a_slab[:, (gbuf * NPG + i) * LCORE:(gbuf * NPG + i + 1) * LCORE],
                                w_slab[:, (gbuf * NPG + i) * LCORE:(gbuf * NPG + i + 1) * LCORE],
                                0.0, ALU.mult, ALU.add).then_inc(sV.h, 1)
                    blk.vector(v_)
                    sV.n += NPG

            def emit_state(t):
                slot = t % 2
                scans_done_v[t] = sV.n
                with nc.Block() as blk:
                    scansd = scans_done_v[t]

                    def v_(v, t=t, scansd=scansd):
                        v.wait_ge(sV.h, scansd)
                        if t >= 1:
                            v.wait_ge(gq.h, std_done[t - 1])
                        s0 = (N * t) % HS
                        run1 = min(N, HS - s0)
                        c1 = v.tensor_copy(stout[:, 0:run1],
                                           bass.AP(h_slab, s0 * LCORE + LCORE - 1,
                                                   [[HS * LCORE, P], [LCORE, run1]]))
                        if run1 < N:
                            v.tensor_copy(stout[:, run1:N],
                                          bass.AP(h_slab, LCORE - 1,
                                                  [[HS * LCORE, P], [LCORE, N - run1]])).then_inc(sV.h, 1)
                        else:
                            c1.then_inc(sV.h, 1)
                    blk.vector(v_); sV.n += 1
                    fext = sV.n

                    def s_(s, t=t, slot=slot, fext=fext):
                        s.wait_ge(sV.h, fext)
                        for n in range(N):
                            act = s.activation(stout[:, N + n:N + n + 1],
                                               srel[:, slot * LCORE + LCORE - 1:slot * LCORE + LCORE],
                                               AF.Exp, scale=A_sb[:, N * t + n:N * t + n + 1])
                            if n == N - 1:
                                act.then_inc(sS.h, 1)
                    blk.scalar(s_); sS.n += 1
                    ppd = sS.n

                    def g_(g, t=t, fext=fext, ppd=ppd, _gb=gq.n):
                        g.wait_ge(gq.h, _gb)
                        g.wait_ge(sV.h, fext)
                        g.wait_ge(sS.h, ppd)
                        g.dma_start(ag_in[t].ap(), stout[:]).then_inc(gq.h, 16)
                    blk.gpsimd(g_); gq.n += 16
                    std_done[t] = gq.n

                    cc_step = 16 if n_cores == 1 else 1

                    def g2_(g, t=t, cc_step=cc_step, _cb=sC.n):
                        g.wait_ge(sC.h, _cb)
                        g.wait_ge(gq.h, std_done[t])
                        if n_cores == 1:
                            g.dma_start(ag_out[t].ap()[0:P, :], ag_in[t].ap()).then_inc(sC.h, cc_step)
                        else:
                            g.collective_compute(
                                "AllGather", ALU.bypass,
                                ins=[ag_in[t].ap()], outs=[ag_out[t].ap()],
                                replica_groups=[list(range(n_cores))],
                            ).then_inc(sC.h, cc_step)
                    blk.gpsimd(g2_); sC.n += cc_step
                    ag_done[t] = sC.n

            def emit_ztail(t):
                slot = t % 2
                with nc.Block() as blk:
                    def g_(g, t=t, _gb=gq.n):
                        g.wait_ge(gq.h, _gb)
                        if t >= 1:
                            g.wait_ge(sP.h, zv_done[t - 1])
                        g.dma_start(ssk_tile[:], ssk_sp.ap()[P * t:P * (t + 1), :]).then_inc(gq.h, 16)
                    blk.gpsimd(g_); gq.n += 16
                    sskl = gq.n
                    sskl_done[t] = sskl
                    _pb2 = sP.n

                    def p_(p, t=t, _pb2=_pb2):
                        p.wait_ge(sP.h, yred_done[(t, 1)])
                        p.tensor_add(y_f[:], y_f[:], y_f2[:]).then_inc(sP.h, 1)
                    blk.gpsimd(p_); sP.n += 1
                    yadd = sP.n

                    def v_(v, t=t, slot=slot, sskl=sskl, yadd=yadd):
                        v.wait_ge(sP.h, yadd)
                        vf(v, v.scalar_tensor_tensor(zt_bf[:],
                                                     uc_ring[:, slot * LCORE:(slot + 1) * LCORE],
                                                     wd_sb[:, t:t + 1], y_f[:],
                                                     ALU.mult, ALU.add))
                        v.wait_ge(gq.h, sskl)
                        v.tensor_mul(z_bf[:], zt_bf[:], ssk_tile[:]).then_inc(sP.h, 1)
                    blk.vector(v_); sP.n += 1
                    zv_done[t] = sP.n

                    def g2_(g, t=t, _gb=gq.n):
                        g.wait_ge(gq.h, _gb)
                        g.wait_ge(sP.h, zv_done[t])
                        g.dma_start(z_sp.ap()[P * t:P * (t + 1), :], z_bf[:]).then_inc(gq.h, 16)
                    blk.gpsimd(g2_); gq.n += 16

            for t in range(TE):
                emit_delta(t)
                with nc.Block() as gblk:
                    emit_scan_group(gblk, t, 0)
                    emit_scan_group(gblk, t, 1)
                    if t >= 1:
                        emit_combine(gblk, t - 1)
                    for g in range(NGR):
                        if t >= 1:
                            emit_corr_group(gblk, t - 1, g)
                        if g + 2 < NGR:
                            emit_scan_group(gblk, t, g + 2)
                emit_state(t)
                if t >= 1:
                    emit_ztail(t - 1)
            with nc.Block() as gblk:
                emit_combine(gblk, TE - 1)
                for g in range(NGR):
                    emit_corr_group(gblk, TE - 1, g)
            emit_ztail(TE - 1)

        barrier()
        sfree("ssk_tile", "uc_ring", "dpre_b")

        o1b = es.enter_context(nc.sbuf_tensor("o1b", [P, TD * LCORE], BF16))
        es2 = ExitStack()
        z_all = es2.enter_context(nc.sbuf_tensor("z_all", [P, TE * LCORE], BF16))
        wostg = es2.enter_context(nc.sbuf_tensor("wostg", [P, E], F32))
        wostgb = es2.enter_context(nc.sbuf_tensor("wostgb", [P, 2 * E], BF16))
        mix_bf = es2.enter_context(nc.sbuf_tensor("mix_bf", [P, TD * LCORE], BF16))
        mix2 = es2.enter_context(nc.sbuf_tensor("mix2", [P, LCORE], BF16))
        xT_f = es2.enter_context(nc.sbuf_tensor("xT_f", [P, LCORE], F32))
        lrow = es2.enter_context(nc.sbuf_tensor("lrow", [1, 4 * LCORE], F32))
        r2b = es2.enter_context(nc.sbuf_tensor("r2b", [P, LCORE], BF16))
        mr2b = es2.enter_context(nc.sbuf_tensor("mr2b", [P, LCORE], BF16))
        t1b = es2.enter_context(nc.sbuf_tensor("t1b", [P, LCORE], BF16))
        o1f = es2.enter_context(nc.sbuf_tensor("o1f", [P, LCORE], F32))
        o1sq = es2.enter_context(nc.sbuf_tensor("o1sq", [P, LCORE], BF16))
        psMW = es2.enter_context(nc.psum_tensor("psMW", [P, 1024], F32))
        psM = es2.enter_context(nc.psum_tensor("psM", [P, 512], F32))
        psM2 = es2.enter_context(nc.psum_tensor("psM2", [P, 512], F32))
        psS1 = es2.enter_context(nc.psum_tensor("psS1", [1, LCORE], F32))
        psS2 = es2.enter_context(nc.psum_tensor("psS2", [1, LCORE], F32))
        with es2:
            with nc.Block() as blk:
                def g_(g, _gb=gq.n):
                    g.wait_ge(gq.h, _gb)
                    for t in range(TE):
                        g.dma_start(z_all[:, t * LCORE:(t + 1) * LCORE],
                                    z_sp.ap()[P * t:P * (t + 1), :]).then_inc(gq.h, 16)
                blk.gpsimd(g_); gq.n += 16 * TE
                zld = gq.n

            stats_prev = sT.n
            t0base = sT.n
            mo_arr = {}
            mev_arr = {}
            for dt in range(TD):
                buf = dt % 2
                with nc.Block() as blk:
                    tprev = mo_arr.get(dt - 2, t0base)

                    wc_prev = sV.n

                    def q_(sync, dt=dt, buf=buf, tprev=tprev, wc_prev=wc_prev, _qb=sq.n):
                        sync.wait_ge(sq.h, _qb)
                        sync.wait_ge(sT.h, tprev)
                        if dt >= 1:
                            sync.wait_ge(sV.h, wc_prev)
                        for pc in range(2):
                            sync.dma_start(
                                wostg[:, pc * 1024:(pc + 1) * 1024].rearrange(
                                    "p (a c) -> p a c", c=P),
                                WoutT[1024 * pc:1024 * (pc + 1),
                                      P * dt:P * (dt + 1)].rearrange("(a p) c -> p a c", p=P),
                            ).then_inc(sq.h, 16)
                    blk.sync(q_); sq.n += 32
                    ldo = sq.n

                    def qg_(g, dt=dt, buf=buf, tprev=tprev, wc_prev=wc_prev, _gb=gq.n):
                        g.wait_ge(gq.h, _gb)
                        g.wait_ge(sT.h, tprev)
                        if dt >= 1:
                            g.wait_ge(sV.h, wc_prev)
                        for pc in range(2, 4):
                            g.dma_start(
                                wostg[:, pc * 1024:(pc + 1) * 1024].rearrange(
                                    "p (a c) -> p a c", c=P),
                                WoutT[1024 * pc:1024 * (pc + 1),
                                      P * dt:P * (dt + 1)].rearrange("(a p) c -> p a c", p=P),
                            ).then_inc(gq.h, 16)
                    blk.gpsimd(qg_); gq.n += 32
                    ldo_g = gq.n

                    def v0_(v, buf=buf, ldo=ldo, ldo_g=ldo_g):
                        v.wait_ge(sq.h, ldo)
                        v.wait_ge(gq.h, ldo_g)
                        for hh in range(2):
                            v.tensor_copy(wostgb[:, buf * E + hh * (E // 2):
                                                 buf * E + (hh + 1) * (E // 2)],
                                          wostg[:, hh * (E // 2):(hh + 1) * (E // 2)]
                                          ).then_inc(sV.h, 1)
                    blk.vector(v0_); sV.n += 2
                    woc = sV.n

                    def t_(t_eng, dt=dt, buf=buf, woc=woc,
                           _pm=mev_arr.get(dt - 1, 0)):
                        t_eng.wait_ge(sV.h, woc)
                        if dt >= 1:
                            t_eng.wait_ge(sS.h, _pm)
                        if dt == 0:
                            t_eng.wait_ge(gq.h, zld)
                        for et in range(TE):
                            for lc in range(2):
                                mm = t_eng.matmul(
                                    psMW[:, 512 * lc:512 * (lc + 1)],
                                    wostgb[:, buf * E + P * et:buf * E + P * (et + 1)],
                                    z_all[:, et * LCORE + 512 * lc:et * LCORE + 512 * (lc + 1)],
                                    start=(et == 0), stop=(et == TE - 1),
                                    skip_group_check=True)
                                if et == TE - 1 and lc == 1:
                                    mm.then_inc(sT.h, 1)
                    blk.tensor(t_); sT.n += 1
                    momm = sT.n
                    mo_arr[dt] = momm

                    def s2_(s, dt=dt, momm=momm, stats_prev=stats_prev):
                        s.wait_ge(sT.h, stats_prev)
                        s.wait_ge(sT.h, momm)
                        for lc in range(2):
                            s.copy(mix_bf[:, dt * LCORE + 512 * lc:dt * LCORE + 512 * (lc + 1)],
                                   psMW[:, 512 * lc:512 * (lc + 1)])
                            act = s.activation(mix2[:, 512 * lc:512 * (lc + 1)],
                                               psMW[:, 512 * lc:512 * (lc + 1)], AF.Square)
                            if lc == 1:
                                act.then_inc(sS.h, 1)
                    blk.scalar(s2_); sS.n += 1
                    mev = sS.n
                    mev_arr[dt] = mev

                    def t2_(t_eng, dt=dt, mev=mev):
                        t_eng.wait_ge(sS.h, mev)
                        for lc in range(2):
                            t_eng.matmul(psS1[:, 512 * lc:512 * (lc + 1)], onescol_b[:],
                                         mix_bf[:, dt * LCORE + 512 * lc:dt * LCORE + 512 * (lc + 1)],
                                         start=(dt == 0), stop=(dt == TD - 1), skip_group_check=True)
                            mm = t_eng.matmul(psS2[:, 512 * lc:512 * (lc + 1)], onescol_b[:],
                                              mix2[:, 512 * lc:512 * (lc + 1)],
                                              start=(dt == 0), stop=(dt == TD - 1), skip_group_check=True)
                            if lc == 1:
                                mm.then_inc(sT.h, 1)
                    blk.tensor(t2_); sT.n += 1
                    stats_prev = sT.n

            with nc.Block() as blk:
                smm = sT.n

                def v_(v, smm=smm):
                    v.wait_ge(sT.h, smm)
                    vf(v, v.tensor_scalar_mul(lrow[0:1, 0:LCORE], psS1[:], 1.0 / D))
                    vf(v, v.tensor_scalar_mul(lrow[0:1, LCORE:2 * LCORE], psS2[:], 1.0 / D))
                    vf(v, v.tensor_mul(lrow[0:1, 2 * LCORE:3 * LCORE], lrow[0:1, 0:LCORE],
                                       lrow[0:1, 0:LCORE]))
                    vf(v, v.tensor_sub(lrow[0:1, LCORE:2 * LCORE], lrow[0:1, LCORE:2 * LCORE],
                                       lrow[0:1, 2 * LCORE:3 * LCORE]))
                    v.tensor_scalar_add(lrow[0:1, LCORE:2 * LCORE],
                                        lrow[0:1, LCORE:2 * LCORE], EPS).then_inc(sV.h, 1)
                blk.vector(v_); sV.n += 1
                vst = sV.n

                def s_(s, vst=vst):
                    s.wait_ge(sV.h, vst)
                    sf(s, s.activation(lrow[0:1, 2 * LCORE:3 * LCORE], lrow[0:1, LCORE:2 * LCORE],
                                       AF.Ln))
                    s.activation(lrow[0:1, 2 * LCORE:3 * LCORE], lrow[0:1, 2 * LCORE:3 * LCORE],
                                 AF.Exp, scale=-0.5).then_inc(sS.h, 1)
                blk.scalar(s_); sS.n += 1
                srr = sS.n

                def v2_(v, srr=srr):
                    v.wait_ge(sS.h, srr)
                    v.tensor_mul(lrow[0:1, 3 * LCORE:4 * LCORE], lrow[0:1, 0:LCORE],
                                 lrow[0:1, 2 * LCORE:3 * LCORE]).then_inc(sV.h, 1)
                blk.vector(v2_); sV.n += 1
                vmr = sV.n

                def t_(t_eng, vmr=vmr):
                    t_eng.wait_ge(sV.h, vmr)
                    for lc in range(2):
                        psr = psM[:] if lc == 0 else psMW[:, 0:512]
                        psm_ = psM2[:] if lc == 0 else psMW[:, 512:1024]
                        t_eng.matmul(psr, ones_f[0:1, :],
                                     lrow[0:1, 2 * LCORE + 512 * lc:2 * LCORE + 512 * (lc + 1)],
                                     start=True, stop=True)
                        t_eng.matmul(psm_, ones_f[0:1, :],
                                     lrow[0:1, 3 * LCORE + 512 * lc:3 * LCORE + 512 * (lc + 1)],
                                     start=True, stop=True).then_inc(sT.h, 1)
                blk.tensor(t_); sT.n += 2
                bmm2 = sT.n

                def s2_(s, bmm2=bmm2):
                    s.wait_ge(sT.h, bmm2)
                    s.copy(r2b[:, 0:512], psM[:])
                    s.copy(mr2b[:, 0:512], psM2[:])
                    s.copy(r2b[:, 512:1024], psMW[:, 0:512])
                    s.copy(mr2b[:, 512:1024], psMW[:, 512:1024]).then_inc(sS.h, 1)
                blk.scalar(s2_); sS.n += 1
                bcd = sS.n

            o1v_done = [0] * TD
            for dt in range(TD):
                with nc.Block() as blk:
                    def g_(g, dt=dt, _gb=gq.n):
                        g.wait_ge(gq.h, _gb)
                        if dt >= 1:
                            g.wait_ge(sV.h, o1v_done[dt - 1])
                        g.dma_start(xT_f[:], xt_sp.ap()[P * dt:P * (dt + 1), :]).then_inc(gq.h, 16)
                    blk.gpsimd(g_); gq.n += 16
                    xl = gq.n

                    def v_(v, dt=dt, xl=xl, bcd=bcd):
                        v.wait_ge(sS.h, bcd)
                        v.wait_ge(gq.h, xl)
                        vf(v, v.tensor_mul(t1b[:], mix_bf[:, dt * LCORE:(dt + 1) * LCORE], r2b[:]))
                        vf(v, v.tensor_sub(o1f[:], t1b[:], mr2b[:]))
                        vf(v, v.tensor_add(o1f[:], o1f[:], xT_f[:]))
                        v.tensor_copy(o1b[:, dt * LCORE:(dt + 1) * LCORE], o1f[:]).then_inc(sV.h, 1)
                    blk.vector(v_); sV.n += 1
                    o1v_done[dt] = sV.n

                    def g2_(g, dt=dt, _gb=gq.n):
                        g.wait_ge(gq.h, _gb)
                        g.wait_ge(sV.h, o1v_done[dt])
                        g.dma_start(o1_sp.ap()[P * dt:P * (dt + 1), :], o1f[:]).then_inc(gq.h, 16)
                    blk.gpsimd(g2_); gq.n += 16

            st2_arr = [0] * TD
            for dt in range(TD):
                with nc.Block() as blk:
                    def s_(s, dt=dt):
                        s.wait_ge(sV.h, o1v_done[dt])
                        if dt >= 1:
                            s.wait_ge(sT.h, st2_arr[dt - 1])
                        for lc in range(2):
                            act = s.activation(
                                o1sq[:, 512 * lc:512 * (lc + 1)],
                                o1b[:, dt * LCORE + 512 * lc:dt * LCORE + 512 * (lc + 1)],
                                AF.Square)
                            if lc == 1:
                                act.then_inc(sS.h, 1)
                    blk.scalar(s_); sS.n += 1
                    sq2d = sS.n

                    def t_(t_eng, dt=dt, sq2d=sq2d):
                        t_eng.wait_ge(sS.h, sq2d)
                        for lc in range(2):
                            t_eng.matmul(psS1[:, 512 * lc:512 * (lc + 1)], onescol_b[:],
                                         o1b[:, dt * LCORE + 512 * lc:dt * LCORE + 512 * (lc + 1)],
                                         start=(dt == 0), stop=(dt == TD - 1),
                                         skip_group_check=True)
                            mm = t_eng.matmul(psS2[:, 512 * lc:512 * (lc + 1)], onescol_b[:],
                                              o1sq[:, 512 * lc:512 * (lc + 1)],
                                              start=(dt == 0), stop=(dt == TD - 1),
                                              skip_group_check=True)
                            if lc == 1:
                                mm.then_inc(sT.h, 1)
                    blk.tensor(t_); sT.n += 1
                    st2_arr[dt] = sT.n

            with nc.Block() as blk:
                smm3 = sT.n

                def v_(v, smm3=smm3):
                    v.wait_ge(sT.h, smm3)
                    vf(v, v.tensor_scalar_mul(lrow[0:1, 0:LCORE], psS1[:], 1.0 / D))
                    vf(v, v.tensor_scalar_mul(lrow[0:1, LCORE:2 * LCORE], psS2[:], 1.0 / D))
                    vf(v, v.tensor_mul(lrow[0:1, 2 * LCORE:3 * LCORE], lrow[0:1, 0:LCORE],
                                       lrow[0:1, 0:LCORE]))
                    vf(v, v.tensor_sub(lrow[0:1, LCORE:2 * LCORE], lrow[0:1, LCORE:2 * LCORE],
                                       lrow[0:1, 2 * LCORE:3 * LCORE]))
                    v.tensor_scalar_add(lrow[0:1, LCORE:2 * LCORE],
                                        lrow[0:1, LCORE:2 * LCORE], EPS).then_inc(sV.h, 1)
                blk.vector(v_); sV.n += 1
                vst3 = sV.n

                def s_(s, vst3=vst3):
                    s.wait_ge(sV.h, vst3)
                    sf(s, s.activation(lrow[0:1, 2 * LCORE:3 * LCORE], lrow[0:1, LCORE:2 * LCORE],
                                       AF.Ln))
                    s.activation(lrow[0:1, 2 * LCORE:3 * LCORE], lrow[0:1, 2 * LCORE:3 * LCORE],
                                 AF.Exp, scale=-0.5).then_inc(sS.h, 1)
                blk.scalar(s_); sS.n += 1
                srr3 = sS.n

                def v2_(v, srr3=srr3):
                    v.wait_ge(sS.h, srr3)
                    v.tensor_mul(lrow[0:1, 3 * LCORE:4 * LCORE], lrow[0:1, 0:LCORE],
                                 lrow[0:1, 2 * LCORE:3 * LCORE]).then_inc(sV.h, 1)
                blk.vector(v2_); sV.n += 1
                vmr3 = sV.n

                def t_(t_eng, vmr3=vmr3):
                    t_eng.wait_ge(sV.h, vmr3)
                    for lc in range(2):
                        psr = psM[:] if lc == 0 else psMW[:, 0:512]
                        psm_ = psM2[:] if lc == 0 else psMW[:, 512:1024]
                        t_eng.matmul(psr, ones_f[0:1, :],
                                     lrow[0:1, 2 * LCORE + 512 * lc:2 * LCORE + 512 * (lc + 1)],
                                     start=True, stop=True)
                        t_eng.matmul(psm_, ones_f[0:1, :],
                                     lrow[0:1, 3 * LCORE + 512 * lc:3 * LCORE + 512 * (lc + 1)],
                                     start=True, stop=True).then_inc(sT.h, 1)
                blk.tensor(t_); sT.n += 2
                bmm4 = sT.n

                def s2_(s, bmm4=bmm4):
                    s.wait_ge(sT.h, bmm4)
                    s.copy(r2b[:, 0:512], psM[:])
                    s.copy(mr2b[:, 0:512], psM2[:])
                    s.copy(r2b[:, 512:1024], psMW[:, 0:512])
                    s.copy(mr2b[:, 512:1024], psMW[:, 512:1024]).then_inc(sS.h, 1)
                blk.scalar(s2_); sS.n += 1
                bcd3 = sS.n

            for dt in range(TD):
                with nc.Block() as blk:
                    vprev3 = sV.n

                    def v_(v, dt=dt, bcd3=bcd3, vprev3=vprev3, smm3=st2_arr[TD - 1]):
                        v.wait_ge(sS.h, bcd3)
                        v.wait_ge(sT.h, smm3)
                        v.wait_ge(sV.h, vprev3)
                        vf(v, v.tensor_mul(t1b[:], o1b[:, dt * LCORE:(dt + 1) * LCORE], r2b[:]))
                        v.tensor_sub(o1b[:, dt * LCORE:(dt + 1) * LCORE], t1b[:],
                                     mr2b[:]).then_inc(sV.h, 1)
                    blk.vector(v_); sV.n += 1

        barrier()

        es2 = ExitStack()
        gustg = es2.enter_context(nc.sbuf_tensor("gustg", [P, 4 * 1024], F32))
        dstg = es2.enter_context(nc.sbuf_tensor("dstg", [P, E], F32))
        dstgb = es2.enter_context(nc.sbuf_tensor("dstgb", [P, 2 * INTER], BF16))
        sg_f = es2.enter_context(nc.sbuf_tensor("sg_f", [P, 1024], F32))
        hffn = es2.enter_context(nc.sbuf_tensor("hffn", [P, 64 * 512], BF16))
        ffn_bf = es2.enter_context(nc.sbuf_tensor("ffn_bf", [P, TD * 512], BF16))
        ffn2 = es2.enter_context(nc.sbuf_tensor("ffn2", [P, 512], BF16))
        lrow2 = es2.enter_context(nc.sbuf_tensor("lrow2", [1, 4 * 512], F32))
        r3b = es2.enter_context(nc.sbuf_tensor("r3b", [P, 512], BF16))
        mr3b = es2.enter_context(nc.sbuf_tensor("mr3b", [P, 512], BF16))
        t1c = es2.enter_context(nc.sbuf_tensor("t1c", [P, 512], BF16))
        o1l = es2.enter_context(nc.sbuf_tensor("o1l", [P, 512], F32))
        o2f = es2.enter_context(nc.sbuf_tensor("o2f", [P, 512], F32))
        ostg = es2.enter_context(nc.sbuf_tensor("ostg", [P, 2 * P], F32))
        psG = es2.enter_context(nc.psum_tensor("psG", [P, 512], F32))
        psUp = es2.enter_context(nc.psum_tensor("psUp", [P, 512], F32))
        psDn = es2.enter_context(nc.psum_tensor("psDn", [P, 512], F32))
        psDn2 = es2.enter_context(nc.psum_tensor("psDn2", [P, 512], F32))
        psT1 = es2.enter_context(nc.psum_tensor("psT1", [1, 512], F32))
        psT2 = es2.enter_context(nc.psum_tensor("psT2", [1, 512], F32))
        psO = es2.enter_context(nc.psum_tensor("psO", [P, P], F32))
        with es2:
            for lh in range(2):
                l0 = 512 * lh
                ggmm = {}
                gmul = {}
                gvc = {}
                tbase = sT.n
                vbase = sV.n
                sbase = sS.n
                _bc = None
                for gi in range(64):
                    pr = gi // 2
                    j = gi % 2
                    pb = pr % 2
                    pg = psG if gi % 2 == 0 else psDn
                    pu = psUp if gi % 2 == 0 else psDn2
                    if gi % 8 == 0:
                        if _bc is not None:
                            _bc.__exit__(None, None, None)
                        _bc = nc.Block()
                        blk = _bc.__enter__()
                    if True:
                        if j == 0:
                            tfree = ggmm.get(gi - 3, tbase)
                            for bb in range(2):
                                stg = gustg if bb == 0 else dstg

                                def q_(eng, pr=pr, bb=bb, tfree=tfree, stg=stg,
                                       _qh=(sq.h if bb == 0 else gq.h),
                                       _qb=(sq.n if bb == 0 else gq.n),
                                       _vc=gvc.get((pr - 1, bb), vbase)):
                                    eng.wait_ge(_qh, _qb)
                                    eng.wait_ge(sT.h, tfree)
                                    eng.wait_ge(sV.h, _vc)
                                    col0 = 256 * pr + INTER * bb
                                    for hh in range(2):
                                        for pc in range(2):
                                            eng.dma_start(
                                                stg[:, hh * 2048 + pc * 1024:
                                                    hh * 2048 + (pc + 1) * 1024]
                                                .rearrange("p (a c) -> p a c", c=256),
                                                guT[1024 * hh + 512 * pc:
                                                    1024 * hh + 512 * (pc + 1),
                                                    col0:col0 + 256].rearrange(
                                                        "(a p) c -> p a c", p=P),
                                            ).then_inc(_qh, 16)
                                if bb == 0:
                                    blk.sync(q_); sq.n += 64
                                    ldg = ('q', sq.n)
                                else:
                                    blk.gpsimd(q_); gq.n += 64
                                    ldg = ('g', gq.n)

                                def v0_(v, pb=pb, bb=bb, ldg=ldg, stg=stg):
                                    v.wait_ge(sq.h if ldg[0] == 'q' else gq.h, ldg[1])
                                    for hh in range(2):
                                        v.tensor_copy(
                                            dstgb[:, pb * 8192 + bb * 4096 + hh * 2048:
                                                  pb * 8192 + bb * 4096 + (hh + 1) * 2048],
                                            stg[:, hh * 2048:(hh + 1) * 2048]
                                            ).then_inc(sV.h, 1)
                                blk.vector(v0_); sV.n += 2
                                gvc[(pr, bb)] = sV.n
                            gc_pair = sV.n

                        def t_(t_eng, gi=gi, pb=pb, j=j, l0=l0, gc=gc_pair, pg=pg, pu=pu,
                               _w2=gmul.get(gi - 2, vbase), _sb=sbase):
                            t_eng.wait_ge(sV.h, gc)
                            t_eng.wait_ge(sV.h, _w2)
                            if gi < 2:
                                t_eng.wait_ge(sS.h, _sb)
                            for dt in range(TD):
                                off = pb * 8192 + (dt // 8) * 2048 + (dt % 8) * 256 + j * P
                                mm = t_eng.matmul(
                                    pg[:],
                                    dstgb[:, off:off + P],
                                    o1b[:, dt * LCORE + l0:dt * LCORE + l0 + 512],
                                    start=(dt == 0), stop=(dt == TD - 1))
                                if dt == TD - 1:
                                    mm.then_inc(sT.h, 1)
                            for dt in range(TD):
                                off = pb * 8192 + 4096 + (dt // 8) * 2048 + (dt % 8) * 256 + j * P
                                mm = t_eng.matmul(
                                    pu[:],
                                    dstgb[:, off:off + P],
                                    o1b[:, dt * LCORE + l0:dt * LCORE + l0 + 512],
                                    start=(dt == 0), stop=(dt == TD - 1))
                                if dt == TD - 1:
                                    mm.then_inc(sT.h, 1)
                        blk.tensor(t_); sT.n += 2
                        gmm = sT.n
                        ggmm[gi] = gmm

                        def s2_(s, gi=gi, gmm=gmm, pg=pg):
                            s.wait_ge(sT.h, gmm - 1)
                            s.activation(sg_f[:, (gi % 2) * 512:(gi % 2 + 1) * 512],
                                         pg[:], AF.Silu).then_inc(sS.h, 1)
                        blk.scalar(s2_); sS.n += 1
                        sgd = sS.n

                        def v_(v, gi=gi, gmm=gmm, sgd=sgd, pu=pu):
                            v.wait_ge(sT.h, gmm)
                            v.wait_ge(sS.h, sgd)
                            v.tensor_mul(hffn[:, gi * 512:(gi + 1) * 512],
                                         sg_f[:, (gi % 2) * 512:(gi % 2 + 1) * 512],
                                         pu[:]).then_inc(sV.h, 1)
                        blk.vector(v_); sV.n += 1
                        gmul[gi] = sV.n

                _bc.__exit__(None, None, None)
                _bc = None
                hffn_done = sV.n
                for dt in range(TD):
                    buf = dt % 2
                    if dt % 8 == 0:
                        if _bc is not None:
                            _bc.__exit__(None, None, None)
                        _bc = nc.Block()
                        blk = _bc.__enter__()
                    if True:
                        if dt == 0:
                            tbase2 = sT.n
                            sbase2 = sS.n
                            dmm_arr = {}
                            sevac = {}
                            dvc = {}
                        tfree2 = dmm_arr.get(dt - 2, tbase2)

                        for bb in range(2):
                            stg = dstg if bb == 0 else gustg

                            def q_(eng, dt=dt, bb=bb, tfree2=tfree2, stg=stg,
                                   _qh=(sq.h if bb == 0 else gq.h),
                                   _qb=(sq.n if bb == 0 else gq.n),
                                   _vc=dvc.get((dt - 1, bb), 0) if dt >= 1 else None):
                                eng.wait_ge(_qh, _qb)
                                eng.wait_ge(sT.h, tfree2)
                                if _vc is not None:
                                    eng.wait_ge(sV.h, _vc)
                                for hh2 in range(2):
                                    hh = bb * 2 + hh2
                                    for pc in range(2):
                                        eng.dma_start(
                                            stg[:, hh2 * 2048 + pc * 1024:
                                                hh2 * 2048 + (pc + 1) * 1024].rearrange(
                                                "p (a c) -> p a c", c=P),
                                            dwT[2048 * hh + 1024 * pc:2048 * hh + 1024 * (pc + 1),
                                                P * dt:P * (dt + 1)].rearrange("(a p) c -> p a c", p=P),
                                        ).then_inc(_qh, 16)
                            if bb == 0:
                                blk.sync(q_); sq.n += 64
                                ldd2 = ('q', sq.n)
                            else:
                                blk.gpsimd(q_); gq.n += 64
                                ldd2 = ('g', gq.n)

                            def v0_(v, buf=buf, bb=bb, ldd2=ldd2, stg=stg):
                                v.wait_ge(sq.h if ldd2[0] == 'q' else gq.h, ldd2[1])
                                for hh2 in range(2):
                                    hh = bb * 2 + hh2
                                    v.tensor_copy(dstgb[:, buf * INTER + hh * 2048:
                                                        buf * INTER + (hh + 1) * 2048],
                                                  stg[:, hh2 * 2048:(hh2 + 1) * 2048]
                                                  ).then_inc(sV.h, 1)
                            blk.vector(v0_); sV.n += 2
                            dvc[(dt, bb)] = sV.n
                        dc = sV.n

                        def t_(t_eng, dt=dt, buf=buf, dc=dc, hffn_done=hffn_done,
                               _pf=None, _sb2=sbase2):
                            t_eng.wait_ge(sV.h, dc)
                            t_eng.wait_ge(sV.h, hffn_done)
                            if dt < 2:
                                t_eng.wait_ge(sS.h, _sb2)
                            else:
                                t_eng.wait_ge(sS.h, sevac[dt - 2])
                            ps = psDn if dt % 2 == 0 else psDn2
                            for ii in range(64):
                                mm = t_eng.matmul(ps[:],
                                                  dstgb[:, buf * INTER + P * ii:
                                                        buf * INTER + P * (ii + 1)],
                                                  hffn[:, ii * 512:(ii + 1) * 512],
                                                  start=(ii == 0), stop=(ii == 63))
                                if ii == 63:
                                    mm.then_inc(sT.h, 1)
                        blk.tensor(t_); sT.n += 1
                        dmm2 = sT.n
                        dmm_arr[dt] = dmm2

                        def s2_(s, dt=dt, dmm2=dmm2):
                            ps = psDn if dt % 2 == 0 else psDn2
                            s.wait_ge(sT.h, dmm2)
                            if dt >= 1:
                                s.wait_ge(sT.h, sT.n)
                            s.copy(ffn_bf[:, dt * 512:(dt + 1) * 512], ps[:])
                            s.activation(ffn2[:], ps[:], AF.Square).then_inc(sS.h, 1)
                        blk.scalar(s2_); sS.n += 1
                        fev = sS.n
                        sevac[dt] = fev

                        def t2_(t_eng, dt=dt, fev=fev):
                            t_eng.wait_ge(sS.h, fev)
                            t_eng.matmul(psT1[:], onescol_b[:], ffn_bf[:, dt * 512:(dt + 1) * 512],
                                         start=(dt == 0), stop=(dt == TD - 1), skip_group_check=True)
                            t_eng.matmul(psT2[:], onescol_b[:], ffn2[:],
                                         start=(dt == 0), stop=(dt == TD - 1),
                                         skip_group_check=True).then_inc(sT.h, 1)
                        blk.tensor(t2_); sT.n += 1

                _bc.__exit__(None, None, None)
                _bc = None
                with nc.Block() as blk:
                    smm2 = sT.n

                    def v_(v, smm2=smm2):
                        v.wait_ge(sT.h, smm2)
                        vf(v, v.tensor_scalar_mul(lrow2[0:1, 0:512], psT1[:], 1.0 / D))
                        vf(v, v.tensor_scalar_mul(lrow2[0:1, 512:1024], psT2[:], 1.0 / D))
                        vf(v, v.tensor_mul(lrow2[0:1, 1024:1536], lrow2[0:1, 0:512],
                                           lrow2[0:1, 0:512]))
                        vf(v, v.tensor_sub(lrow2[0:1, 512:1024], lrow2[0:1, 512:1024],
                                           lrow2[0:1, 1024:1536]))
                        v.tensor_scalar_add(lrow2[0:1, 512:1024], lrow2[0:1, 512:1024],
                                            EPS).then_inc(sV.h, 1)
                    blk.vector(v_); sV.n += 1
                    vst2 = sV.n

                    def s_(s, vst2=vst2):
                        s.wait_ge(sV.h, vst2)
                        sf(s, s.activation(lrow2[0:1, 1024:1536], lrow2[0:1, 512:1024], AF.Ln))
                        s.activation(lrow2[0:1, 1024:1536], lrow2[0:1, 1024:1536],
                                     AF.Exp, scale=-0.5).then_inc(sS.h, 1)
                    blk.scalar(s_); sS.n += 1
                    sr2 = sS.n

                    def v2_(v, sr2=sr2):
                        v.wait_ge(sS.h, sr2)
                        v.tensor_mul(lrow2[0:1, 1536:2048], lrow2[0:1, 0:512],
                                     lrow2[0:1, 1024:1536]).then_inc(sV.h, 1)
                    blk.vector(v2_); sV.n += 1
                    vmr2 = sV.n

                    def t_(t_eng, vmr2=vmr2):
                        t_eng.wait_ge(sV.h, vmr2)
                        t_eng.matmul(psDn[:], ones_f[0:1, :], lrow2[0:1, 1024:1536],
                                     start=True, stop=True)
                        t_eng.matmul(psDn2[:], ones_f[0:1, :], lrow2[0:1, 1536:2048],
                                     start=True, stop=True).then_inc(sT.h, 1)
                    blk.tensor(t_); sT.n += 1
                    bmm3 = sT.n

                    def s2_(s, bmm3=bmm3):
                        s.wait_ge(sT.h, bmm3)
                        s.copy(r3b[:], psDn[:])
                        s.copy(mr3b[:], psDn2[:]).then_inc(sS.h, 1)
                    blk.scalar(s2_); sS.n += 1
                    bcd2 = sS.n

                for dt in range(TD):
                    if dt % 8 == 0:
                        if _bc is not None:
                            _bc.__exit__(None, None, None)
                        _bc = nc.Block()
                        blk = _bc.__enter__()
                    if True:
                        o2prev = sT.n

                        def g_(g, dt=dt, lh=lh, o2prev=o2prev, _gb=gq.n):
                            g.wait_ge(gq.h, _gb)
                            g.wait_ge(sT.h, o2prev)
                            g.dma_start(o1l[:], o1_sp.ap()[P * dt:P * (dt + 1),
                                                           512 * lh:512 * (lh + 1)]).then_inc(gq.h, 16)
                        blk.gpsimd(g_); gq.n += 16
                        o1ld = gq.n

                        def v_(v, dt=dt, o1ld=o1ld, bcd2=bcd2, o2prev=o2prev):
                            v.wait_ge(sS.h, bcd2)
                            v.wait_ge(sT.h, o2prev)
                            vf(v, v.tensor_mul(t1c[:], ffn_bf[:, dt * 512:(dt + 1) * 512], r3b[:]))
                            vf(v, v.tensor_sub(o2f[:], t1c[:], mr3b[:]))
                            v.wait_ge(gq.h, o1ld)
                            vf(v, v.tensor_add(o2f[:], o2f[:], o1l[:]))
                            v.nop().then_inc(sV.h, 1)
                        blk.vector(v_); sV.n += 1
                        o2d = sV.n

                        s_out_base = sS.n
                        g_out_base = gq.n

                        def t_(t_eng, dt=dt, o2d=o2d, s_out_base=s_out_base):
                            t_eng.wait_ge(sV.h, o2d)
                            for lb in range(4):
                                if lb >= 1:
                                    t_eng.wait_ge(sS.h, s_out_base + lb)
                                t_eng.transpose(psO[:], o2f[:, P * lb:P * (lb + 1)],
                                                ident[:]).then_inc(sT.h, 1)
                        blk.tensor(t_); sT.n += 4
                        otr = sT.n

                        def s_(s, otr=otr, g_out_base=g_out_base):
                            s.wait_ge(gq.h, g_out_base)
                            for lb in range(2):
                                s.wait_ge(sT.h, otr - 4 + lb + 1)
                                s.copy(ostg[:, lb * P:(lb + 1) * P], psO[:]).then_inc(sS.h, 1)
                        blk.scalar(s_); sS.n += 2
                        oevA = sS.n

                        def g2a_(g, dt=dt, lh=lh, oevA=oevA, _gb=gq.n):
                            g.wait_ge(gq.h, _gb)
                            g.wait_ge(sS.h, oevA)
                            for lb in range(2):
                                g.dma_start(out[512 * lh + P * lb:512 * lh + P * (lb + 1),
                                                P * dt:P * (dt + 1)],
                                            ostg[:, lb * P:(lb + 1) * P]).then_inc(gq.h, 16)
                        blk.gpsimd(g2a_); gq.n += 32
                        bA = gq.n

                        def s_b(s, otr=otr, bA=bA):
                            s.wait_ge(gq.h, bA)
                            for lb in range(2, 4):
                                s.wait_ge(sT.h, otr - 4 + lb + 1)
                                s.copy(ostg[:, (lb - 2) * P:(lb - 1) * P],
                                       psO[:]).then_inc(sS.h, 1)
                        blk.scalar(s_b); sS.n += 2
                        oevB = sS.n

                        def g2b_(g, dt=dt, lh=lh, oevB=oevB, _gb=bA):
                            g.wait_ge(gq.h, _gb)
                            g.wait_ge(sS.h, oevB)
                            for lb in range(2, 4):
                                g.dma_start(out[512 * lh + P * lb:512 * lh + P * (lb + 1),
                                                P * dt:P * (dt + 1)],
                                            ostg[:, (lb - 2) * P:(lb - 1) * P]).then_inc(gq.h, 16)
                        blk.gpsimd(g2b_); gq.n += 32

                _bc.__exit__(None, None, None)
                _bc = None

        with nc.Block() as blk:
            qq, gg = sq.n, gq.n

            def q_(sync, qq=qq, gg=gg):
                sync.wait_ge(gq.h, gg)
                sync.wait_ge(sq.h, qq)
            blk.sync(q_)

    return nc


def kernel(**inputs):
    x = np.ascontiguousarray(np.asarray(inputs["L0_attn_x"], dtype=np.float32)[0])
    L = x.shape[0]
    assert L == NCORES * LCORE

    f32 = lambda a: np.ascontiguousarray(np.asarray(a, dtype=np.float32))
    shared = dict(
        WinT=f32(np.asarray(inputs["L0_attn_W_in"]).T),
        WskT=f32(np.asarray(inputs["L0_attn_W_skip"]).T),
        Wd1T=f32(np.asarray(inputs["L0_attn_Wd1"]).T),
        Wd2T=f32(np.asarray(inputs["L0_attn_Wd2"]).T),
        WBCT=f32(np.concatenate([np.asarray(inputs["L0_attn_WB"]).T,
                                 np.asarray(inputs["L0_attn_WC"]).T], axis=1)),
        WoutT=f32(np.asarray(inputs["L0_attn_W_out"]).T),
        guT=f32(np.asarray(inputs["L0_ffn_w_gate_up"]).T),
        dwT=f32(np.asarray(inputs["L0_ffn_w_down"]).T),
        A_log=f32(inputs["L0_attn_A_log"]),
        conv_w=f32(np.asarray(inputs["L0_attn_conv_w"]).reshape(E, 4)),
        conv_b=f32(np.asarray(inputs["L0_attn_conv_b"]).reshape(1, E)),
        bd2=f32(np.asarray(inputs["L0_attn_bd2"]).reshape(1, E)),
        W_D=f32(np.asarray(inputs["L0_attn_W_D"]).reshape(1, E)),
        norm_w=f32(np.asarray(inputs["L0_attn_norm_w"]).reshape(1, D)),
        ident=np.eye(P, dtype=np.float32),
    )

    in_maps = []
    for k in range(NCORES):
        oh = np.zeros((1, NCORES), dtype=np.float32)
        oh[0, k] = 1.0
        xs = np.ascontiguousarray(x[k * LCORE:(k + 1) * LCORE])
        if k == 0:
            xp = np.zeros((HALO, D), dtype=np.float32)
        else:
            xp = np.ascontiguousarray(x[k * LCORE - HALO:k * LCORE])
        m = dict(shared)
        m.update(x_shard=xs, x_prev=xp, onehot=oh)
        in_maps.append(m)

    if "nc" not in _NC_CACHE:
        _NC_CACHE["nc"] = build_nc(NCORES)
    nc = _NC_CACHE["nc"]
    res = run_bass_kernel_spmd(nc, in_maps, core_ids=list(range(NCORES)))
    outs = [np.asarray(res.results[k]["out_shard"], dtype=np.float32) for k in range(NCORES)]
    return np.concatenate(outs, axis=0)[None, :, :]


# version 18
# speedup vs baseline: 1.2147x; 1.0422x over previous
import numpy as np
from contextlib import ExitStack

import concourse.bass as bass
import concourse.mybir as mybir
from concourse.bass_utils import run_bass_kernel_spmd

F32 = mybir.dt.float32
BF16 = mybir.dt.bfloat16
AF = mybir.ActivationFunctionType
ALU = mybir.AluOpType
AX = mybir.AxisListType

P = 128
D = 2048
E = 4096
N = 16
DTR = 128
INTER = 8192
LCORE = 1024
HALO = 128
LX = LCORE + HALO
TE = E // P
TD = D // P
NCORES = 8
EPS = 1e-5
NPG = 2
NGR = N // NPG

_NC_CACHE = {}


def build_nc(n_cores=NCORES):
    nc = bass.Bass("TRN2", target_bir_lowering=False)

    xin = nc.dram_tensor("x_shard", [LCORE, D], F32, kind="ExternalInput").ap()
    xprev = nc.dram_tensor("x_prev", [HALO, D], F32, kind="ExternalInput").ap()
    onehot = nc.dram_tensor("onehot", [1, NCORES], F32, kind="ExternalInput").ap()
    ident_in = nc.dram_tensor("ident", [P, P], F32, kind="ExternalInput").ap()

    WinT = nc.dram_tensor("WinT", [D, E], F32, kind="ExternalInput").ap()
    WskT = nc.dram_tensor("WskT", [D, E], F32, kind="ExternalInput").ap()
    Wd1T = nc.dram_tensor("Wd1T", [E, DTR], F32, kind="ExternalInput").ap()
    Wd2T = nc.dram_tensor("Wd2T", [DTR, E], F32, kind="ExternalInput").ap()
    WBCT = nc.dram_tensor("WBCT", [E, 2 * N], F32, kind="ExternalInput").ap()
    WoutT = nc.dram_tensor("WoutT", [E, D], F32, kind="ExternalInput").ap()
    guT = nc.dram_tensor("guT", [D, 2 * INTER], F32, kind="ExternalInput").ap()
    dwT = nc.dram_tensor("dwT", [INTER, D], F32, kind="ExternalInput").ap()
    A_log = nc.dram_tensor("A_log", [E, N], F32, kind="ExternalInput").ap()
    conv_w = nc.dram_tensor("conv_w", [E, 4], F32, kind="ExternalInput").ap()
    conv_b = nc.dram_tensor("conv_b", [1, E], F32, kind="ExternalInput").ap()
    bd2_in = nc.dram_tensor("bd2", [1, E], F32, kind="ExternalInput").ap()
    WD_in = nc.dram_tensor("W_D", [1, E], F32, kind="ExternalInput").ap()
    nw_in = nc.dram_tensor("norm_w", [1, D], F32, kind="ExternalInput").ap()

    out = nc.dram_tensor("out_shard", [LCORE, D], F32, kind="ExternalOutput").ap()

    uc_sp = nc.dram_tensor("uc_sp", [E, LCORE], BF16)
    ssk_sp = nc.dram_tensor("ssk_sp", [E, LCORE], BF16)
    z_sp = nc.dram_tensor("z_sp", [E, LCORE], BF16)
    o1_sp = nc.dram_tensor("o1_sp", [D, LCORE], F32)
    xt_sp = nc.dram_tensor("xt_sp", [D, LCORE], F32)
    winb_sp = nc.dram_tensor("winb_sp", [D, E], BF16)
    wskb_sp = nc.dram_tensor("wskb_sp", [D, E], BF16)
    bc_sp = nc.dram_tensor("bc_sp", [2 * N, LCORE], BF16)
    ag_in = [nc.dram_tensor(f"ag_in_{t}", [P, 2 * N], F32) for t in range(TE)]
    ag_out = [nc.dram_tensor(f"ag_out_{t}", [n_cores * P, 2 * N], F32) for t in range(TE)]

    es = ExitStack()
    with es:
        class Sem:
            def __init__(self, name):
                self.h = es.enter_context(nc.semaphore(name))
                self.n = 0

        sq = Sem("sq"); gq = Sem("gq")
        sT = Sem("sT"); sS = Sem("sS"); sV = Sem("sV"); sP = Sem("sP"); sC = Sem("sC")

        def vf(v, inst):
            inst.then_inc(sV.h, 1); sV.n += 1
            v.wait_ge(sV.h, sV.n)

        def sf(s, inst):
            inst.then_inc(sS.h, 1); sS.n += 1
            s.wait_ge(sS.h, sS.n)

        ident = es.enter_context(nc.sbuf_tensor("ident_sb", [P, P], F32))
        ones_f = es.enter_context(nc.sbuf_tensor("ones_f", [1, P], F32))
        ones_b = es.enter_context(nc.sbuf_tensor("ones_b", [1, P], BF16))
        onescol_b = es.enter_context(nc.sbuf_tensor("onescol_b", [P, 1], BF16))
        oh_bc = es.enter_context(nc.sbuf_tensor("oh_bc", [P, NCORES], F32))
        A_sb = es.enter_context(nc.sbuf_tensor("A_sb", [P, TE * N], F32))
        cw_sb = es.enter_context(nc.sbuf_tensor("cw_sb", [P, TE * 4], F32))
        cb_sb = es.enter_context(nc.sbuf_tensor("cb_sb", [P, TE], F32))
        bd2_sb = es.enter_context(nc.sbuf_tensor("bd2_sb", [P, TE], F32))
        wd_sb = es.enter_context(nc.sbuf_tensor("wd_sb", [P, TE], F32))
        nw_sb = es.enter_context(nc.sbuf_tensor("nw_sb", [P, TD], F32))
        hbuf = es.enter_context(nc.sbuf_tensor("hbuf", [P, N * NCORES], F32))

        _scoped = {}

        def salloc(name, shape, dt):
            cm = nc.sbuf_tensor(name, shape, dt)
            t = cm.__enter__()
            _scoped[name] = cm
            return t

        def sfree(*names):
            for nm in names:
                _scoped.pop(nm).__exit__(None, None, None)

        dpre_b = salloc("dpre_b", [P, LCORE], BF16)
        uc_ring = salloc("uc_ring", [P, 2 * LCORE], BF16)
        ssk_tile = salloc("ssk_tile", [P, LCORE], BF16)

        def barrier():
            with nc.Block() as blk:
                def mk(sem):
                    def f(e):
                        e.nop().then_inc(sem.h, 1)
                    return f
                blk.tensor(mk(sT)); blk.scalar(mk(sS)); blk.vector(mk(sV))
                blk.gpsimd(mk(sP)); blk.sync(mk(sq))
                sT.n += 1; sS.n += 1; sV.n += 1; sP.n += 1; sq.n += 1
                tt, ss, vv, pp, qq, gg = sT.n, sS.n, sV.n, sP.n, sq.n, gq.n

                def wall(e):
                    e.wait_ge(sT.h, tt); e.wait_ge(sS.h, ss); e.wait_ge(sV.h, vv)
                    e.wait_ge(sP.h, pp); e.wait_ge(sq.h, qq); e.wait_ge(gq.h, gg)
                blk.tensor(wall); blk.scalar(wall); blk.vector(wall)
                blk.gpsimd(wall); blk.sync(wall)

        with (
            nc.sbuf_tensor("vrow", [TE, P], F32) as vrow,
            nc.psum_tensor("psK0", [P, P], F32) as psK0,
        ):
            with nc.Block() as blk:
                def q_(sync, _qb=sq.n):
                    sync.wait_ge(sq.h, _qb)
                    sync.dma_start(ident[:], ident_in).then_inc(sq.h, 16)
                    sync.dma_start(oh_bc[0:1, :], onehot).then_inc(sq.h, 16)
                blk.sync(q_); sq.n += 32
                ld0 = sq.n

                def g_(g):
                    g.memset(ones_f[:], 1.0)
                    g.memset(ones_b[:], 1.0)
                    g.memset(onescol_b[:], 1.0)
                    g.memset(hbuf[:], 0.0).then_inc(sP.h, 1)
                blk.gpsimd(g_); sP.n += 1
                ones_done = sP.n

            for src, dst, rows in ((conv_b, cb_sb, TE), (bd2_in, bd2_sb, TE),
                                   (WD_in, wd_sb, TE), (nw_in, nw_sb, TD)):
                with nc.Block() as blk:
                    sprev = sS.n

                    def q_(sync, src=src, rows=rows, sprev=sprev, _qb=sq.n):
                        sync.wait_ge(sq.h, _qb)
                        sync.wait_ge(sS.h, sprev)
                        sync.dma_start(vrow[0:rows, :],
                                       src.rearrange("o (r p) -> (o r) p", p=P)).then_inc(sq.h, 16)
                    blk.sync(q_); sq.n += 16
                    ldv = sq.n

                    def t_(t, rows=rows, ldv=ldv):
                        t.wait_ge(sq.h, ldv)
                        t.wait_ge(sS.h, sS.n)
                        t.transpose(psK0[:, 0:rows], vrow[0:rows, :],
                                    ident[0:rows, 0:rows]).then_inc(sT.h, 1)
                    blk.tensor(t_); sT.n += 1
                    trv = sT.n

                    def s_(s, dst=dst, rows=rows, trv=trv):
                        s.wait_ge(sT.h, trv)
                        s.copy(dst[:, 0:rows], psK0[:, 0:rows]).then_inc(sS.h, 1)
                    blk.scalar(s_); sS.n += 1

            with nc.Block() as blk:
                def q_(sync, _qb=sq.n):
                    sync.wait_ge(sq.h, _qb)
                    for t in range(TE):
                        sync.dma_start(cw_sb[:, 4 * t:4 * t + 4],
                                       conv_w[P * t:P * (t + 1), :]).then_inc(sq.h, 16)
                        sync.dma_start(A_sb[:, N * t:N * (t + 1)],
                                       A_log[P * t:P * (t + 1), :]).then_inc(sq.h, 16)
                blk.sync(q_); sq.n += 32 * TE
                lda = sq.n

                def s_(s):
                    s.wait_ge(sq.h, lda)
                    for t in range(TE):
                        s.activation(A_sb[:, N * t:N * (t + 1)], A_sb[:, N * t:N * (t + 1)],
                                     AF.Exp).then_inc(sS.h, 1)
                blk.scalar(s_); sS.n += TE
                expa = sS.n

                def v_(v):
                    v.wait_ge(sS.h, expa)
                    v.tensor_scalar_mul(A_sb[:], A_sb[:], -1.0).then_inc(sV.h, 1)
                blk.vector(v_); sV.n += 1

                sprev2 = sS.n

                def t_(t, sprev2=sprev2, cur=None):
                    t.wait_ge(sq.h, sq.n)
                    t.wait_ge(sP.h, ones_done)
                    t.wait_ge(sS.h, sprev2)
                    t.matmul(psK0[:, 0:NCORES], ones_f[0:1, :], oh_bc[0:1, :],
                             start=True, stop=True).then_inc(sT.h, 1)
                blk.tensor(t_); sT.n += 1
                ohm = sT.n

                def s2_(s):
                    s.wait_ge(sT.h, ohm)
                    s.copy(oh_bc[:], psK0[:, 0:NCORES]).then_inc(sS.h, 1)
                blk.scalar(s2_); sS.n += 1

        barrier()

        keW_row = salloc("keW_row", [1, E], BF16)
        keS_row = salloc("keS_row", [1, E], BF16)
        keW_f = salloc("keW_f", [1, E], F32)
        keS_f = salloc("keS_f", [1, E], F32)
        with (
            nc.sbuf_tensor("wstg", [P, 2 * E], F32) as wstg,
            nc.sbuf_tensor("wstgb", [P, 2 * E], BF16) as wstgb,
            nc.sbuf_tensor("ke_acc", [P, TE], F32) as ke_acc,
            nc.sbuf_tensor("keT", [TE, P], F32) as keT,
            nc.psum_tensor("psK1", [P, P], F32) as psK1,
            nc.psum_tensor("psK2", [P, TE], F32) as psK2,
        ):
            for wi, (wsrc, wdst, kef, keb) in enumerate(
                    ((WinT, winb_sp, keW_f, keW_row), (WskT, wskb_sp, keS_f, keS_row))):
                for dt in range(TD):
                    buf = dt % 2
                    with nc.Block() as blk:
                        sfree0 = sS.n
                        tfree0 = sT.n

                        def q_(sync, dt=dt, buf=buf, wsrc=wsrc, sfree0=sfree0, tfree0=tfree0,
                               _qb=sq.n):
                            sync.wait_ge(sq.h, _qb)
                            sync.wait_ge(sS.h, sfree0)
                            sync.wait_ge(sT.h, tfree0)
                            sync.dma_start(wstg[:, buf * E:(buf + 1) * E],
                                           wsrc[P * dt:P * (dt + 1), :]).then_inc(sq.h, 16)
                        blk.sync(q_); sq.n += 16
                        ldw_ = sq.n

                        def s_(s, dt=dt, buf=buf, ldw_=ldw_):
                            s.wait_ge(sq.h, ldw_)
                            s.activation(wstgb[:, buf * E:(buf + 1) * E],
                                         wstg[:, buf * E:(buf + 1) * E],
                                         AF.Copy, scale=nw_sb[:, dt:dt + 1]).then_inc(sS.h, 1)
                        blk.scalar(s_); sS.n += 1
                        cast_done = sS.n

                        def q2_(sync, dt=dt, buf=buf, wdst=wdst, cast_done=cast_done,
                                _qb=sq.n):
                            sync.wait_ge(sq.h, _qb)
                            sync.wait_ge(sS.h, cast_done)
                            sync.dma_start(wdst.ap()[P * dt:P * (dt + 1), :],
                                           wstgb[:, buf * E:(buf + 1) * E]).then_inc(sq.h, 16)
                        blk.sync(q2_); sq.n += 16

                        vke_prev = sV.n

                        def t_(t, dt=dt, buf=buf, cast_done=cast_done, vke_prev=vke_prev):
                            t.wait_ge(sS.h, cast_done)
                            t.wait_ge(sV.h, vke_prev)
                            for eb in range(TE):
                                mm = t.matmul(psK2[:, eb:eb + 1],
                                              wstgb[:, buf * E + P * eb:buf * E + P * (eb + 1)],
                                              onescol_b[:],
                                              start=True, stop=True)
                                if eb == TE - 1:
                                    mm.then_inc(sT.h, 1)
                        blk.tensor(t_); sT.n += 1
                        kemm_t = sT.n

                        vke_prev2 = sV.n

                        def vke_(v, dt=dt, kemm_t=kemm_t, vke_prev2=vke_prev2):
                            v.wait_ge(sT.h, kemm_t)
                            v.wait_ge(sV.h, vke_prev2)
                            if dt == 0:
                                v.tensor_copy(ke_acc[:], psK2[:]).then_inc(sV.h, 1)
                            else:
                                v.tensor_add(ke_acc[:], ke_acc[:], psK2[:]).then_inc(sV.h, 1)
                        blk.vector(vke_); sV.n += 1

                with nc.Block() as blk:
                    keacc_v = sV.n

                    def t2_(t, keacc_v=keacc_v, gprev=gq.n):
                        t.wait_ge(sV.h, keacc_v)
                        t.wait_ge(gq.h, gprev)
                        t.transpose(psK1[0:TE, :], ke_acc[:], ident[:]).then_inc(sT.h, 1)
                    blk.tensor(t2_); sT.n += 1
                    ketr = sT.n

                    def s15_(s, ketr=ketr):
                        s.wait_ge(sT.h, ketr)
                        s.copy(keT[:], psK1[0:TE, :]).then_inc(sS.h, 1)
                    blk.scalar(s15_); sS.n += 1
                    ketc = sS.n

                    def g_(g, kef=kef, ketc=ketc, _gb=gq.n):
                        g.wait_ge(gq.h, _gb)
                        g.wait_ge(sS.h, ketc)
                        for eb in range(TE):
                            g.dma_start(kef[0:1, P * eb:P * (eb + 1)],
                                        keT[eb:eb + 1, :]).then_inc(gq.h, 16)
                    blk.gpsimd(g_); gq.n += 16 * TE
                    kedma = gq.n

                    def s2_(s, keb=keb, kef=kef, kedma=kedma):
                        s.wait_ge(gq.h, kedma)
                        s.copy(keb[:], kef[:]).then_inc(sS.h, 1)
                    blk.scalar(s2_); sS.n += 1

        barrier()

        xT_bf = salloc("xT_bf", [P, TD * LX], BF16)
        rb_bc = salloc("rb_bc", [P, LX], F32)
        r_row = salloc("r_row", [1, LX], F32)
        nmr_rowb = salloc("nmr_rowb", [1, LX], BF16)
        nmr_f = salloc("nmr_f", [1, LX], F32)
        es2 = ExitStack()
        sx = es2.enter_context(nc.sbuf_tensor("sx", [P, 2 * D], F32))
        xstage = es2.enter_context(nc.sbuf_tensor("xstage", [P, TD * P], F32))
        sxsq = es2.enter_context(nc.sbuf_tensor("sxsq", [P, D], F32))
        scol = es2.enter_context(nc.sbuf_tensor("scol", [P, 16], F32))
        rcols = es2.enter_context(nc.sbuf_tensor("rcols", [P, 9], F32))
        nmrcols = es2.enter_context(nc.sbuf_tensor("nmrcols", [P, 9], F32))
        rowst = es2.enter_context(nc.sbuf_tensor("rowst", [9, P], F32))
        rowst2 = es2.enter_context(nc.sbuf_tensor("rowst2", [9, P], F32))
        psX = es2.enter_context(nc.psum_tensor("psX", [P, P], F32))
        psX2 = es2.enter_context(nc.psum_tensor("psX2", [P, P], F32))
        with es2:
            NJ = 9
            for j in range(NJ):
                buf = j % 2
                with nc.Block() as blk:
                    vprev = sV.n
                    tprev = sT.n
                    s_base = sS.n

                    def q_(sync, j=j, buf=buf, vprev=vprev, tprev=tprev, _qb=sq.n):
                        sync.wait_ge(sq.h, _qb)
                        sync.wait_ge(sV.h, vprev)
                        sync.wait_ge(sT.h, tprev)
                        src = xprev if j == 0 else xin[P * (j - 1):P * j, :]
                        sync.dma_start(sx[:, buf * D:(buf + 1) * D], src).then_inc(sq.h, 16)
                    blk.sync(q_); sq.n += 16
                    ldx = sq.n

                    def s_(s, buf=buf, ldx=ldx):
                        s.wait_ge(sq.h, ldx)
                        s.wait_ge(sV.h, sV.n)
                        s.activation(sxsq[:], sx[:, buf * D:(buf + 1) * D], AF.Square).then_inc(sS.h, 1)
                    blk.scalar(s_); sS.n += 1
                    sqd = sS.n
                    s_base = sS.n

                    def v_(v, j=j, buf=buf, ldx=ldx, sqd=sqd):
                        x_ = sx[:, buf * D:(buf + 1) * D]
                        v.wait_ge(sq.h, ldx)
                        vf(v, v.reduce_sum(scol[:, 0:1], x_, AX.X))
                        v.wait_ge(sS.h, sqd)
                        vf(v, v.reduce_sum(scol[:, 1:2], sxsq[:], AX.X))
                        vf(v, v.tensor_scalar_mul(scol[:, 2:3], scol[:, 0:1], 1.0 / D))
                        vf(v, v.tensor_mul(scol[:, 3:4], scol[:, 2:3], scol[:, 2:3]))
                        vf(v, v.tensor_scalar_mul(scol[:, 4:5], scol[:, 1:2], 1.0 / D))
                        vf(v, v.tensor_sub(scol[:, 5:6], scol[:, 4:5], scol[:, 3:4]))
                        vf(v, v.tensor_scalar_add(scol[:, 6:7], scol[:, 5:6], EPS))
                        vf(v, v.reciprocal(scol[:, 7:8], scol[:, 6:7]))
                        vf(v, v.tensor_mul(scol[:, 8:9], scol[:, 5:6], scol[:, 7:8]))
                        v.tensor_scalar_add(scol[:, 9:10], scol[:, 8:9], EPS).then_inc(sV.h, 1)
                    blk.vector(v_); sV.n += 1
                    vstat = sV.n

                    def s2_(s, vstat=vstat):
                        s.wait_ge(sV.h, vstat)
                        sf(s, s.activation(scol[:, 10:11], scol[:, 6:7], AF.Ln))
                        s.activation(scol[:, 11:12], scol[:, 9:10], AF.Ln).then_inc(sS.h, 1)
                    blk.scalar(s2_); sS.n += 1
                    slog = sS.n

                    def v2_(v, slog=slog):
                        v.wait_ge(sS.h, slog)
                        v.tensor_add(scol[:, 12:13], scol[:, 10:11], scol[:, 11:12]).then_inc(sV.h, 1)
                    blk.vector(v2_); sV.n += 1
                    vln = sV.n

                    def s3_(s, vln=vln):
                        s.wait_ge(sV.h, vln)
                        s.activation(scol[:, 13:14], scol[:, 12:13], AF.Exp, scale=-0.5).then_inc(sS.h, 1)
                    blk.scalar(s3_); sS.n += 1
                    sr = sS.n

                    def v3_(v, j=j, sr=sr):
                        v.wait_ge(sS.h, sr)
                        vf(v, v.tensor_copy(rcols[:, j:j + 1], scol[:, 13:14]))
                        vf(v, v.tensor_mul(scol[:, 14:15], scol[:, 2:3], scol[:, 13:14]))
                        v.tensor_scalar_mul(nmrcols[:, j:j + 1], scol[:, 14:15], -1.0).then_inc(sV.h, 1)
                    blk.vector(v3_); sV.n += 1

                    s_evac_base = sS.n
                    g_evac_base = gq.n
                    nse = 2 if j >= 1 else 1

                    def t_(t, j=j, buf=buf, ldx=ldx, s_evac_base=s_evac_base, nse=nse):
                        t.wait_ge(sq.h, ldx)
                        x_ = sx[:, buf * D:(buf + 1) * D]
                        for dt in range(TD):
                            ps = psX if dt % 2 == 0 else psX2
                            if dt >= 2:
                                t.wait_ge(sS.h, s_evac_base + nse * (dt - 1))
                            t.transpose(ps[:], x_[:, P * dt:P * (dt + 1)], ident[:]).then_inc(sT.h, 1)
                    blk.tensor(t_); sT.n += TD
                    ttr = sT.n

                    def s4_(s, j=j, ttr=ttr, g_evac_base=g_evac_base):
                        if j >= 1:
                            s.wait_ge(gq.h, g_evac_base)
                        for dt in range(TD):
                            s.wait_ge(sT.h, ttr - TD + dt + 1)
                            ps = psX if dt % 2 == 0 else psX2
                            s.copy(xT_bf[:, dt * LX + P * j:dt * LX + P * (j + 1)],
                                   ps[:]).then_inc(sS.h, 1)
                            if j >= 1:
                                s.copy(xstage[:, dt * P:(dt + 1) * P],
                                       ps[:]).then_inc(sS.h, 1)
                    blk.scalar(s4_); sS.n += nse * TD
                    sxe = sS.n

                    if j >= 1:
                        def g_(g, j=j, sxe=sxe, nse=nse, _gb=gq.n):
                            g.wait_ge(gq.h, _gb)
                            for dt in range(TD):
                                g.wait_ge(sS.h, sxe - nse * TD + nse * (dt + 1))
                                g.dma_start(xt_sp.ap()[P * dt:P * (dt + 1),
                                                       P * (j - 1):P * j],
                                            xstage[:, dt * P:(dt + 1) * P]).then_inc(gq.h, 16)
                        blk.gpsimd(g_); gq.n += 16 * TD

            with nc.Block() as blk:
                vdone = sV.n
                gdone = gq.n
                sdone = sS.n

                def t_(t, vdone=vdone, gdone=gdone, sdone=sdone):
                    t.wait_ge(sV.h, vdone)
                    t.wait_ge(gq.h, gdone)
                    t.wait_ge(sS.h, sdone)
                    t.transpose(psX[0:9, :], rcols[:], ident[:]).then_inc(sT.h, 1)
                    t.transpose(psX2[0:9, :], nmrcols[:], ident[:]).then_inc(sT.h, 1)
                blk.tensor(t_); sT.n += 2
                rtr = sT.n

                def se_(s, rtr=rtr):
                    s.wait_ge(sT.h, rtr)
                    s.copy(rowst[:], psX[0:9, :])
                    s.copy(rowst2[:], psX2[0:9, :]).then_inc(sS.h, 1)
                blk.scalar(se_); sS.n += 1
                rse = sS.n

                def g_(g, rse=rse, _gb=gq.n):
                    g.wait_ge(gq.h, _gb)
                    g.wait_ge(sS.h, rse)
                    for j in range(9):
                        g.dma_start(r_row[0:1, P * j:P * (j + 1)], rowst[j:j + 1, :]).then_inc(gq.h, 16)
                        g.dma_start(nmr_f[0:1, P * j:P * (j + 1)], rowst2[j:j + 1, :]).then_inc(gq.h, 16)
                blk.gpsimd(g_); gq.n += 16 * 18
                rdma = gq.n

                def s_(s, rdma=rdma):
                    s.wait_ge(gq.h, rdma)
                    s.copy(nmr_rowb[:], nmr_f[:]).then_inc(sS.h, 1)
                blk.scalar(s_); sS.n += 1

                sb_base = sS.n

                def t2_(t, rdma=rdma, sb_base=sb_base):
                    t.wait_ge(gq.h, rdma)
                    for i in range(9):
                        ps = psX if i % 2 == 0 else psX2
                        if i >= 2:
                            t.wait_ge(sS.h, sb_base + i - 1)
                        t.matmul(ps[:], ones_f[0:1, :], r_row[0:1, P * i:P * (i + 1)],
                                 start=True, stop=True).then_inc(sT.h, 1)
                blk.tensor(t2_); sT.n += 9
                btr = sT.n

                def s5_(s, btr=btr):
                    for i in range(9):
                        s.wait_ge(sT.h, btr - 9 + i + 1)
                        ps = psX if i % 2 == 0 else psX2
                        s.copy(rb_bc[:, P * i:P * (i + 1)], ps[:]).then_inc(sS.h, 1)
                blk.scalar(s5_); sS.n += 9

        barrier()

        WS = 160
        ucmm_done = [0] * TE
        ucspill_done = [0] * TE
        ucdone_arr = [0] * TE
        sskd_arr = [0] * TE
        es2 = ExitStack()
        wv = es2.enter_context(nc.sbuf_tensor("wv", [P, 2 * TD * P], BF16))
        wvs = es2.enter_context(nc.sbuf_tensor("wvs", [P, 2 * TD * P], BF16))
        wsm = es2.enter_context(nc.sbuf_tensor("wsm", [P, 2 * WS], BF16))
        wsm_f = es2.enter_context(nc.sbuf_tensor("wsm_f", [P, 2 * WS], F32))
        up_ext = es2.enter_context(nc.sbuf_tensor("up_ext", [P, LX], F32))
        cva = es2.enter_context(nc.sbuf_tensor("cva", [P, LCORE], F32))
        cvb = es2.enter_context(nc.sbuf_tensor("cvb", [P, LCORE], F32))
        skf = es2.enter_context(nc.sbuf_tensor("skf", [P, LCORE], F32))
        psU = es2.enter_context(nc.psum_tensor("psU", [P, 512], F32))
        psU2 = es2.enter_context(nc.psum_tensor("psU2", [P, 512], F32))
        psDP = es2.enter_context(nc.psum_tensor("psDP", [P, LCORE], F32))
        psBC = es2.enter_context(nc.psum_tensor("psBC", [2 * N, LCORE], F32))
        with es2:
            UCHUNKS = [(0, 512), (512, 1024), (1024, LX)]
            _bc3 = None
            for t in range(TE):
                buf = t % 2
                ucslot = t % 2
                if t % 4 == 0:
                    if _bc3 is not None:
                        _bc3.__exit__(None, None, None)
                    _bc3 = nc.Block()
                    blk = _bc3.__enter__()
                if True:
                    if t == 0:
                        s3tbase = sT.n
                        s3t_arr = {}
                        s3wc_arr = {}
                    tprev = s3t_arr.get(t - 2, s3tbase)

                    def q_(sync, t=t, buf=buf, tprev=tprev, _qb=sq.n,
                           _wc=s3wc_arr.get(t - 2, 0)):
                        sync.wait_ge(sq.h, _qb)
                        sync.wait_ge(sT.h, tprev)
                        if t >= 2:
                            sync.wait_ge(sV.h, _wc)
                        sync.dma_start(
                            wv[:, buf * TD * P:(buf + 1) * TD * P].rearrange("p (a c) -> p a c", c=P),
                            winb_sp.ap()[:, P * t:P * (t + 1)].rearrange("(a p) c -> p a c", p=P),
                        ).then_inc(sq.h, 16)
                        sync.dma_start(
                            wvs[:, buf * TD * P:(buf + 1) * TD * P].rearrange("p (a c) -> p a c", c=P),
                            wskb_sp.ap()[:, P * t:P * (t + 1)].rearrange("(a p) c -> p a c", p=P),
                        ).then_inc(sq.h, 16)
                        sync.dma_start(wsm_f[:, buf * WS:buf * WS + P],
                                       Wd1T[P * t:P * (t + 1), :]).then_inc(sq.h, 16)
                        sync.dma_start(wsm_f[:, buf * WS + P:buf * WS + WS],
                                       WBCT[P * t:P * (t + 1), :]).then_inc(sq.h, 16)
                    blk.sync(q_); sq.n += 64
                    ldw1 = sq.n

                    def v00_(v, buf=buf, ldw1=ldw1):
                        v.wait_ge(sq.h, ldw1)
                        v.tensor_copy(wsm[:, buf * WS:(buf + 1) * WS],
                                      wsm_f[:, buf * WS:(buf + 1) * WS]).then_inc(sV.h, 1)
                    blk.vector(v00_); sV.n += 1
                    wdcast = sV.n
                    s3wc_arr[t] = wdcast

                    vprev_evac = sV.n

                    def t_(t_eng, t=t, buf=buf, ldw1=ldw1, vprev_evac=vprev_evac):
                        t_eng.wait_ge(sq.h, ldw1)
                        t_eng.wait_ge(sV.h, vprev_evac)
                        for ci, (c0, c1) in enumerate(UCHUNKS):
                            ps = psU if ci % 2 == 0 else psU2
                            cw = c1 - c0
                            if ci == 2:
                                t_eng.wait_ge(sV.h, vprev_evac + 1)
                            for dt in range(TD):
                                t_eng.matmul(ps[:, 0:cw],
                                             wv[:, buf * TD * P + dt * P:buf * TD * P + (dt + 1) * P],
                                             xT_bf[:, dt * LX + c0:dt * LX + c1],
                                             start=(dt == 0), stop=False)
                            t_eng.matmul(ps[:, 0:cw], keW_row[0:1, P * t:P * (t + 1)],
                                         nmr_rowb[0:1, c0:c1],
                                         start=False, stop=True).then_inc(sT.h, 1)
                    blk.tensor(t_); sT.n += 3
                    umm = sT.n

                    def v_(v, umm=umm):
                        for ci, (c0, c1) in enumerate(UCHUNKS):
                            cw = c1 - c0
                            ps = psU if ci % 2 == 0 else psU2
                            v.wait_ge(sT.h, umm - 3 + ci + 1)
                            v.tensor_mul(up_ext[:, c0:c1], ps[:, 0:cw],
                                         rb_bc[:, c0:c1]).then_inc(sV.h, 1)
                    blk.vector(v_); sV.n += 3
                    uev = sV.n

                    def v2_(v, t=t, uev=uev):
                        if t >= 1:
                            v.wait_ge(sS.h, ucdone_arr[t - 1])
                        v.wait_ge(sV.h, uev)
                        vf(v, v.tensor_single_scalar(cva[:], up_ext[:, 125:125 + LCORE],
                                                     cw_sb[:, 4 * t:4 * t + 1], ALU.mult))
                        vf(v, v.scalar_tensor_tensor(cvb[:], up_ext[:, 126:126 + LCORE],
                                                     cw_sb[:, 4 * t + 1:4 * t + 2], cva[:],
                                                     ALU.mult, ALU.add))
                        vf(v, v.scalar_tensor_tensor(cva[:], up_ext[:, 127:127 + LCORE],
                                                     cw_sb[:, 4 * t + 2:4 * t + 3], cvb[:],
                                                     ALU.mult, ALU.add))
                        v.scalar_tensor_tensor(cvb[:], up_ext[:, 128:128 + LCORE],
                                               cw_sb[:, 4 * t + 3:4 * t + 4], cva[:],
                                               ALU.mult, ALU.add).then_inc(sV.h, 1)
                    blk.vector(v2_); sV.n += 1
                    convd = sV.n

                    def s2_(s, t=t, ucslot=ucslot, convd=convd):
                        s.wait_ge(sV.h, convd)
                        if t >= 2:
                            s.wait_ge(sT.h, ucmm_done[t - 2])
                            s.wait_ge(gq.h, ucspill_done[t - 2])
                        s.activation(uc_ring[:, ucslot * LCORE:(ucslot + 1) * LCORE], cvb[:],
                                     AF.Silu, bias=cb_sb[:, t:t + 1]).then_inc(sS.h, 1)
                    blk.scalar(s2_); sS.n += 1
                    ucdone = sS.n
                    ucdone_arr[t] = ucdone

                    def g_(g, t=t, ucslot=ucslot, ucdone=ucdone, _gb=gq.n):
                        g.wait_ge(gq.h, _gb)
                        g.wait_ge(sS.h, ucdone)
                        g.dma_start(uc_sp.ap()[P * t:P * (t + 1), :],
                                    uc_ring[:, ucslot * LCORE:(ucslot + 1) * LCORE]).then_inc(gq.h, 16)
                    blk.gpsimd(g_); gq.n += 16
                    ucspill_done[t] = gq.n

                    def t2_(t_eng, t=t, buf=buf, ucslot=ucslot, ucdone=ucdone, wdcast=wdcast,
                            _dummy=None):
                        t_eng.wait_ge(sS.h, ucdone)
                        t_eng.wait_ge(sV.h, wdcast)
                        uc = uc_ring[:, ucslot * LCORE:(ucslot + 1) * LCORE]
                        for lc in range(2):
                            t_eng.matmul(psDP[:, 512 * lc:512 * (lc + 1)],
                                         wsm[:, buf * WS:buf * WS + P],
                                         uc[:, 512 * lc:512 * (lc + 1)],
                                         start=(t == 0), stop=(t == TE - 1), skip_group_check=True)
                        for lc in range(2):
                            mm = t_eng.matmul(psBC[:, 512 * lc:512 * (lc + 1)],
                                              wsm[:, buf * WS + P:buf * WS + WS],
                                              uc[:, 512 * lc:512 * (lc + 1)],
                                              start=(t == 0), stop=(t == TE - 1), skip_group_check=True)
                            if lc == 1:
                                mm.then_inc(sT.h, 1)
                    blk.tensor(t2_); sT.n += 1
                    ucmm_done[t] = sT.n

                    def t3_(t_eng, t=t, buf=buf, uev=uev):
                        t_eng.wait_ge(sV.h, uev)
                        for lc in range(2):
                            ps = psU if lc == 0 else psU2
                            c0 = HALO + 512 * lc
                            for dt in range(TD):
                                t_eng.matmul(ps[:],
                                             wvs[:, buf * TD * P + dt * P:buf * TD * P + (dt + 1) * P],
                                             xT_bf[:, dt * LX + c0:dt * LX + c0 + 512],
                                             start=(dt == 0), stop=False)
                            t_eng.matmul(ps[:], keS_row[0:1, P * t:P * (t + 1)],
                                         nmr_rowb[0:1, c0:c0 + 512],
                                         start=False, stop=True).then_inc(sT.h, 1)
                    blk.tensor(t3_); sT.n += 2
                    skmm = sT.n
                    s3t_arr[t] = skmm

                    def v3_(v, t=t, skmm=skmm):
                        if t >= 1:
                            v.wait_ge(sS.h, sskd_arr[t - 1])
                        for lc in range(2):
                            ps = psU if lc == 0 else psU2
                            v.wait_ge(sT.h, skmm - 2 + lc + 1)
                            v.tensor_mul(skf[:, 512 * lc:512 * (lc + 1)], ps[:],
                                         rb_bc[:, HALO + 512 * lc:HALO + 512 * (lc + 1)]).then_inc(sV.h, 1)
                    blk.vector(v3_); sV.n += 2
                    skev = sV.n

                    def s3_(s, t=t, skev=skev):
                        s.wait_ge(sV.h, skev)
                        if t >= 1:
                            s.wait_ge(gq.h, _sskspill[t - 1])
                        s.activation(ssk_tile[:], skf[:], AF.Silu).then_inc(sS.h, 1)
                    blk.scalar(s3_); sS.n += 1
                    sskd = sS.n
                    sskd_arr[t] = sskd

                    def g2_(g, t=t, sskd=sskd, _gb=gq.n):
                        g.wait_ge(gq.h, _gb)
                        g.wait_ge(sS.h, sskd)
                        g.dma_start(ssk_sp.ap()[P * t:P * (t + 1), :], ssk_tile[:]).then_inc(gq.h, 16)
                    blk.gpsimd(g2_); gq.n += 16
                    if t == 0:
                        _sskspill = [0] * TE
                    _sskspill[t] = gq.n

            _bc3.__exit__(None, None, None)
            _bc3 = None
            with nc.Block() as blk:
                tall = sT.n

                def s_(s, tall=tall):
                    s.wait_ge(sT.h, tall)
                    s.wait_ge(gq.h, _sskspill[TE - 1])
                    s.copy(dpre_b[:], psDP[:])
                    s.copy(ssk_tile[0:2 * N, :], psBC[:]).then_inc(sS.h, 1)
                blk.scalar(s_); sS.n += 1
                bcev = sS.n

                def g_(g, bcev=bcev, _gb=gq.n):
                    g.wait_ge(gq.h, _gb)
                    g.wait_ge(sS.h, bcev)
                    g.dma_start(bc_sp.ap(), ssk_tile[0:2 * N, :]).then_inc(gq.h, 16)
                blk.gpsimd(g_); gq.n += 16

        barrier()
        sfree("nmr_f", "nmr_rowb", "r_row", "rb_bc", "xT_bf",
              "keS_f", "keW_f", "keS_row", "keW_row")

        HS = 24
        es2 = ExitStack()
        wd2s = es2.enter_context(nc.sbuf_tensor("wd2s", [P, 2 * P], BF16))
        wd2f = es2.enter_context(nc.sbuf_tensor("wd2f", [P, 2 * P], F32))
        spx = es2.enter_context(nc.sbuf_tensor("spx", [P, LCORE], F32))
        spa = es2.enter_context(nc.sbuf_tensor("spa", [P, LCORE], F32))
        delta_f = es2.enter_context(nc.sbuf_tensor("delta_f", [P, 2 * LCORE], F32))
        srel = es2.enter_context(nc.sbuf_tensor("srel", [P, 2 * LCORE], F32))
        du_f = es2.enter_context(nc.sbuf_tensor("du_f", [P, 2 * LCORE], BF16))
        a_slab = es2.enter_context(nc.sbuf_tensor("a_slab", [P, 2 * NPG * LCORE], F32))
        w_slab = es2.enter_context(nc.sbuf_tensor("w_slab", [P, 2 * NPG * LCORE], BF16))
        h_slab = es2.enter_context(nc.sbuf_tensor("h_slab", [P, HS * LCORE], BF16))
        p_slab = es2.enter_context(nc.sbuf_tensor("p_slab", [P, 2 * NPG * LCORE], BF16))
        ht_buf = es2.enter_context(nc.sbuf_tensor("ht_buf", [P, NPG * LCORE], BF16))
        q_buf = es2.enter_context(nc.sbuf_tensor("q_buf", [P, (N // 2) * LCORE], BF16))
        B_sb = es2.enter_context(nc.sbuf_tensor("B_sb", [P, 4 * LCORE], BF16))
        C_sb = es2.enter_context(nc.sbuf_tensor("C_sb", [P, 4 * LCORE], BF16))
        y_f = es2.enter_context(nc.sbuf_tensor("y_f", [P, LCORE], F32))
        y_f2 = es2.enter_context(nc.sbuf_tensor("y_f2", [P, LCORE], F32))
        trA = es2.enter_context(nc.sbuf_tensor("trA", [P, LCORE], F32))
        trB = es2.enter_context(nc.sbuf_tensor("trB", [P, LCORE], F32))
        trC = es2.enter_context(nc.sbuf_tensor("trC", [P, LCORE], F32))
        zt_bf = es2.enter_context(nc.sbuf_tensor("zt_bf", [P, LCORE], BF16))
        z_bf = es2.enter_context(nc.sbuf_tensor("z_bf", [P, LCORE], BF16))
        stout = es2.enter_context(nc.sbuf_tensor("stout", [P, 2 * N], F32))
        gsb = es2.enter_context(nc.sbuf_tensor("gsb", [P, NCORES * 2 * N], F32))
        hcur = es2.enter_context(nc.sbuf_tensor("hcur", [P, N], F32))
        htmp = es2.enter_context(nc.sbuf_tensor("htmp", [P, N], F32))
        hsel = es2.enter_context(nc.sbuf_tensor("hsel", [P, N], F32))
        psDel = es2.enter_context(nc.psum_tensor("psDel", [P, LCORE], F32))
        with es2:
            ag_done = [0] * TE
            std_done = [0] * TE
            zv_done = [0] * TE
            yred_done = {}
            hsel_done = [0] * TE
            scans_done_v = [0] * TE
            scv = {}
            wpool = {}
            qpool = {}
            htv = {}
            l1h = {}
            dready_a = [0] * TE
            sskl_done = [0] * TE

            def hslot(t, n):
                return ((N * t + n) % HS) * LCORE

            def emit_delta(t):
                buf = t % 2
                slot = t % 2
                with nc.Block() as blk:
                    tprev = sT.n

                    def q_(sync, t=t, buf=buf, tprev=tprev, _qb=sq.n):
                        sync.wait_ge(sq.h, _qb)
                        sync.wait_ge(sT.h, tprev)
                        sync.dma_start(wd2f[:, buf * P:(buf + 1) * P],
                                       Wd2T[:, P * t:P * (t + 1)]).then_inc(sq.h, 16)
                    blk.sync(q_); sq.n += 16
                    ldd = sq.n

                    def g_(g, t=t, slot=slot, _gb=gq.n):
                        g.wait_ge(gq.h, _gb)
                        if t >= 2:
                            g.wait_ge(sP.h, zv_done[t - 2])
                        g.dma_start(uc_ring[:, slot * LCORE:(slot + 1) * LCORE],
                                    uc_sp.ap()[P * t:P * (t + 1), :]).then_inc(gq.h, 16)
                    blk.gpsimd(g_); gq.n += 16
                    lduc = gq.n

                    def s_(s, buf=buf, ldd=ldd):
                        s.wait_ge(sq.h, ldd)
                        s.copy(wd2s[:, buf * P:(buf + 1) * P],
                               wd2f[:, buf * P:(buf + 1) * P]).then_inc(sS.h, 1)
                    blk.scalar(s_); sS.n += 1
                    w2c = sS.n

                    sfree_ = sS.n

                    def t_(t_eng, buf=buf, w2c=w2c, sfree_=sfree_):
                        t_eng.wait_ge(sS.h, w2c)
                        t_eng.wait_ge(sS.h, sfree_)
                        for lc in range(2):
                            mm = t_eng.matmul(psDel[:, 512 * lc:512 * (lc + 1)],
                                              wd2s[:, buf * P:(buf + 1) * P],
                                              dpre_b[:, 512 * lc:512 * (lc + 1)],
                                              start=True, stop=True)
                            if lc == 1:
                                mm.then_inc(sT.h, 1)
                    blk.tensor(t_); sT.n += 1
                    dmm = sT.n

                    vfree = sV.n

                    def v0_(v, t=t, dmm=dmm, vfree=vfree):
                        v.wait_ge(sT.h, dmm)
                        v.wait_ge(sV.h, vfree)
                        v.tensor_scalar_add(spx[:], psDel[:], bd2_sb[:, t:t + 1]).then_inc(sV.h, 1)
                    blk.vector(v0_); sV.n += 1
                    vbias = sV.n

                    def s2_(s, vbias=vbias):
                        s.wait_ge(sV.h, vbias)
                        sf(s, s.activation(spa[:], spx[:], AF.Abs))
                        sf(s, s.activation(spa[:], spa[:], AF.Exp, scale=-1.0))
                        sf(s, s.activation(spa[:], spa[:], AF.Ln, bias=1.0))
                        s.activation(spx[:], spx[:], AF.Relu).then_inc(sS.h, 1)
                    blk.scalar(s2_); sS.n += 1
                    spd = sS.n

                    def v_(v, slot=slot, spd=spd, lduc=lduc,
                           _df=wpool.get((t - 2, N - 1), 0)):
                        v.wait_ge(sS.h, spd)
                        v.wait_ge(sP.h, _df)
                        dslab = delta_f[:, slot * LCORE:(slot + 1) * LCORE]
                        vf(v, v.tensor_add(dslab, spa[:], spx[:]))
                        v.tensor_tensor_scan(srel[:, slot * LCORE:(slot + 1) * LCORE],
                                             dslab, dslab, 0.0, ALU.add, ALU.bypass)
                        v.wait_ge(gq.h, lduc)
                        v.tensor_mul(du_f[:, slot * LCORE:(slot + 1) * LCORE], dslab,
                                     uc_ring[:, slot * LCORE:(slot + 1) * LCORE]).then_inc(sV.h, 1)
                    blk.vector(v_); sV.n += 1
                    dready_a[t] = sV.n

            def emit_combine(blk, t):
                if True:
                    def g_(g, t=t, _gb=gq.n):
                        g.wait_ge(gq.h, _gb)
                        g.wait_ge(sC.h, ag_done[t])
                        if t >= 1:
                            g.wait_ge(sV.h, hsel_done[t - 1])
                        g.dma_start(
                            gsb[:, 0:n_cores * 2 * N].rearrange("p (r c) -> p r c", r=n_cores),
                            ag_out[t].ap().rearrange("(r p) c -> p r c", p=P),
                        ).then_inc(gq.h, 16)
                    blk.gpsimd(g_); gq.n += 16
                    gld = gq.n

                    def v_(v, t=t, gld=gld):
                        v.wait_ge(gq.h, gld)
                        vf(v, v.memset(hcur[:], 0.0))
                        for r in range(n_cores):
                            vf(v, v.tensor_copy(bass.AP(hbuf, r, [[N * NCORES, P], [NCORES, N]]),
                                                hcur[:]))
                            if r < n_cores - 1:
                                vf(v, v.tensor_mul(htmp[:], gsb[:, r * 2 * N + N:(r + 1) * 2 * N],
                                                   hcur[:]))
                                vf(v, v.tensor_add(hcur[:], htmp[:],
                                                   gsb[:, r * 2 * N:r * 2 * N + N]))
                        vf(v, v.tensor_mul(hbuf[:], hbuf[:],
                                           bass.AP(oh_bc, 0, [[NCORES, P], [0, N], [1, NCORES]])))
                        v.tensor_reduce(hsel[:], hbuf.ap().rearrange("p (n r) -> p n r", r=NCORES),
                                        AX.X, ALU.add).then_inc(sV.h, 1)
                    blk.vector(v_); sV.n += 1
                    hsel_done[t] = sV.n

            def emit_corr_group(blk, t, g):
                slot = t % 2
                gbuf = g % 2
                if True:
                    def q_(sync, t=t, g=g, _qb=sq.n,
                           _fr=[qpool.get((t, g * NPG + i - 4),
                                          qpool.get((t - 1, g * NPG + i + 12), 0))
                                for i in range(NPG)]):
                        sync.wait_ge(sq.h, _qb)
                        for i in range(NPG):
                            n = g * NPG + i
                            sync.wait_ge(sP.h, _fr[i])
                            sync.dma_start(
                                C_sb[:, (n % 4) * LCORE:(n % 4 + 1) * LCORE],
                                bass.AP(bc_sp, (N + n) * LCORE, [[0, P], [1, LCORE]]),
                            ).then_inc(sq.h, 16)
                    blk.sync(q_); sq.n += 16 * NPG
                    cld = sq.n

                    def s_(s, t=t, g=g, gbuf=gbuf, slot=slot,
                           _pf=(htv.get((t, (g - 1) * NPG), 0) if g >= 2
                                else (htv.get((t - 1, (g + 6) * NPG), 0) if t >= 1 else 0))):
                        s.wait_ge(sV.h, _pf)
                        for i in range(NPG):
                            n = g * NPG + i
                            s.activation(
                                p_slab[:, (gbuf * NPG + i) * LCORE:(gbuf * NPG + i + 1) * LCORE],
                                srel[:, slot * LCORE:(slot + 1) * LCORE],
                                AF.Exp, scale=A_sb[:, N * t + n:N * t + n + 1]).then_inc(sS.h, 1)
                    blk.scalar(s_); sS.n += NPG
                    pexp = sS.n

                    for i in range(NPG):
                        htv[(t, g * NPG + i)] = sV.n + i + 1

                    def v_(v, t=t, g=g, gbuf=gbuf, slot=slot, pexp=pexp,
                           _hf=[(qpool.get((t, g * NPG + i - 2),
                                           qpool.get((t - 1, g * NPG + i + 14), 0)))
                                for i in range(NPG)]):
                        v.wait_ge(sS.h, pexp)
                        v.wait_ge(sV.h, hsel_done[t])
                        for i in range(NPG):
                            n = g * NPG + i
                            v.wait_ge(sP.h, _hf[i])
                            v.scalar_tensor_tensor(
                                ht_buf[:, i * LCORE:(i + 1) * LCORE],
                                p_slab[:, (gbuf * NPG + i) * LCORE:(gbuf * NPG + i + 1) * LCORE],
                                hsel[:, n:n + 1],
                                h_slab[:, hslot(t, n):hslot(t, n) + LCORE],
                                ALU.mult, ALU.add).then_inc(sV.h, 1)
                    blk.vector(v_); sV.n += NPG

                    def p2_(v, t=t, g=g, cld=cld,
                            _qf=[(l1h.get((t - 1, 1, (g * NPG + i) // 2), 0)
                                  if g * NPG + i < N // 2
                                  else l1h.get((t, 0, ((g * NPG + i) % 8) // 2), 0))
                                 for i in range(NPG)]):
                        v.wait_ge(sq.h, cld)
                        for i in range(NPG):
                            n = g * NPG + i
                            v.wait_ge(sP.h, _qf[i])
                            v.wait_ge(sV.h, htv[(t, n)])
                            v.tensor_mul(
                                q_buf[:, (n % (N // 2)) * LCORE:(n % (N // 2) + 1) * LCORE],
                                ht_buf[:, i * LCORE:(i + 1) * LCORE],
                                C_sb[:, (n % 4) * LCORE:(n % 4 + 1) * LCORE]).then_inc(sP.h, 1)
                    blk.vector(p2_)
                    for i in range(NPG):
                        sP.n += 1
                        qpool[(t, g * NPG + i)] = sP.n

                    half = g // 4
                    pi = g % 4
                    tgt = (trA, trB, trC, (y_f if half == 0 else y_f2))[pi]

                    def pl1_(p, t=t, g=g, tgt=tgt, pi=pi):
                        p.wait_ge(sP.h, qpool[(t, 2 * g + 1)])
                        if pi == 3 and t >= 1:
                            p.wait_ge(sP.h, zv_done[t - 1])
                        p.tensor_add(tgt[:],
                                     q_buf[:, (2 * (g % 4)) * LCORE:(2 * (g % 4) + 1) * LCORE],
                                     q_buf[:, (2 * (g % 4) + 1) * LCORE:(2 * (g % 4) + 2) * LCORE]
                                     ).then_inc(sP.h, 1)
                    blk.gpsimd(pl1_); sP.n += 1
                    l1h[(t, half, pi)] = sP.n

                    if pi == 3:
                        _pb = sP.n

                        def p_(p, t=t, half=half, _pb=_pb):
                            p.wait_ge(sP.h, _pb)
                            dst = y_f if half == 0 else y_f2
                            p.tensor_add(trA[:], trA[:], trB[:])
                            p.tensor_add(dst[:], dst[:], trC[:]).then_inc(sP.h, 1)
                            p.wait_ge(sP.h, _pb + 1)
                            p.tensor_add(dst[:], dst[:], trA[:]).then_inc(sP.h, 1)
                        blk.gpsimd(p_); sP.n += 2
                        yred_done[(t, half)] = sP.n

            def emit_scan_group(blk, t, g):
                slot = t % 2
                gbuf = g % 2
                if True:
                    def q_(sync, t=t, g=g, _qb=sq.n,
                           _fr=[wpool.get((t, g * NPG + i - 4),
                                          wpool.get((t - 1, g * NPG + i + 12), 0))
                                for i in range(NPG)]):
                        sync.wait_ge(sq.h, _qb)
                        for i in range(NPG):
                            n = g * NPG + i
                            sync.wait_ge(sP.h, _fr[i])
                            sync.dma_start(
                                B_sb[:, (n % 4) * LCORE:(n % 4 + 1) * LCORE],
                                bass.AP(bc_sp, n * LCORE, [[0, P], [1, LCORE]]),
                            ).then_inc(sq.h, 16)
                    blk.sync(q_); sq.n += 16 * NPG
                    bld = sq.n

                    def s_(s, t=t, g=g, gbuf=gbuf, slot=slot,
                           _af=(scv.get((t, (g - 1) * NPG), 0) if g >= 2
                                else (scv.get((t - 1, (g + 6) * NPG), 0) if t >= 1 else 0))):
                        s.wait_ge(sV.h, dready_a[t])
                        s.wait_ge(sV.h, _af)
                        for i in range(NPG):
                            n = g * NPG + i
                            s.activation(
                                a_slab[:, (gbuf * NPG + i) * LCORE:(gbuf * NPG + i + 1) * LCORE],
                                delta_f[:, slot * LCORE:(slot + 1) * LCORE],
                                AF.Exp, scale=A_sb[:, N * t + n:N * t + n + 1]).then_inc(sS.h, 1)
                    blk.scalar(s_); sS.n += NPG
                    aexp = sS.n

                    def p_(v, t=t, g=g, gbuf=gbuf, slot=slot, bld=bld,
                           _wf=(scv.get((t, (g - 1) * NPG), 0) if g >= 2
                                else (scv.get((t - 1, (g + 6) * NPG), 0) if t >= 1 else 0))):
                        v.wait_ge(sq.h, bld)
                        v.wait_ge(sV.h, dready_a[t])
                        v.wait_ge(sV.h, _wf)
                        for i in range(NPG):
                            n = g * NPG + i
                            v.tensor_mul(
                                w_slab[:, (gbuf * NPG + i) * LCORE:(gbuf * NPG + i + 1) * LCORE],
                                du_f[:, slot * LCORE:(slot + 1) * LCORE],
                                B_sb[:, (n % 4) * LCORE:(n % 4 + 1) * LCORE]).then_inc(sP.h, 1)
                    blk.vector(p_)
                    for i in range(NPG):
                        sP.n += 1
                        wpool[(t, g * NPG + i)] = sP.n

                    for i in range(NPG):
                        scv[(t, g * NPG + i)] = sV.n + i + 1

                    def v_(v, t=t, g=g, gbuf=gbuf, slot=slot, aexp=aexp):
                        v.wait_ge(sS.h, aexp)
                        for i in range(NPG):
                            n = g * NPG + i
                            v.wait_ge(sP.h, wpool[(t, n)])
                            if n < 8:
                                if t >= 2:
                                    v.wait_ge(sV.h, htv[(t - 2, n + 8)])
                            else:
                                if t >= 1:
                                    v.wait_ge(sV.h, htv[(t - 1, n - 8)])
                            v.tensor_tensor_scan(
                                h_slab[:, hslot(t, n):hslot(t, n) + LCORE],
                                a_slab[:, (gbuf * NPG + i) * LCORE:(gbuf * NPG + i + 1) * LCORE],
                                w_slab[:, (gbuf * NPG + i) * LCORE:(gbuf * NPG + i + 1) * LCORE],
                                0.0, ALU.mult, ALU.add).then_inc(sV.h, 1)
                    blk.vector(v_)
                    sV.n += NPG

            def emit_state(t):
                slot = t % 2
                scans_done_v[t] = sV.n
                with nc.Block() as blk:
                    scansd = scans_done_v[t]

                    def v_(v, t=t, scansd=scansd):
                        v.wait_ge(sV.h, scansd)
                        if t >= 1:
                            v.wait_ge(gq.h, std_done[t - 1])
                        s0 = (N * t) % HS
                        run1 = min(N, HS - s0)
                        c1 = v.tensor_copy(stout[:, 0:run1],
                                           bass.AP(h_slab, s0 * LCORE + LCORE - 1,
                                                   [[HS * LCORE, P], [LCORE, run1]]))
                        if run1 < N:
                            v.tensor_copy(stout[:, run1:N],
                                          bass.AP(h_slab, LCORE - 1,
                                                  [[HS * LCORE, P], [LCORE, N - run1]])).then_inc(sV.h, 1)
                        else:
                            c1.then_inc(sV.h, 1)
                    blk.vector(v_); sV.n += 1
                    fext = sV.n

                    def s_(s, t=t, slot=slot, fext=fext):
                        s.wait_ge(sV.h, fext)
                        for n in range(N):
                            act = s.activation(stout[:, N + n:N + n + 1],
                                               srel[:, slot * LCORE + LCORE - 1:slot * LCORE + LCORE],
                                               AF.Exp, scale=A_sb[:, N * t + n:N * t + n + 1])
                            if n == N - 1:
                                act.then_inc(sS.h, 1)
                    blk.scalar(s_); sS.n += 1
                    ppd = sS.n

                    def g_(g, t=t, fext=fext, ppd=ppd, _gb=gq.n):
                        g.wait_ge(gq.h, _gb)
                        g.wait_ge(sV.h, fext)
                        g.wait_ge(sS.h, ppd)
                        g.dma_start(ag_in[t].ap(), stout[:]).then_inc(gq.h, 16)
                    blk.gpsimd(g_); gq.n += 16
                    std_done[t] = gq.n

                    cc_step = 16 if n_cores == 1 else 1

                    def g2_(g, t=t, cc_step=cc_step, _cb=sC.n):
                        g.wait_ge(sC.h, _cb)
                        g.wait_ge(gq.h, std_done[t])
                        if n_cores == 1:
                            g.dma_start(ag_out[t].ap()[0:P, :], ag_in[t].ap()).then_inc(sC.h, cc_step)
                        else:
                            g.collective_compute(
                                "AllGather", ALU.bypass,
                                ins=[ag_in[t].ap()], outs=[ag_out[t].ap()],
                                replica_groups=[list(range(n_cores))],
                            ).then_inc(sC.h, cc_step)
                    blk.gpsimd(g2_); sC.n += cc_step
                    ag_done[t] = sC.n

            def emit_ztail(t):
                slot = t % 2
                with nc.Block() as blk:
                    def g_(g, t=t, _gb=gq.n):
                        g.wait_ge(gq.h, _gb)
                        if t >= 1:
                            g.wait_ge(sP.h, zv_done[t - 1])
                        g.dma_start(ssk_tile[:], ssk_sp.ap()[P * t:P * (t + 1), :]).then_inc(gq.h, 16)
                    blk.gpsimd(g_); gq.n += 16
                    sskl = gq.n
                    sskl_done[t] = sskl
                    _pb2 = sP.n

                    def p_(p, t=t, _pb2=_pb2):
                        p.wait_ge(sP.h, yred_done[(t, 1)])
                        p.tensor_add(y_f[:], y_f[:], y_f2[:]).then_inc(sP.h, 1)
                    blk.gpsimd(p_); sP.n += 1
                    yadd = sP.n

                    def v_(v, t=t, slot=slot, sskl=sskl, yadd=yadd):
                        v.wait_ge(sP.h, yadd)
                        vf(v, v.scalar_tensor_tensor(zt_bf[:],
                                                     uc_ring[:, slot * LCORE:(slot + 1) * LCORE],
                                                     wd_sb[:, t:t + 1], y_f[:],
                                                     ALU.mult, ALU.add))
                        v.wait_ge(gq.h, sskl)
                        v.tensor_mul(z_bf[:], zt_bf[:], ssk_tile[:]).then_inc(sP.h, 1)
                    blk.vector(v_); sP.n += 1
                    zv_done[t] = sP.n

                    def g2_(g, t=t, _gb=gq.n):
                        g.wait_ge(gq.h, _gb)
                        g.wait_ge(sP.h, zv_done[t])
                        g.dma_start(z_sp.ap()[P * t:P * (t + 1), :], z_bf[:]).then_inc(gq.h, 16)
                    blk.gpsimd(g2_); gq.n += 16

            for t in range(TE):
                emit_delta(t)
                with nc.Block() as gblk:
                    emit_scan_group(gblk, t, 0)
                    emit_scan_group(gblk, t, 1)
                    if t >= 1:
                        emit_combine(gblk, t - 1)
                    for g in range(NGR):
                        if t >= 1:
                            emit_corr_group(gblk, t - 1, g)
                        if g + 2 < NGR:
                            emit_scan_group(gblk, t, g + 2)
                emit_state(t)
                if t >= 1:
                    emit_ztail(t - 1)
            with nc.Block() as gblk:
                emit_combine(gblk, TE - 1)
                for g in range(NGR):
                    emit_corr_group(gblk, TE - 1, g)
            emit_ztail(TE - 1)

        barrier()
        sfree("ssk_tile", "uc_ring", "dpre_b")

        o1b = es.enter_context(nc.sbuf_tensor("o1b", [P, TD * LCORE], BF16))
        es2 = ExitStack()
        z_all = es2.enter_context(nc.sbuf_tensor("z_all", [P, TE * LCORE], BF16))
        wostg = es2.enter_context(nc.sbuf_tensor("wostg", [P, E], F32))
        wostgb = es2.enter_context(nc.sbuf_tensor("wostgb", [P, 2 * E], BF16))
        mix_bf = es2.enter_context(nc.sbuf_tensor("mix_bf", [P, TD * LCORE], BF16))
        mix2 = es2.enter_context(nc.sbuf_tensor("mix2", [P, LCORE], BF16))
        xT_f = es2.enter_context(nc.sbuf_tensor("xT_f", [P, LCORE], F32))
        lrow = es2.enter_context(nc.sbuf_tensor("lrow", [1, 4 * LCORE], F32))
        r2b = es2.enter_context(nc.sbuf_tensor("r2b", [P, LCORE], BF16))
        mr2b = es2.enter_context(nc.sbuf_tensor("mr2b", [P, LCORE], BF16))
        t1b = es2.enter_context(nc.sbuf_tensor("t1b", [P, LCORE], BF16))
        o1f = es2.enter_context(nc.sbuf_tensor("o1f", [P, LCORE], F32))
        o1sq = es2.enter_context(nc.sbuf_tensor("o1sq", [P, LCORE], BF16))
        psMW = es2.enter_context(nc.psum_tensor("psMW", [P, 1024], F32))
        psM = es2.enter_context(nc.psum_tensor("psM", [P, 512], F32))
        psM2 = es2.enter_context(nc.psum_tensor("psM2", [P, 512], F32))
        psS1 = es2.enter_context(nc.psum_tensor("psS1", [1, LCORE], F32))
        psS2 = es2.enter_context(nc.psum_tensor("psS2", [1, LCORE], F32))
        with es2:
            with nc.Block() as blk:
                def g_(g, _gb=gq.n):
                    g.wait_ge(gq.h, _gb)
                    for t in range(TE):
                        g.dma_start(z_all[:, t * LCORE:(t + 1) * LCORE],
                                    z_sp.ap()[P * t:P * (t + 1), :]).then_inc(gq.h, 16)
                blk.gpsimd(g_); gq.n += 16 * TE
                zld = gq.n

            stats_prev = sT.n
            t0base = sT.n
            mo_arr = {}
            mev_arr = {}
            for dt in range(TD):
                buf = dt % 2
                with nc.Block() as blk:
                    tprev = mo_arr.get(dt - 2, t0base)

                    wc_prev = sV.n

                    def q_(sync, dt=dt, buf=buf, tprev=tprev, wc_prev=wc_prev, _qb=sq.n):
                        sync.wait_ge(sq.h, _qb)
                        sync.wait_ge(sT.h, tprev)
                        if dt >= 1:
                            sync.wait_ge(sV.h, wc_prev)
                        for pc in range(2):
                            sync.dma_start(
                                wostg[:, pc * 1024:(pc + 1) * 1024].rearrange(
                                    "p (a c) -> p a c", c=P),
                                WoutT[1024 * pc:1024 * (pc + 1),
                                      P * dt:P * (dt + 1)].rearrange("(a p) c -> p a c", p=P),
                            ).then_inc(sq.h, 16)
                    blk.sync(q_); sq.n += 32
                    ldo = sq.n

                    def qg_(g, dt=dt, buf=buf, tprev=tprev, wc_prev=wc_prev, _gb=gq.n):
                        g.wait_ge(gq.h, _gb)
                        g.wait_ge(sT.h, tprev)
                        if dt >= 1:
                            g.wait_ge(sV.h, wc_prev)
                        for pc in range(2, 4):
                            g.dma_start(
                                wostg[:, pc * 1024:(pc + 1) * 1024].rearrange(
                                    "p (a c) -> p a c", c=P),
                                WoutT[1024 * pc:1024 * (pc + 1),
                                      P * dt:P * (dt + 1)].rearrange("(a p) c -> p a c", p=P),
                            ).then_inc(gq.h, 16)
                    blk.gpsimd(qg_); gq.n += 32
                    ldo_g = gq.n

                    def v0_(v, buf=buf, ldo=ldo, ldo_g=ldo_g):
                        v.wait_ge(sq.h, ldo)
                        v.wait_ge(gq.h, ldo_g)
                        for hh in range(2):
                            v.tensor_copy(wostgb[:, buf * E + hh * (E // 2):
                                                 buf * E + (hh + 1) * (E // 2)],
                                          wostg[:, hh * (E // 2):(hh + 1) * (E // 2)]
                                          ).then_inc(sV.h, 1)
                    blk.vector(v0_); sV.n += 2
                    woc = sV.n

                    def t_(t_eng, dt=dt, buf=buf, woc=woc,
                           _pm=mev_arr.get(dt - 1, 0)):
                        t_eng.wait_ge(sV.h, woc)
                        if dt >= 1:
                            t_eng.wait_ge(sS.h, _pm)
                        if dt == 0:
                            t_eng.wait_ge(gq.h, zld)
                        for et in range(TE):
                            for lc in range(2):
                                mm = t_eng.matmul(
                                    psMW[:, 512 * lc:512 * (lc + 1)],
                                    wostgb[:, buf * E + P * et:buf * E + P * (et + 1)],
                                    z_all[:, et * LCORE + 512 * lc:et * LCORE + 512 * (lc + 1)],
                                    start=(et == 0), stop=(et == TE - 1),
                                    skip_group_check=True)
                                if et == TE - 1 and lc == 1:
                                    mm.then_inc(sT.h, 1)
                    blk.tensor(t_); sT.n += 1
                    momm = sT.n
                    mo_arr[dt] = momm

                    def s2_(s, dt=dt, momm=momm, stats_prev=stats_prev):
                        s.wait_ge(sT.h, stats_prev)
                        s.wait_ge(sT.h, momm)
                        for lc in range(2):
                            s.copy(mix_bf[:, dt * LCORE + 512 * lc:dt * LCORE + 512 * (lc + 1)],
                                   psMW[:, 512 * lc:512 * (lc + 1)])
                            act = s.activation(mix2[:, 512 * lc:512 * (lc + 1)],
                                               psMW[:, 512 * lc:512 * (lc + 1)], AF.Square)
                            if lc == 1:
                                act.then_inc(sS.h, 1)
                    blk.scalar(s2_); sS.n += 1
                    mev = sS.n
                    mev_arr[dt] = mev

                    def t2_(t_eng, dt=dt, mev=mev):
                        t_eng.wait_ge(sS.h, mev)
                        for lc in range(2):
                            t_eng.matmul(psS1[:, 512 * lc:512 * (lc + 1)], onescol_b[:],
                                         mix_bf[:, dt * LCORE + 512 * lc:dt * LCORE + 512 * (lc + 1)],
                                         start=(dt == 0), stop=(dt == TD - 1), skip_group_check=True)
                            mm = t_eng.matmul(psS2[:, 512 * lc:512 * (lc + 1)], onescol_b[:],
                                              mix2[:, 512 * lc:512 * (lc + 1)],
                                              start=(dt == 0), stop=(dt == TD - 1), skip_group_check=True)
                            if lc == 1:
                                mm.then_inc(sT.h, 1)
                    blk.tensor(t2_); sT.n += 1
                    stats_prev = sT.n

            with nc.Block() as blk:
                smm = sT.n

                def v_(v, smm=smm):
                    v.wait_ge(sT.h, smm)
                    vf(v, v.tensor_scalar_mul(lrow[0:1, 0:LCORE], psS1[:], 1.0 / D))
                    vf(v, v.tensor_scalar_mul(lrow[0:1, LCORE:2 * LCORE], psS2[:], 1.0 / D))
                    vf(v, v.tensor_mul(lrow[0:1, 2 * LCORE:3 * LCORE], lrow[0:1, 0:LCORE],
                                       lrow[0:1, 0:LCORE]))
                    vf(v, v.tensor_sub(lrow[0:1, LCORE:2 * LCORE], lrow[0:1, LCORE:2 * LCORE],
                                       lrow[0:1, 2 * LCORE:3 * LCORE]))
                    v.tensor_scalar_add(lrow[0:1, LCORE:2 * LCORE],
                                        lrow[0:1, LCORE:2 * LCORE], EPS).then_inc(sV.h, 1)
                blk.vector(v_); sV.n += 1
                vst = sV.n

                def s_(s, vst=vst):
                    s.wait_ge(sV.h, vst)
                    sf(s, s.activation(lrow[0:1, 2 * LCORE:3 * LCORE], lrow[0:1, LCORE:2 * LCORE],
                                       AF.Ln))
                    s.activation(lrow[0:1, 2 * LCORE:3 * LCORE], lrow[0:1, 2 * LCORE:3 * LCORE],
                                 AF.Exp, scale=-0.5).then_inc(sS.h, 1)
                blk.scalar(s_); sS.n += 1
                srr = sS.n

                def v2_(v, srr=srr):
                    v.wait_ge(sS.h, srr)
                    v.tensor_mul(lrow[0:1, 3 * LCORE:4 * LCORE], lrow[0:1, 0:LCORE],
                                 lrow[0:1, 2 * LCORE:3 * LCORE]).then_inc(sV.h, 1)
                blk.vector(v2_); sV.n += 1
                vmr = sV.n

                def t_(t_eng, vmr=vmr):
                    t_eng.wait_ge(sV.h, vmr)
                    for lc in range(2):
                        psr = psM[:] if lc == 0 else psMW[:, 0:512]
                        psm_ = psM2[:] if lc == 0 else psMW[:, 512:1024]
                        t_eng.matmul(psr, ones_f[0:1, :],
                                     lrow[0:1, 2 * LCORE + 512 * lc:2 * LCORE + 512 * (lc + 1)],
                                     start=True, stop=True)
                        t_eng.matmul(psm_, ones_f[0:1, :],
                                     lrow[0:1, 3 * LCORE + 512 * lc:3 * LCORE + 512 * (lc + 1)],
                                     start=True, stop=True).then_inc(sT.h, 1)
                blk.tensor(t_); sT.n += 2
                bmm2 = sT.n

                def s2_(s, bmm2=bmm2):
                    s.wait_ge(sT.h, bmm2)
                    s.copy(r2b[:, 0:512], psM[:])
                    s.copy(mr2b[:, 0:512], psM2[:])
                    s.copy(r2b[:, 512:1024], psMW[:, 0:512])
                    s.copy(mr2b[:, 512:1024], psMW[:, 512:1024]).then_inc(sS.h, 1)
                blk.scalar(s2_); sS.n += 1
                bcd = sS.n

            o1v_done = [0] * TD
            for dt in range(TD):
                with nc.Block() as blk:
                    def g_(g, dt=dt, _gb=gq.n):
                        g.wait_ge(gq.h, _gb)
                        if dt >= 1:
                            g.wait_ge(sV.h, o1v_done[dt - 1])
                        g.dma_start(xT_f[:], xt_sp.ap()[P * dt:P * (dt + 1), :]).then_inc(gq.h, 16)
                    blk.gpsimd(g_); gq.n += 16
                    xl = gq.n

                    def v_(v, dt=dt, xl=xl, bcd=bcd):
                        v.wait_ge(sS.h, bcd)
                        v.wait_ge(gq.h, xl)
                        vf(v, v.tensor_mul(t1b[:], mix_bf[:, dt * LCORE:(dt + 1) * LCORE], r2b[:]))
                        vf(v, v.tensor_sub(o1f[:], t1b[:], mr2b[:]))
                        vf(v, v.tensor_add(o1f[:], o1f[:], xT_f[:]))
                        v.tensor_copy(o1b[:, dt * LCORE:(dt + 1) * LCORE], o1f[:]).then_inc(sV.h, 1)
                    blk.vector(v_); sV.n += 1
                    o1v_done[dt] = sV.n

                    def g2_(g, dt=dt, _gb=gq.n):
                        g.wait_ge(gq.h, _gb)
                        g.wait_ge(sV.h, o1v_done[dt])
                        g.dma_start(o1_sp.ap()[P * dt:P * (dt + 1), :], o1f[:]).then_inc(gq.h, 16)
                    blk.gpsimd(g2_); gq.n += 16

            st2_arr = [0] * TD
            for dt in range(TD):
                with nc.Block() as blk:
                    def s_(s, dt=dt):
                        s.wait_ge(sV.h, o1v_done[dt])
                        if dt >= 1:
                            s.wait_ge(sT.h, st2_arr[dt - 1])
                        for lc in range(2):
                            act = s.activation(
                                o1sq[:, 512 * lc:512 * (lc + 1)],
                                o1b[:, dt * LCORE + 512 * lc:dt * LCORE + 512 * (lc + 1)],
                                AF.Square)
                            if lc == 1:
                                act.then_inc(sS.h, 1)
                    blk.scalar(s_); sS.n += 1
                    sq2d = sS.n

                    def t_(t_eng, dt=dt, sq2d=sq2d):
                        t_eng.wait_ge(sS.h, sq2d)
                        for lc in range(2):
                            t_eng.matmul(psS1[:, 512 * lc:512 * (lc + 1)], onescol_b[:],
                                         o1b[:, dt * LCORE + 512 * lc:dt * LCORE + 512 * (lc + 1)],
                                         start=(dt == 0), stop=(dt == TD - 1),
                                         skip_group_check=True)
                            mm = t_eng.matmul(psS2[:, 512 * lc:512 * (lc + 1)], onescol_b[:],
                                              o1sq[:, 512 * lc:512 * (lc + 1)],
                                              start=(dt == 0), stop=(dt == TD - 1),
                                              skip_group_check=True)
                            if lc == 1:
                                mm.then_inc(sT.h, 1)
                    blk.tensor(t_); sT.n += 1
                    st2_arr[dt] = sT.n

            with nc.Block() as blk:
                smm3 = sT.n

                def v_(v, smm3=smm3):
                    v.wait_ge(sT.h, smm3)
                    vf(v, v.tensor_scalar_mul(lrow[0:1, 0:LCORE], psS1[:], 1.0 / D))
                    vf(v, v.tensor_scalar_mul(lrow[0:1, LCORE:2 * LCORE], psS2[:], 1.0 / D))
                    vf(v, v.tensor_mul(lrow[0:1, 2 * LCORE:3 * LCORE], lrow[0:1, 0:LCORE],
                                       lrow[0:1, 0:LCORE]))
                    vf(v, v.tensor_sub(lrow[0:1, LCORE:2 * LCORE], lrow[0:1, LCORE:2 * LCORE],
                                       lrow[0:1, 2 * LCORE:3 * LCORE]))
                    v.tensor_scalar_add(lrow[0:1, LCORE:2 * LCORE],
                                        lrow[0:1, LCORE:2 * LCORE], EPS).then_inc(sV.h, 1)
                blk.vector(v_); sV.n += 1
                vst3 = sV.n

                def s_(s, vst3=vst3):
                    s.wait_ge(sV.h, vst3)
                    sf(s, s.activation(lrow[0:1, 2 * LCORE:3 * LCORE], lrow[0:1, LCORE:2 * LCORE],
                                       AF.Ln))
                    s.activation(lrow[0:1, 2 * LCORE:3 * LCORE], lrow[0:1, 2 * LCORE:3 * LCORE],
                                 AF.Exp, scale=-0.5).then_inc(sS.h, 1)
                blk.scalar(s_); sS.n += 1
                srr3 = sS.n

                def v2_(v, srr3=srr3):
                    v.wait_ge(sS.h, srr3)
                    v.tensor_mul(lrow[0:1, 3 * LCORE:4 * LCORE], lrow[0:1, 0:LCORE],
                                 lrow[0:1, 2 * LCORE:3 * LCORE]).then_inc(sV.h, 1)
                blk.vector(v2_); sV.n += 1
                vmr3 = sV.n

                def t_(t_eng, vmr3=vmr3):
                    t_eng.wait_ge(sV.h, vmr3)
                    for lc in range(2):
                        psr = psM[:] if lc == 0 else psMW[:, 0:512]
                        psm_ = psM2[:] if lc == 0 else psMW[:, 512:1024]
                        t_eng.matmul(psr, ones_f[0:1, :],
                                     lrow[0:1, 2 * LCORE + 512 * lc:2 * LCORE + 512 * (lc + 1)],
                                     start=True, stop=True)
                        t_eng.matmul(psm_, ones_f[0:1, :],
                                     lrow[0:1, 3 * LCORE + 512 * lc:3 * LCORE + 512 * (lc + 1)],
                                     start=True, stop=True).then_inc(sT.h, 1)
                blk.tensor(t_); sT.n += 2
                bmm4 = sT.n

                def s2_(s, bmm4=bmm4):
                    s.wait_ge(sT.h, bmm4)
                    s.copy(r2b[:, 0:512], psM[:])
                    s.copy(mr2b[:, 0:512], psM2[:])
                    s.copy(r2b[:, 512:1024], psMW[:, 0:512])
                    s.copy(mr2b[:, 512:1024], psMW[:, 512:1024]).then_inc(sS.h, 1)
                blk.scalar(s2_); sS.n += 1
                bcd3 = sS.n

            for dt in range(TD):
                with nc.Block() as blk:
                    vprev3 = sV.n

                    def v_(v, dt=dt, bcd3=bcd3, vprev3=vprev3, smm3=st2_arr[TD - 1]):
                        v.wait_ge(sS.h, bcd3)
                        v.wait_ge(sT.h, smm3)
                        v.wait_ge(sV.h, vprev3)
                        vf(v, v.tensor_mul(t1b[:], o1b[:, dt * LCORE:(dt + 1) * LCORE], r2b[:]))
                        v.tensor_sub(o1b[:, dt * LCORE:(dt + 1) * LCORE], t1b[:],
                                     mr2b[:]).then_inc(sV.h, 1)
                    blk.vector(v_); sV.n += 1

        barrier()

        es2 = ExitStack()
        gustg = es2.enter_context(nc.sbuf_tensor("gustg", [P, 4 * 1024], F32))
        dstg = es2.enter_context(nc.sbuf_tensor("dstg", [P, E], F32))
        dstgb = es2.enter_context(nc.sbuf_tensor("dstgb", [P, 2 * INTER], BF16))
        sg_f = es2.enter_context(nc.sbuf_tensor("sg_f", [P, 1024], F32))
        hffn = es2.enter_context(nc.sbuf_tensor("hffn", [P, 64 * 512], BF16))
        ffn_bf = es2.enter_context(nc.sbuf_tensor("ffn_bf", [P, TD * 512], BF16))
        ffn2 = es2.enter_context(nc.sbuf_tensor("ffn2", [P, 512], BF16))
        lrow2 = es2.enter_context(nc.sbuf_tensor("lrow2", [1, 4 * 512], F32))
        r3b = es2.enter_context(nc.sbuf_tensor("r3b", [P, 512], BF16))
        mr3b = es2.enter_context(nc.sbuf_tensor("mr3b", [P, 512], BF16))
        t1c = es2.enter_context(nc.sbuf_tensor("t1c", [P, 512], BF16))
        o1l = es2.enter_context(nc.sbuf_tensor("o1l", [P, 512], F32))
        o2f = es2.enter_context(nc.sbuf_tensor("o2f", [P, 512], F32))
        ostg = es2.enter_context(nc.sbuf_tensor("ostg", [P, 2 * P], F32))
        psG = es2.enter_context(nc.psum_tensor("psG", [P, 512], F32))
        psUp = es2.enter_context(nc.psum_tensor("psUp", [P, 512], F32))
        psDn = es2.enter_context(nc.psum_tensor("psDn", [P, 512], F32))
        psDn2 = es2.enter_context(nc.psum_tensor("psDn2", [P, 512], F32))
        psT1 = es2.enter_context(nc.psum_tensor("psT1", [1, 512], F32))
        psT2 = es2.enter_context(nc.psum_tensor("psT2", [1, 512], F32))
        psO = es2.enter_context(nc.psum_tensor("psO", [P, P], F32))
        with es2:
            for lh in range(2):
                l0 = 512 * lh
                ggmm = {}
                gmul = {}
                gvc = {}
                tbase = sT.n
                vbase = sV.n
                sbase = sS.n
                _bc = None
                for gi in range(64):
                    pr = gi // 2
                    j = gi % 2
                    pb = pr % 2
                    pg = psG if gi % 2 == 0 else psDn
                    pu = psUp if gi % 2 == 0 else psDn2
                    if gi % 8 == 0:
                        if _bc is not None:
                            _bc.__exit__(None, None, None)
                        _bc = nc.Block()
                        blk = _bc.__enter__()
                    if True:
                        if j == 0:
                            tfree = ggmm.get(gi - 3, tbase)
                            for bb in range(2):
                                stg = gustg if bb == 0 else dstg

                                def q_(eng, pr=pr, bb=bb, tfree=tfree, stg=stg,
                                       _qh=(sq.h if bb == 0 else gq.h),
                                       _qb=(sq.n if bb == 0 else gq.n),
                                       _vc=gvc.get((pr - 1, bb), vbase)):
                                    eng.wait_ge(_qh, _qb)
                                    eng.wait_ge(sT.h, tfree)
                                    eng.wait_ge(sV.h, _vc)
                                    col0 = 256 * pr + INTER * bb
                                    for hh in range(2):
                                        for pc in range(2):
                                            eng.dma_start(
                                                stg[:, hh * 2048 + pc * 1024:
                                                    hh * 2048 + (pc + 1) * 1024]
                                                .rearrange("p (a c) -> p a c", c=256),
                                                guT[1024 * hh + 512 * pc:
                                                    1024 * hh + 512 * (pc + 1),
                                                    col0:col0 + 256].rearrange(
                                                        "(a p) c -> p a c", p=P),
                                            ).then_inc(_qh, 16)
                                if bb == 0:
                                    blk.sync(q_); sq.n += 64
                                    ldg = ('q', sq.n)
                                else:
                                    blk.gpsimd(q_); gq.n += 64
                                    ldg = ('g', gq.n)

                                def v0_(v, pb=pb, bb=bb, ldg=ldg, stg=stg):
                                    v.wait_ge(sq.h if ldg[0] == 'q' else gq.h, ldg[1])
                                    for hh in range(2):
                                        v.tensor_copy(
                                            dstgb[:, pb * 8192 + bb * 4096 + hh * 2048:
                                                  pb * 8192 + bb * 4096 + (hh + 1) * 2048],
                                            stg[:, hh * 2048:(hh + 1) * 2048]
                                            ).then_inc(sV.h, 1)
                                blk.vector(v0_); sV.n += 2
                                gvc[(pr, bb)] = sV.n
                            gc_pair = sV.n

                        def t_(t_eng, gi=gi, pb=pb, j=j, l0=l0, gc=gc_pair, pg=pg, pu=pu,
                               _w2=gmul.get(gi - 2, vbase), _sb=sbase):
                            t_eng.wait_ge(sV.h, gc)
                            t_eng.wait_ge(sV.h, _w2)
                            if gi < 2:
                                t_eng.wait_ge(sS.h, _sb)
                            for dt in range(TD):
                                off = pb * 8192 + (dt // 8) * 2048 + (dt % 8) * 256 + j * P
                                mm = t_eng.matmul(
                                    pg[:],
                                    dstgb[:, off:off + P],
                                    o1b[:, dt * LCORE + l0:dt * LCORE + l0 + 512],
                                    start=(dt == 0), stop=(dt == TD - 1))
                                if dt == TD - 1:
                                    mm.then_inc(sT.h, 1)
                            for dt in range(TD):
                                off = pb * 8192 + 4096 + (dt // 8) * 2048 + (dt % 8) * 256 + j * P
                                mm = t_eng.matmul(
                                    pu[:],
                                    dstgb[:, off:off + P],
                                    o1b[:, dt * LCORE + l0:dt * LCORE + l0 + 512],
                                    start=(dt == 0), stop=(dt == TD - 1))
                                if dt == TD - 1:
                                    mm.then_inc(sT.h, 1)
                        blk.tensor(t_); sT.n += 2
                        gmm = sT.n
                        ggmm[gi] = gmm

                        def s2_(s, gi=gi, gmm=gmm, pg=pg):
                            s.wait_ge(sT.h, gmm - 1)
                            s.activation(sg_f[:, (gi % 2) * 512:(gi % 2 + 1) * 512],
                                         pg[:], AF.Silu).then_inc(sS.h, 1)
                        blk.scalar(s2_); sS.n += 1
                        sgd = sS.n

                        def v_(v, gi=gi, gmm=gmm, sgd=sgd, pu=pu):
                            v.wait_ge(sT.h, gmm)
                            v.wait_ge(sS.h, sgd)
                            v.tensor_mul(hffn[:, gi * 512:(gi + 1) * 512],
                                         sg_f[:, (gi % 2) * 512:(gi % 2 + 1) * 512],
                                         pu[:]).then_inc(sV.h, 1)
                        blk.vector(v_); sV.n += 1
                        gmul[gi] = sV.n

                _bc.__exit__(None, None, None)
                _bc = None
                hffn_done = sV.n
                for dt in range(TD):
                    buf = dt % 2
                    if dt % 8 == 0:
                        if _bc is not None:
                            _bc.__exit__(None, None, None)
                        _bc = nc.Block()
                        blk = _bc.__enter__()
                    if True:
                        if dt == 0:
                            tbase2 = sT.n
                            sbase2 = sS.n
                            dmm_arr = {}
                            sevac = {}
                            dvc = {}
                        tfree2 = dmm_arr.get(dt - 2, tbase2)

                        for bb in range(2):
                            stg = dstg if bb == 0 else gustg

                            def q_(eng, dt=dt, bb=bb, tfree2=tfree2, stg=stg,
                                   _qh=(sq.h if bb == 0 else gq.h),
                                   _qb=(sq.n if bb == 0 else gq.n),
                                   _vc=dvc.get((dt - 1, bb), 0) if dt >= 1 else None):
                                eng.wait_ge(_qh, _qb)
                                eng.wait_ge(sT.h, tfree2)
                                if _vc is not None:
                                    eng.wait_ge(sV.h, _vc)
                                for hh2 in range(2):
                                    hh = bb * 2 + hh2
                                    for pc in range(2):
                                        eng.dma_start(
                                            stg[:, hh2 * 2048 + pc * 1024:
                                                hh2 * 2048 + (pc + 1) * 1024].rearrange(
                                                "p (a c) -> p a c", c=P),
                                            dwT[2048 * hh + 1024 * pc:2048 * hh + 1024 * (pc + 1),
                                                P * dt:P * (dt + 1)].rearrange("(a p) c -> p a c", p=P),
                                        ).then_inc(_qh, 16)
                            if bb == 0:
                                blk.sync(q_); sq.n += 64
                                ldd2 = ('q', sq.n)
                            else:
                                blk.gpsimd(q_); gq.n += 64
                                ldd2 = ('g', gq.n)

                            def v0_(v, buf=buf, bb=bb, ldd2=ldd2, stg=stg):
                                v.wait_ge(sq.h if ldd2[0] == 'q' else gq.h, ldd2[1])
                                for hh2 in range(2):
                                    hh = bb * 2 + hh2
                                    v.tensor_copy(dstgb[:, buf * INTER + hh * 2048:
                                                        buf * INTER + (hh + 1) * 2048],
                                                  stg[:, hh2 * 2048:(hh2 + 1) * 2048]
                                                  ).then_inc(sV.h, 1)
                            blk.vector(v0_); sV.n += 2
                            dvc[(dt, bb)] = sV.n
                        dc = sV.n

                        def t_(t_eng, dt=dt, buf=buf, dc=dc, hffn_done=hffn_done,
                               _pf=None, _sb2=sbase2):
                            t_eng.wait_ge(sV.h, dc)
                            t_eng.wait_ge(sV.h, hffn_done)
                            if dt < 2:
                                t_eng.wait_ge(sS.h, _sb2)
                            else:
                                t_eng.wait_ge(sS.h, sevac[dt - 2])
                            ps = psDn if dt % 2 == 0 else psDn2
                            for ii in range(64):
                                mm = t_eng.matmul(ps[:],
                                                  dstgb[:, buf * INTER + P * ii:
                                                        buf * INTER + P * (ii + 1)],
                                                  hffn[:, ii * 512:(ii + 1) * 512],
                                                  start=(ii == 0), stop=(ii == 63))
                                if ii == 63:
                                    mm.then_inc(sT.h, 1)
                        blk.tensor(t_); sT.n += 1
                        dmm2 = sT.n
                        dmm_arr[dt] = dmm2

                        def s2_(s, dt=dt, dmm2=dmm2):
                            ps = psDn if dt % 2 == 0 else psDn2
                            s.wait_ge(sT.h, dmm2)
                            if dt >= 1:
                                s.wait_ge(sT.h, sT.n)
                            s.copy(ffn_bf[:, dt * 512:(dt + 1) * 512], ps[:])
                            s.activation(ffn2[:], ps[:], AF.Square).then_inc(sS.h, 1)
                        blk.scalar(s2_); sS.n += 1
                        fev = sS.n
                        sevac[dt] = fev

                        def t2_(t_eng, dt=dt, fev=fev):
                            t_eng.wait_ge(sS.h, fev)
                            t_eng.matmul(psT1[:], onescol_b[:], ffn_bf[:, dt * 512:(dt + 1) * 512],
                                         start=(dt == 0), stop=(dt == TD - 1), skip_group_check=True)
                            t_eng.matmul(psT2[:], onescol_b[:], ffn2[:],
                                         start=(dt == 0), stop=(dt == TD - 1),
                                         skip_group_check=True).then_inc(sT.h, 1)
                        blk.tensor(t2_); sT.n += 1

                _bc.__exit__(None, None, None)
                _bc = None
                with nc.Block() as blk:
                    smm2 = sT.n

                    def v_(v, smm2=smm2):
                        v.wait_ge(sT.h, smm2)
                        vf(v, v.tensor_scalar_mul(lrow2[0:1, 0:512], psT1[:], 1.0 / D))
                        vf(v, v.tensor_scalar_mul(lrow2[0:1, 512:1024], psT2[:], 1.0 / D))
                        vf(v, v.tensor_mul(lrow2[0:1, 1024:1536], lrow2[0:1, 0:512],
                                           lrow2[0:1, 0:512]))
                        vf(v, v.tensor_sub(lrow2[0:1, 512:1024], lrow2[0:1, 512:1024],
                                           lrow2[0:1, 1024:1536]))
                        v.tensor_scalar_add(lrow2[0:1, 512:1024], lrow2[0:1, 512:1024],
                                            EPS).then_inc(sV.h, 1)
                    blk.vector(v_); sV.n += 1
                    vst2 = sV.n

                    def s_(s, vst2=vst2):
                        s.wait_ge(sV.h, vst2)
                        sf(s, s.activation(lrow2[0:1, 1024:1536], lrow2[0:1, 512:1024], AF.Ln))
                        s.activation(lrow2[0:1, 1024:1536], lrow2[0:1, 1024:1536],
                                     AF.Exp, scale=-0.5).then_inc(sS.h, 1)
                    blk.scalar(s_); sS.n += 1
                    sr2 = sS.n

                    def v2_(v, sr2=sr2):
                        v.wait_ge(sS.h, sr2)
                        v.tensor_mul(lrow2[0:1, 1536:2048], lrow2[0:1, 0:512],
                                     lrow2[0:1, 1024:1536]).then_inc(sV.h, 1)
                    blk.vector(v2_); sV.n += 1
                    vmr2 = sV.n

                    def t_(t_eng, vmr2=vmr2):
                        t_eng.wait_ge(sV.h, vmr2)
                        t_eng.matmul(psDn[:], ones_f[0:1, :], lrow2[0:1, 1024:1536],
                                     start=True, stop=True)
                        t_eng.matmul(psDn2[:], ones_f[0:1, :], lrow2[0:1, 1536:2048],
                                     start=True, stop=True).then_inc(sT.h, 1)
                    blk.tensor(t_); sT.n += 1
                    bmm3 = sT.n

                    def s2_(s, bmm3=bmm3):
                        s.wait_ge(sT.h, bmm3)
                        s.copy(r3b[:], psDn[:])
                        s.copy(mr3b[:], psDn2[:]).then_inc(sS.h, 1)
                    blk.scalar(s2_); sS.n += 1
                    bcd2 = sS.n

                for dt in range(TD):
                    if dt % 8 == 0:
                        if _bc is not None:
                            _bc.__exit__(None, None, None)
                        _bc = nc.Block()
                        blk = _bc.__enter__()
                    if True:
                        o2prev = sT.n

                        def g_(g, dt=dt, lh=lh, o2prev=o2prev, _gb=gq.n):
                            g.wait_ge(gq.h, _gb)
                            g.wait_ge(sT.h, o2prev)
                            g.dma_start(o1l[:], o1_sp.ap()[P * dt:P * (dt + 1),
                                                           512 * lh:512 * (lh + 1)]).then_inc(gq.h, 16)
                        blk.gpsimd(g_); gq.n += 16
                        o1ld = gq.n

                        def v_(v, dt=dt, o1ld=o1ld, bcd2=bcd2, o2prev=o2prev):
                            v.wait_ge(sS.h, bcd2)
                            v.wait_ge(sT.h, o2prev)
                            vf(v, v.tensor_mul(t1c[:], ffn_bf[:, dt * 512:(dt + 1) * 512], r3b[:]))
                            vf(v, v.tensor_sub(o2f[:], t1c[:], mr3b[:]))
                            v.wait_ge(gq.h, o1ld)
                            vf(v, v.tensor_add(o2f[:], o2f[:], o1l[:]))
                            v.nop().then_inc(sV.h, 1)
                        blk.vector(v_); sV.n += 1
                        o2d = sV.n

                        s_out_base = sS.n
                        g_out_base = gq.n

                        def t_(t_eng, dt=dt, o2d=o2d, s_out_base=s_out_base):
                            t_eng.wait_ge(sV.h, o2d)
                            for lb in range(4):
                                if lb >= 1:
                                    t_eng.wait_ge(sS.h, s_out_base + lb)
                                t_eng.transpose(psO[:], o2f[:, P * lb:P * (lb + 1)],
                                                ident[:]).then_inc(sT.h, 1)
                        blk.tensor(t_); sT.n += 4
                        otr = sT.n

                        def s_(s, otr=otr, g_out_base=g_out_base):
                            s.wait_ge(gq.h, g_out_base)
                            for lb in range(2):
                                s.wait_ge(sT.h, otr - 4 + lb + 1)
                                s.copy(ostg[:, lb * P:(lb + 1) * P], psO[:]).then_inc(sS.h, 1)
                        blk.scalar(s_); sS.n += 2
                        oevA = sS.n

                        def g2a_(g, dt=dt, lh=lh, oevA=oevA, _gb=gq.n):
                            g.wait_ge(gq.h, _gb)
                            g.wait_ge(sS.h, oevA)
                            for lb in range(2):
                                g.dma_start(out[512 * lh + P * lb:512 * lh + P * (lb + 1),
                                                P * dt:P * (dt + 1)],
                                            ostg[:, lb * P:(lb + 1) * P]).then_inc(gq.h, 16)
                        blk.gpsimd(g2a_); gq.n += 32
                        bA = gq.n

                        def s_b(s, otr=otr, bA=bA):
                            s.wait_ge(gq.h, bA)
                            for lb in range(2, 4):
                                s.wait_ge(sT.h, otr - 4 + lb + 1)
                                s.copy(ostg[:, (lb - 2) * P:(lb - 1) * P],
                                       psO[:]).then_inc(sS.h, 1)
                        blk.scalar(s_b); sS.n += 2
                        oevB = sS.n

                        def g2b_(g, dt=dt, lh=lh, oevB=oevB, _gb=bA):
                            g.wait_ge(gq.h, _gb)
                            g.wait_ge(sS.h, oevB)
                            for lb in range(2, 4):
                                g.dma_start(out[512 * lh + P * lb:512 * lh + P * (lb + 1),
                                                P * dt:P * (dt + 1)],
                                            ostg[:, (lb - 2) * P:(lb - 1) * P]).then_inc(gq.h, 16)
                        blk.gpsimd(g2b_); gq.n += 32

                _bc.__exit__(None, None, None)
                _bc = None

        with nc.Block() as blk:
            qq, gg = sq.n, gq.n

            def q_(sync, qq=qq, gg=gg):
                sync.wait_ge(gq.h, gg)
                sync.wait_ge(sq.h, qq)
            blk.sync(q_)

    return nc


def kernel(**inputs):
    x = np.ascontiguousarray(np.asarray(inputs["L0_attn_x"], dtype=np.float32)[0])
    L = x.shape[0]
    assert L == NCORES * LCORE

    f32 = lambda a: np.ascontiguousarray(np.asarray(a, dtype=np.float32))
    shared = dict(
        WinT=f32(np.asarray(inputs["L0_attn_W_in"]).T),
        WskT=f32(np.asarray(inputs["L0_attn_W_skip"]).T),
        Wd1T=f32(np.asarray(inputs["L0_attn_Wd1"]).T),
        Wd2T=f32(np.asarray(inputs["L0_attn_Wd2"]).T),
        WBCT=f32(np.concatenate([np.asarray(inputs["L0_attn_WB"]).T,
                                 np.asarray(inputs["L0_attn_WC"]).T], axis=1)),
        WoutT=f32(np.asarray(inputs["L0_attn_W_out"]).T),
        guT=f32(np.asarray(inputs["L0_ffn_w_gate_up"]).T),
        dwT=f32(np.asarray(inputs["L0_ffn_w_down"]).T),
        A_log=f32(inputs["L0_attn_A_log"]),
        conv_w=f32(np.asarray(inputs["L0_attn_conv_w"]).reshape(E, 4)),
        conv_b=f32(np.asarray(inputs["L0_attn_conv_b"]).reshape(1, E)),
        bd2=f32(np.asarray(inputs["L0_attn_bd2"]).reshape(1, E)),
        W_D=f32(np.asarray(inputs["L0_attn_W_D"]).reshape(1, E)),
        norm_w=f32(np.asarray(inputs["L0_attn_norm_w"]).reshape(1, D)),
        ident=np.eye(P, dtype=np.float32),
    )

    in_maps = []
    for k in range(NCORES):
        oh = np.zeros((1, NCORES), dtype=np.float32)
        oh[0, k] = 1.0
        xs = np.ascontiguousarray(x[k * LCORE:(k + 1) * LCORE])
        if k == 0:
            xp = np.zeros((HALO, D), dtype=np.float32)
        else:
            xp = np.ascontiguousarray(x[k * LCORE - HALO:k * LCORE])
        m = dict(shared)
        m.update(x_shard=xs, x_prev=xp, onehot=oh)
        in_maps.append(m)

    if "nc" not in _NC_CACHE:
        _NC_CACHE["nc"] = build_nc(NCORES)
    nc = _NC_CACHE["nc"]
    res = run_bass_kernel_spmd(nc, in_maps, core_ids=list(range(NCORES)))
    outs = [np.asarray(res.results[k]["out_shard"], dtype=np.float32) for k in range(NCORES)]
    return np.concatenate(outs, axis=0)[None, :, :]
